# version 03
# baseline (speedup 1.0000x reference)
import math
import numpy as np
import ml_dtypes
import concourse.bass as bass
import concourse.mybir as mybir
from concourse.bass_utils import run_bass_kernel_spmd

F32 = mybir.dt.float32
BF16 = mybir.dt.bfloat16
AX = mybir.AxisListType
ALU = mybir.AluOpType
ACT = mybir.ActivationFunctionType

T = 2048
D = 2048
H = 16
KV = 4
DH = 128
FF = 1024
NE = 8
NT = 16
DC = 16
SCALE = 1.0 / math.sqrt(DH)
LN_EPS = 1e-5
RMS_EPS = 1e-6
HALF_T = 1024
HT = 8

DEBUG = False

ENGS = ("sync", "tensor", "vector", "scalar", "gpsimd")


class Sem:
    def __init__(self, handle, name):
        self.h = handle
        self.name = name
        self.n = 0


class Rec:
    def __init__(self):
        self.ops = {k: [] for k in ENGS}
        self._last_wait = {}

    def op(self, eng, fn, inc=None):
        if inc is not None:
            sem, k = inc
            sem.n += k

            def run(E, fn=fn, h=sem.h, k=k):
                fn(E).then_inc(h, k)
            self.ops[eng].append(run)
            return (sem, sem.n)
        else:
            self.ops[eng].append(lambda E, fn=fn: fn(E))
            return None

    def wait(self, eng, token):
        if token is None:
            return
        sem, v = token
        if v <= 0:
            return
        key = (eng, sem.name)
        if self._last_wait.get(key, -1) >= v:
            return
        self._last_wait[key] = v
        self.ops[eng].append(lambda E, h=sem.h, v=v: E.wait_ge(h, v))

    def now(self, sem):
        return (sem, sem.n)


def build_nc(debug=False):
    nc = bass.Bass("TRN2", target_bir_lowering=False)

    x_d = nc.dram_tensor("x", [T, D], F32, kind="ExternalInput").ap()
    wqT_d = nc.dram_tensor("wqT", [D, H * DH], F32, kind="ExternalInput").ap()
    wkT_d = nc.dram_tensor("wkT", [D, KV * DH], F32, kind="ExternalInput").ap()
    wvT_d = nc.dram_tensor("wvT", [D, KV * DH], F32, kind="ExternalInput").ap()
    woT_d = nc.dram_tensor("woT", [H * DH, D], F32, kind="ExternalInput").ap()
    rwT_d = nc.dram_tensor("rwT", [D, NE], F32, kind="ExternalInput").ap()
    w1_d = nc.dram_tensor("w1", [NE, D, FF], F32, kind="ExternalInput").ap()
    w2_d = nc.dram_tensor("w2", [NE, D, FF], F32, kind="ExternalInput").ap()
    w3_d = nc.dram_tensor("w3", [NE, FF, D], F32, kind="ExternalInput").ap()
    cs_d = nc.dram_tensor("cs_tab", [T, 512], F32, kind="ExternalInput").ap()
    const_d = nc.dram_tensor("consts", [128, 3 * 128], BF16, kind="ExternalInput").ap()

    out_d = nc.dram_tensor("out", [T, D], F32, kind="ExternalOutput").ap()

    dbg = {}
    if debug:
        dbg["xnT"] = nc.dram_tensor("dbg_xnT", [128, DC, HALF_T], BF16, kind="ExternalOutput").ap()
        dbg["kT"] = nc.dram_tensor("dbg_kT", [128, KV, T], BF16, kind="ExternalOutput").ap()
        dbg["vT"] = nc.dram_tensor("dbg_vT", [128, KV, NT, 129], BF16, kind="ExternalOutput").ap()
        dbg["lg"] = nc.dram_tensor("dbg_lg", [128, NT, NE], F32, kind="ExternalOutput").ap()
        dbg["W"] = nc.dram_tensor("dbg_W", [128, NT, NE], F32, kind="ExternalOutput").ap()
        dbg["qT"] = nc.dram_tensor("dbg_qT", [128, H, HALF_T], BF16, kind="ExternalOutput").ap()
        dbg["ao"] = nc.dram_tensor("dbg_ao", [128, HT, D], BF16, kind="ExternalOutput").ap()
        dbg["y"] = nc.dram_tensor("dbg_y", [128, HT, D], BF16, kind="ExternalOutput").ap()

    ARENA_SIZE = 209728
    arena = nc.alloc_sbuf_tensor("ARENA", [128, ARENA_SIZE], mybir.dt.uint8)
    ARENA_BASE = nc.lookup_mloc(arena).addr

    def sb(name, shape, dtype, off):
        return nc.alloc_sbuf_tensor_at(name, shape, dtype, offset=ARENA_BASE + off).ap()

    XN = sb("XN", [128, DC, HALF_T], BF16, 0)
    QT = sb("QT", [128, H, HALF_T], BF16, 32768)
    SQSCR = sb("SQSCR", [128, D], F32, 32768)
    XLO_TMP = sb("XLO_TMP", [128, DC, 128], BF16, 40960)
    HBUF = sb("HBUF", [128, 8, HALF_T], BF16, 32768)
    FXSTG = [sb("FXSTG0", [128, D], F32, 32768),
             sb("FXSTG1", [128, D], F32, 40960)]
    W3ST = [sb("W3ST0", [128, 8, 512], BF16, 49152),
            sb("W3ST1", [128, 8, 512], BF16, 57344)]

    AO = sb("AO", [128, HT, D], BF16, 65536)
    WKST = sb("WKST", [128, DC, 512], BF16, 65536)
    WVST = sb("WVST", [128, DC, 512], BF16, 81920)
    W1ST = [sb("W1ST0", [128, DC, 256], BF16, 65536),
            sb("W1ST1", [128, DC, 256], BF16, 73728)]
    W2ST = [sb("W2ST0", [128, DC, 256], BF16, 81920),
            sb("W2ST1", [128, DC, 256], BF16, 90112)]

    ACC = sb("ACC", [128, HT, D], F32, 98304)
    WST = [sb("WST0", [128, DC, 512], BF16, 98304),
           sb("WST1", [128, DC, 512], BF16, 114688)]
    XSTG = [sb("XSTG0", [128, D], F32, 131072),
            sb("XSTG1", [128, D], F32, 139264)]
    CSST = [sb("CSST0", [128, 512], F32, 147456),
            sb("CSST1", [128, 512], F32, 149504)]
    XNTOK = sb("XNTOK", [128, D], BF16, 151552)
    XNLO = sb("XNLO", [128, D], BF16, 155648)
    XNT_TMP = sb("XNT_TMP", [128, DC, 128], BF16, 159744)

    KT = sb("KT", [128, KV, T], BF16, 163840)
    VT = sb("VT", [128, KV, NT, 129], BF16, 180224)
    ID16 = sb("ID16", [128, 128], BF16, 196736)
    MASK_C = sb("MASK_C", [128, 128], BF16, 196992)
    MASK_W = sb("MASK_W", [128, 128], BF16, 197248)
    RWTF = sb("RWTF", [128, DC, NE], F32, 197504)
    RWT_HI = sb("RWT_HI", [128, DC, NE], BF16, 198016)
    RWT_LO = sb("RWT_LO", [128, DC, NE], BF16, 198272)
    LGT = sb("LGT", [128, NT, NE], F32, 198528)
    WRT = sb("WRT", [128, NT, NE], F32, 199040)
    STAT = sb("STAT", [128, 32], F32, 199552)
    ST8A = sb("ST8A", [128, NE], F32, 199680)
    ST8B = sb("ST8B", [128, NE], F32, 199712)
    ATB = [sb("ATB0", [128, 512], BF16, 199744),
           sb("ATB1", [128, 512], BF16, 200768)]
    NT_SCR = sb("NT_SCR", [128, 512], F32, 201792)
    RO_S1 = sb("RO_S1", [128, 256], F32, 203840)
    RO_S2 = sb("RO_S2", [128, 256], F32, 204864)
    RP_SCR = sb("RP_SCR", [128, 512], BF16, 205888)
    RMS_SQ = sb("RMS_SQ", [128, 128], F32, 206912)
    RO_S3 = sb("RO_S3", [128, 256], F32, 207424)
    RO_S4 = sb("RO_S4", [128, 256], F32, 208448)
    ST8C = sb("ST8C", [128, NE], F32, 209472)

    PS = [nc.alloc_psum_tensor(f"PS{i}", [128, 512], F32).ap() for i in range(6)]
    PT = [nc.alloc_psum_tensor(f"PT{i}", [128, 128], BF16).ap() for i in range(2)]

    rec = Rec()
    sems = {}
    for name in ("X", "CS", "W", "ST", "T", "V", "SC"):
        sems[name] = Sem(nc.alloc_semaphore(name=f"S_{name}"), name)
    SX, SCS, SW, SST, STn, SV, SSC = (sems[k] for k in ("X", "CS", "W", "ST", "T", "V", "SC"))

    nc.all_engine_barrier()
    sem_nums = [s.h.num for s in sems.values()]
    nc.gpsimd.dma_reset(range(min(sem_nums), max(sem_nums) + 1))
    for s in sems.values():
        nc.gpsimd.sem_clear(s.h)
    nc.all_engine_barrier()
    warm = nc.alloc_semaphore(name="S_WARM")
    nc.sync.dma_start(out=XSTG[0], in_=x_d[0:128, :]).then_inc(warm, 16)
    nc.gpsimd.dma_start(out=XSTG[1], in_=x_d[128:256, :]).then_inc(warm, 16)
    nc.gpsimd.wait_ge(warm, 32)
    for s in sems.values():
        nc.gpsimd.sem_clear(s.h)
    nc.all_engine_barrier()

    def stat(i):
        return STAT[:, i:i + 1]

    SUMS, MVc, EX2, M2C = 0, 1, 2, 3
    VARV, RSTD, NEGM, RCOL = 4, 5, 6, 7
    KRMS0 = 8
    M1C, M2CC, DMC, W1CC = 12, 13, 14, 15
    RKC0 = 16
    W2CC, EW2 = 20, 21
    SQc, STDV, EWC = 24, 25, 26

    free = {}

    for _rep in range(2):
        rec.op("sync", lambda E: E.dma_start(out=ID16, in_=const_d[:, 0:128]), inc=(SCS, 16))
        rec.op("sync", lambda E: E.dma_start(out=MASK_C, in_=const_d[:, 128:256]), inc=(SCS, 16))
        rec.op("sync", lambda E: E.dma_start(out=MASK_W, in_=const_d[:, 256:384]), inc=(SCS, 16))
        rec.op("sync", lambda E: E.dma_start(
            out=RWTF, in_=rwT_d.rearrange("(dc p) e -> p dc e", p=128)), inc=(SCS, 16))
    consts_tok = rec.now(SCS)
    rec.wait("vector", consts_tok)
    rec.op("vector", lambda E: E.tensor_copy(RWT_HI, RWTF), inc=(SV, 1))
    rec.wait("vector", rec.now(SV))
    rec.op("vector", lambda E: E.tensor_tensor(RWT_LO, RWTF, RWT_HI, ALU.subtract), inc=(SV, 1))
    rwhl_tok = rec.now(SV)
    rec.op("vector", lambda E: E.memset(
        VT[:, :, :, 128:129].rearrange("p a b c -> p (a b c)"), 1.0), inc=(SV, 1))

    for _rep in range(2):
        rec.op("gpsimd", lambda E: E.dma_start(
            out=WKST, in_=wkT_d.rearrange("(dc p) n -> p dc n", p=128)), inc=(SW, 16))
        rec.op("gpsimd", lambda E: E.dma_start(
            out=WVST, in_=wvT_d.rearrange("(dc p) n -> p dc n", p=128)), inc=(SW, 16))
    wkv_tok = rec.now(SW)

    def emit_xn_tile(tt, dest, do_router, do_lo, evac_guard_key=None, dup_load=False):
        b = tt % 2
        rec.wait("sync", free.get(("XSTG", b)))
        rec.op("sync", lambda E, tt=tt, b=b: E.dma_start(
            out=XSTG[b], in_=x_d[tt * 128:(tt + 1) * 128, :]), inc=(SX, 16))
        if dup_load:
            rec.op("sync", lambda E, tt=tt, b=b: E.dma_start(
                out=XSTG[b], in_=x_d[tt * 128:(tt + 1) * 128, :]), inc=(SX, 16))
        x_tok = rec.now(SX)

        rec.wait("vector", x_tok)
        rec.op("vector", lambda E, b=b: E.reduce_sum(stat(SUMS), XSTG[b], axis=AX.X))
        rec.wait("scalar", x_tok)
        rec.wait("scalar", free.get(("SQc",)))
        rec.op("scalar", lambda E, b=b: E.activation(
            SQSCR, XSTG[b], ACT.Square, accum_out=stat(SQc)), inc=(SSC, 1))
        sq_tok = rec.now(SSC)
        rec.wait("vector", sq_tok)
        rec.op("vector", lambda E: E.tensor_copy(STAT[:, 31:32], STAT[:, 30:31]))
        rec.op("vector", lambda E: E.tensor_scalar(stat(MVc), stat(SUMS), 1.0 / D, None, op0=ALU.mult))
        rec.op("vector", lambda E: E.tensor_scalar(
            stat(EX2), stat(SQc), 1.0 / D, LN_EPS, op0=ALU.mult, op1=ALU.add))
        rec.op("vector", lambda E: E.tensor_mul(stat(M2C), stat(MVc), stat(MVc)))
        rec.op("vector", lambda E: E.tensor_scalar(stat(NEGM), stat(MVc), -1.0, None, op0=ALU.mult))
        rec.op("vector", lambda E: E.tensor_sub(stat(VARV), stat(EX2), stat(M2C)),
               inc=(SV, 1))
        var_tok = rec.now(SV)
        rec.wait("scalar", var_tok)
        rec.op("scalar", lambda E: E.activation(stat(STDV), stat(VARV), ACT.Sqrt), inc=(SSC, 1))
        std_tok = rec.now(SSC)
        rec.wait("vector", std_tok)
        rec.op("vector", lambda E: E.reciprocal(stat(RSTD), stat(STDV)), inc=(SV, 1))
        rstd_tok = rec.now(SV)
        rec.wait("vector", rstd_tok)
        rec.wait("vector", free.get(("XNTOK",)))
        if do_lo:
            rec.op("vector", lambda E, b=b: E.tensor_scalar(
                SQSCR, XSTG[b], stat(NEGM), stat(RSTD), op0=ALU.add, op1=ALU.mult))
            rec.op("vector", lambda E: E.tensor_copy(XNTOK, SQSCR))
            rec.op("vector", lambda E: E.tensor_copy(STAT[:, 31:32], STAT[:, 30:31]))
            rec.op("vector", lambda E: E.tensor_tensor(XNLO, SQSCR, XNTOK, ALU.subtract),
                   inc=(SV, 1))
        else:
            rec.op("vector", lambda E, b=b: E.tensor_scalar(
                XNTOK, XSTG[b], stat(NEGM), stat(RSTD), op0=ALU.add, op1=ALU.mult),
                inc=(SV, 1))
        xn_tok = rec.now(SV)
        free[("XSTG", b)] = xn_tok
        free[("SQc",)] = xn_tok

        rec.wait("tensor", xn_tok)
        rec.wait("tensor", consts_tok)
        srcs = [(XNTOK, dest)]
        if do_lo:
            srcs.append((XNLO, lambda dc: XLO_TMP[:, dc, :]))
        for src, dst in srcs:
            for dc in range(DC):
                j = dc % 2
                rec.wait("tensor", free.get(("PT", j)))
                rec.op("tensor", lambda E, dc=dc, j=j, src=src: E.transpose(
                    PT[j], src[:, dc * 128:(dc + 1) * 128], ID16), inc=(STn, 1))
                tr_tok = rec.now(STn)
                rec.wait("vector", tr_tok)
                if evac_guard_key is not None:
                    rec.wait("vector", free.get(evac_guard_key))
                rec.op("vector", lambda E, j=j, d=dst(dc): E.tensor_copy(d, PT[j]), inc=(SV, 1))
                free[("PT", j)] = rec.now(SV)
        free[("XNTOK",)] = (STn, STn.n)
        xnT_tok = rec.now(SV)

        if do_router:
            rec.wait("tensor", xnT_tok)
            rec.wait("tensor", rwhl_tok)
            rec.wait("tensor", free.get(("PS", 5)))
            combos = [(dest, RWT_HI), (dest, RWT_LO),
                      ((lambda dc: XLO_TMP[:, dc, :]), RWT_HI)]
            nmm = 0
            for lt, rw in combos:
                for dc in range(DC):
                    first = nmm == 0
                    last = nmm == 3 * DC - 1
                    rec.op("tensor", lambda E, l0=lt(dc), r0=rw[:, dc, :], f=first, l=last:
                           E.matmul(PS[5][:, 0:NE], l0, r0, start=f, stop=l),
                           inc=(STn, 1) if last else None)
                    nmm += 1
            lg_tok = rec.now(STn)
            rec.wait("vector", lg_tok)
            rec.op("vector", lambda E, tt=tt: E.tensor_copy(LGT[:, tt, :], PS[5][:, 0:NE]),
                   inc=(SV, 1))
            free[("PS", 5)] = rec.now(SV)
        return xnT_tok

    def emit_rms_rope(ps_src, nheads, cs_slot):
        rec.wait("scalar", free.get(("KRMS",)))
        for h4 in range(nheads):
            rec.op("scalar", lambda E, h4=h4: E.activation(
                RMS_SQ, ps_src[:, h4 * 128:(h4 + 1) * 128], ACT.Square,
                accum_out=stat(KRMS0 + h4)),
                inc=(SSC, 1) if h4 == nheads - 1 else None)
        rs_tok = rec.now(SSC)
        rec.wait("vector", rs_tok)
        rec.op("vector", lambda E, n=nheads: E.tensor_scalar(
            STAT[:, KRMS0:KRMS0 + n], STAT[:, KRMS0:KRMS0 + n],
            1.0 / DH, RMS_EPS, op0=ALU.mult, op1=ALU.add), inc=(SV, 1))
        ms_tok = rec.now(SV)
        free[("KRMS",)] = ms_tok
        rec.wait("scalar", ms_tok)
        rec.op("scalar", lambda E, n=nheads: E.activation(
            STAT[:, RKC0:RKC0 + n], STAT[:, KRMS0:KRMS0 + n], ACT.Sqrt), inc=(SSC, 1))
        sq2_tok = rec.now(SSC)
        rec.wait("vector", sq2_tok)
        rec.op("vector", lambda E, n=nheads: E.reciprocal(
            STAT[:, RKC0:RKC0 + n], STAT[:, RKC0:RKC0 + n]), inc=(SV, 1))
        rk_tok = rec.now(SV)
        rec.wait("vector", rk_tok)
        for h4 in range(nheads):
            rec.op("vector", lambda E, h4=h4: E.tensor_scalar_mul(
                NT_SCR[:, h4 * 128:(h4 + 1) * 128], ps_src[:, h4 * 128:(h4 + 1) * 128],
                stat(RKC0 + h4)))
        nh = nheads
        nt3 = NT_SCR[:, 0:nh * 128].rearrange("p (h d) -> p h d", h=nh)
        x1 = nt3[:, :, 0:128:2]
        x2 = nt3[:, :, 1:128:2]
        cos3 = CSST[cs_slot][:, 0:nh * 64].rearrange("p (h d) -> p h d", h=nh)
        sin3 = CSST[cs_slot][:, 256:256 + nh * 64].rearrange("p (h d) -> p h d", h=nh)
        s1 = RO_S1[:, 0:nh * 64].rearrange("p (h d) -> p h d", h=nh)
        s2 = RO_S2[:, 0:nh * 64].rearrange("p (h d) -> p h d", h=nh)
        s3 = RO_S3[:, 0:nh * 64].rearrange("p (h d) -> p h d", h=nh)
        s4 = RO_S4[:, 0:nh * 64].rearrange("p (h d) -> p h d", h=nh)
        o3 = RP_SCR[:, 0:nh * 128].rearrange("p (h d) -> p h d", h=nh)
        rec.wait("vector", free.get(("RP",)))
        rec.op("vector", lambda E: E.tensor_mul(s1, x1, cos3))
        rec.op("vector", lambda E: E.tensor_mul(s2, x2, sin3))
        rec.op("vector", lambda E: E.tensor_mul(s3, x1, sin3))
        rec.op("vector", lambda E: E.tensor_mul(s4, x2, cos3))
        rec.op("vector", lambda E: E.tensor_sub(o3[:, :, 0:64], s1, s2))
        rec.op("vector", lambda E: E.tensor_add(o3[:, :, 64:128], s3, s4), inc=(SV, 1))
        return rec.now(SV)

    for tt in range(NT):
        if tt < HT:
            dest = (lambda dc, tt=tt: XN[:, dc, tt * 128:(tt + 1) * 128])
            guard_key = None
        else:
            dest = (lambda dc: XNT_TMP[:, dc, :])
            guard_key = ("XNT",)
        xnT_tok = emit_xn_tile(tt, dest, do_router=True, do_lo=True,
                               evac_guard_key=guard_key, dup_load=(tt == 0))

        cb = tt % 2
        rec.wait("sync", free.get(("CSST", cb)))
        rec.op("sync", lambda E, tt=tt, cb=cb: E.dma_start(
            out=CSST[cb], in_=cs_d[tt * 128:(tt + 1) * 128, :]), inc=(SCS, 16))
        if tt == 0:
            rec.op("sync", lambda E, tt=tt, cb=cb: E.dma_start(
                out=CSST[cb], in_=cs_d[tt * 128:(tt + 1) * 128, :]), inc=(SCS, 16))
        cs_tok = rec.now(SCS)

        rec.wait("tensor", wkv_tok)
        rec.wait("tensor", free.get(("PS", 0)))
        for dc in range(DC):
            rec.op("tensor", lambda E, d=dest(dc), dc=dc: E.matmul(
                PS[0][:], d, WKST[:, dc, :], start=(dc == 0), stop=(dc == DC - 1)),
                inc=(STn, 1) if dc == DC - 1 else None)
        k_tok = rec.now(STn)
        rec.wait("tensor", free.get(("PS", 1)))
        for dc in range(DC):
            rec.op("tensor", lambda E, d=dest(dc), dc=dc: E.matmul(
                PS[1][:], d, WVST[:, dc, :], start=(dc == 0), stop=(dc == DC - 1)),
                inc=(STn, 1) if dc == DC - 1 else None)
        v_tok = rec.now(STn)
        free[("XNT",)] = v_tok

        rec.wait("scalar", k_tok)
        rec.wait("vector", cs_tok)
        rr_tok = emit_rms_rope(PS[0], KV, cb)
        free[("CSST", cb)] = rr_tok
        free[("PS", 0)] = rr_tok
        rec.wait("tensor", rr_tok)
        for h4 in range(KV):
            j = h4 % 2
            rec.wait("tensor", free.get(("PT", j)))
            rec.op("tensor", lambda E, h4=h4, j=j: E.transpose(
                PT[j], RP_SCR[:, h4 * 128:(h4 + 1) * 128], ID16), inc=(STn, 1))
            tr_tok = rec.now(STn)
            rec.wait("vector", tr_tok)
            rec.op("vector", lambda E, h4=h4, j=j, tt=tt: E.tensor_copy(
                KT[:, h4, tt * 128:(tt + 1) * 128], PT[j]), inc=(SV, 1))
            free[("PT", j)] = rec.now(SV)
        free[("RP",)] = (STn, STn.n)

        rec.wait("vector", v_tok)
        rec.op("vector", lambda E, tt=tt: E.tensor_copy(
            VT[:, :, tt, 0:128],
            PS[1][:].rearrange("p (kv d) -> p kv d", kv=KV)), inc=(SV, 1))
        free[("PS", 1)] = rec.now(SV)

    def emit_q_half(half, xn_ready):
        for hq in range(4):
            wb = hq % 2
            rec.wait("gpsimd", free.get(("WST", wb)))
            rec.op("gpsimd", lambda E, hq=hq, wb=wb: E.dma_start(
                out=WST[wb],
                in_=wqT_d[:, hq * 512:(hq + 1) * 512].rearrange("(dc p) n -> p dc n", p=128)),
                inc=(SW, 16))
            wq_tok = rec.now(SW)
            for ti in range(HT):
                tt = half * HT + ti
                cb = ti % 2
                rec.wait("sync", free.get(("CSST", cb)))
                rec.op("sync", lambda E, tt=tt, cb=cb: E.dma_start(
                    out=CSST[cb], in_=cs_d[tt * 128:(tt + 1) * 128, :]), inc=(SCS, 16))
                cs_tok = rec.now(SCS)
                rec.wait("tensor", wq_tok)
                rec.wait("tensor", xn_ready)
                rec.wait("tensor", free.get(("PS", 0)))
                for dc in range(DC):
                    rec.op("tensor", lambda E, dc=dc, ti=ti, wb=wb: E.matmul(
                        PS[0][:], XN[:, dc, ti * 128:(ti + 1) * 128], WST[wb][:, dc, :],
                        start=(dc == 0), stop=(dc == DC - 1)),
                        inc=(STn, 1) if dc == DC - 1 else None)
                q_tok = rec.now(STn)
                rec.wait("scalar", q_tok)
                rec.wait("vector", cs_tok)
                rr_tok = emit_rms_rope(PS[0], 4, cb)
                free[("CSST", cb)] = rr_tok
                free[("PS", 0)] = rr_tok
                if ti == HT - 1:
                    free[("WST", wb)] = rr_tok
                rec.wait("tensor", rr_tok)
                for h4 in range(4):
                    j = h4 % 2
                    head = hq * 4 + h4
                    rec.wait("tensor", free.get(("PT", j)))
                    rec.op("tensor", lambda E, h4=h4, j=j: E.transpose(
                        PT[j], RP_SCR[:, h4 * 128:(h4 + 1) * 128], ID16), inc=(STn, 1))
                    tr_tok = rec.now(STn)
                    rec.wait("vector", tr_tok)
                    rec.op("vector", lambda E, head=head, j=j, ti=ti: E.tensor_copy(
                        QT[:, head, ti * 128:(ti + 1) * 128], PT[j]), inc=(SV, 1))
                    free[("PT", j)] = rec.now(SV)
                free[("RP",)] = (STn, STn.n)
        return rec.now(SV)

    def vgap():
        rec.op("vector", lambda E: E.tensor_copy(STAT[:, 31:32], STAT[:, 30:31]))

    def emit_router_weights():
        for tt in range(NT):
            lg = LGT[:, tt, :]
            rec.op("vector", lambda E, lg=lg: E.reduce_max(stat(M1C), lg, axis=AX.X),
                   inc=(SV, 1))
            rec.wait("vector", rec.now(SV))
            rec.op("vector", lambda E, lg=lg: E.tensor_scalar(
                ST8A[:], lg, stat(M1C), None, op0=ALU.is_equal))
            vgap()
            rec.op("vector", lambda E, lg=lg: E.scalar_tensor_tensor(
                ST8B[:], ST8A[:], -1e30, lg, op0=ALU.mult, op1=ALU.add))
            vgap()
            rec.op("vector", lambda E: E.reduce_max(stat(M2CC), ST8B[:], axis=AX.X),
                   inc=(SV, 1))
            rec.wait("vector", rec.now(SV))
            rec.op("vector", lambda E: E.tensor_sub(stat(DMC), stat(M2CC), stat(M1C)),
                   inc=(SV, 1))
            dm_tok = rec.now(SV)
            rec.wait("scalar", dm_tok)
            rec.op("scalar", lambda E: E.activation(stat(EWC), stat(DMC), ACT.Exp),
                   inc=(SSC, 1))
            ew_tok = rec.now(SSC)
            rec.wait("vector", ew_tok)
            rec.op("vector", lambda E: E.tensor_scalar(
                stat(EW2), stat(EWC), 1.0, None, op0=ALU.add))
            vgap()
            rec.op("vector", lambda E: E.reciprocal(stat(W1CC), stat(EW2)), inc=(SV, 1))
            rec.wait("vector", rec.now(SV))
            rec.op("vector", lambda E: E.tensor_scalar(
                stat(W2CC), stat(W1CC), -1.0, 1.0, op0=ALU.mult, op1=ALU.add), inc=(SV, 1))
            w2_tok = rec.now(SV)
            rec.op("vector", lambda E: E.tensor_scalar_mul(ST8B[:], ST8A[:], stat(W1CC)))
            rec.op("vector", lambda E, lg=lg: E.scalar_tensor_tensor(
                ST8C[:], ST8A[:], -1e30, lg, op0=ALU.mult, op1=ALU.add))
            vgap()
            rec.op("vector", lambda E: E.tensor_scalar(
                ST8C[:], ST8C[:], stat(M2CC), None, op0=ALU.is_equal))
            rec.wait("vector", w2_tok)
            vgap()
            rec.op("vector", lambda E, tt=tt: E.scalar_tensor_tensor(
                WRT[:, tt, :], ST8C[:], stat(W2CC), ST8B[:], op0=ALU.mult, op1=ALU.add),
                inc=(SV, 1))

    def emit_attn_half(half):
        rec.wait("tensor", rec.now(SV))
        rec.wait("vector", rec.now(STn))
        rec.wait("vector", consts_tok)
        for h in range(H):
            kv = h // 4
            for cr in range(2):
                cq = half * 2 + cr
                kclo = max(0, 4 * cq - 8)
                kchi = 4 * cq + 3
                kcs = list(range(kclo, kchi + 1))
                pv_first = [True] * 4
                for i, kc in enumerate(kcs):
                    sb_ = i % 2
                    rec.wait("tensor", free.get(("PS", 4 + sb_)))
                    rec.op("tensor", lambda E, kc=kc, sb_=sb_, kv=kv, h=h, cr=cr: E.matmul(
                        PS[4 + sb_][:], KT[:, kv, kc * 128:(kc + 1) * 128],
                        QT[:, h, cr * 512:(cr + 1) * 512], start=True, stop=True),
                        inc=(STn, 1))
                    s_tok = rec.now(STn)
                    rec.wait("scalar", s_tok)
                    rec.wait("scalar", free.get(("ATB", sb_)))
                    rec.op("scalar", lambda E, sb_=sb_: E.activation(
                        ATB[sb_][:], PS[4 + sb_][:], ACT.Exp, scale=SCALE), inc=(SSC, 1))
                    exp_tok = rec.now(SSC)
                    free[("PS", 4 + sb_)] = exp_tok
                    a_ready = exp_tok
                    need_vec = False
                    causal_qr = None
                    drop_row0 = (kc == 4 * cq - 8)
                    if 4 * cq <= kc <= 4 * cq + 3:
                        causal_qr = kc - 4 * cq
                        need_vec = True
                    if drop_row0:
                        need_vec = True
                    if need_vec:
                        rec.wait("vector", exp_tok)
                        if causal_qr is not None:
                            rec.op("vector", lambda E, sb_=sb_, qr=causal_qr: E.tensor_mul(
                                ATB[sb_][:, qr * 128:(qr + 1) * 128],
                                ATB[sb_][:, qr * 128:(qr + 1) * 128], MASK_C),
                                inc=None if drop_row0 else (SV, 1))
                        if drop_row0:
                            rec.op("vector", lambda E, sb_=sb_: E.memset(
                                ATB[sb_][0:1, :], 0.0), inc=(SV, 1))
                        a_ready = rec.now(SV)
                    rec.wait("tensor", a_ready)
                    qs_lo = max(0, kc - 4 * cq)
                    qs_hi = 3
                    valid_qrs = list(range(qs_lo, qs_hi + 1))
                    for vi, qr in enumerate(valid_qrs):
                        qs = 4 * cq + qr
                        is_stop = (kc == qs)
                        is_last = (vi == len(valid_qrs) - 1)
                        if pv_first[qr]:
                            rec.wait("tensor", free.get(("PS", qr)))
                        rec.op("tensor", lambda E, sb_=sb_, qr=qr, kv=kv, kc=kc,
                               f=pv_first[qr], st=is_stop: E.matmul(
                            PS[qr][:, 0:129], ATB[sb_][:, qr * 128:(qr + 1) * 128],
                            VT[:, kv, kc, :].squeeze(), start=f, stop=st),
                            inc=(STn, 1) if (is_stop or is_last) else None)
                        pv_first[qr] = False
                        if is_stop:
                            pv_tok = rec.now(STn)
                            ti = qs - half * HT
                            rec.wait("vector", pv_tok)
                            rec.op("vector", lambda E, qr=qr: E.reciprocal(
                                stat(RCOL), PS[qr][:, 128:129]), inc=(SV, 1))
                            rc_tok = rec.now(SV)
                            rec.wait("vector", rc_tok)
                            rec.op("vector", lambda E, qr=qr, ti=ti, h=h: E.tensor_scalar_mul(
                                AO[:, ti, h * 128:(h + 1) * 128], PS[qr][:, 0:128], stat(RCOL)),
                                inc=(SV, 1))
                            free[("PS", qr)] = rec.now(SV)
                    free[("ATB", sb_)] = (STn, STn.n)

    def emit_atT_half():
        rec.wait("tensor", rec.now(SV))
        for ti in range(HT):
            for hc in range(H):
                j = (ti * H + hc) % 2
                rec.wait("tensor", free.get(("PT", j)))
                rec.op("tensor", lambda E, ti=ti, hc=hc, j=j: E.transpose(
                    PT[j], AO[:, ti, hc * 128:(hc + 1) * 128], ID16), inc=(STn, 1))
                tr_tok = rec.now(STn)
                rec.wait("vector", tr_tok)
                rec.op("vector", lambda E, ti=ti, hc=hc, j=j: E.tensor_copy(
                    QT[:, hc, ti * 128:(ti + 1) * 128], PT[j]), inc=(SV, 1))
                free[("PT", j)] = rec.now(SV)

    def emit_wo_half():
        rec.wait("tensor", rec.now(SV))
        for dq in range(4):
            wb = dq % 2
            rec.wait("gpsimd", free.get(("WST", wb)))
            rec.op("gpsimd", lambda E, dq=dq, wb=wb: E.dma_start(
                out=WST[wb],
                in_=woT_d[:, dq * 512:(dq + 1) * 512].rearrange("(hc p) n -> p hc n", p=128)),
                inc=(SW, 16))
            wo_tok = rec.now(SW)
            rec.wait("tensor", wo_tok)
            for ti in range(HT):
                pb = ti % 2
                rec.wait("tensor", free.get(("PS", pb)))
                for hc in range(H):
                    rec.op("tensor", lambda E, hc=hc, ti=ti, wb=wb, pb=pb: E.matmul(
                        PS[pb][:], QT[:, hc, ti * 128:(ti + 1) * 128], WST[wb][:, hc, :],
                        start=(hc == 0), stop=(hc == H - 1)),
                        inc=(STn, 1) if hc == H - 1 else None)
                y_tok = rec.now(STn)
                rec.wait("vector", y_tok)
                rec.op("vector", lambda E, ti=ti, dq=dq, pb=pb: E.tensor_copy(
                    AO[:, ti, dq * 512:(dq + 1) * 512], PS[pb][:]), inc=(SV, 1))
                free[("PS", pb)] = rec.now(SV)
                if ti == HT - 1:
                    free[("WST", wb)] = rec.now(SV)

    def emit_ffn_half(half):
        t_now = rec.now(STn)
        rec.wait("vector", t_now)
        rec.wait("sync", t_now)
        rec.wait("gpsimd", t_now)
        for ti in range(HT):
            tt = half * HT + ti
            b = ti % 2
            rec.wait("sync", free.get(("FXSTG", b)))
            rec.op("sync", lambda E, tt=tt, b=b: E.dma_start(
                out=FXSTG[b], in_=x_d[tt * 128:(tt + 1) * 128, :]), inc=(SX, 16))
            if ti == 0:
                rec.op("sync", lambda E, tt=tt, b=b: E.dma_start(
                    out=FXSTG[b], in_=x_d[tt * 128:(tt + 1) * 128, :]), inc=(SX, 16))
            x_tok = rec.now(SX)
            rec.wait("vector", x_tok)
            rec.op("vector", lambda E, ti=ti, b=b: E.tensor_tensor(
                ACC[:, ti, :], FXSTG[b], AO[:, ti, :], ALU.add), inc=(SV, 1))
            free[("FXSTG", b)] = rec.now(SV)
        fold_tok = rec.now(SV)
        rec.wait("gpsimd", fold_tok)

        for e in range(NE):
            for fq in range(4):
                wb = fq % 2
                rec.wait("gpsimd", free.get(("W1ST", wb)))
                rec.op("gpsimd", lambda E, e=e, fq=fq, wb=wb: E.dma_start(
                    out=W1ST[wb],
                    in_=w1_d[e, :, fq * 256:(fq + 1) * 256].rearrange("(dc p) f -> p dc f", p=128)),
                    inc=(SW, 16))
                rec.op("gpsimd", lambda E, e=e, fq=fq, wb=wb: E.dma_start(
                    out=W2ST[wb],
                    in_=w2_d[e, :, fq * 256:(fq + 1) * 256].rearrange("(dc p) f -> p dc f", p=128)),
                    inc=(SW, 16))
                w12_tok = rec.now(SW)
                rec.wait("tensor", w12_tok)
                for fcl in range(2):
                    fc = fq * 2 + fcl
                    for tq in range(2):
                        rec.wait("tensor", free.get(("PS", 2)))
                        for dc in range(DC):
                            rec.op("tensor", lambda E, dc=dc, fcl=fcl, tq=tq, wb=wb: E.matmul(
                                PS[2][:], W1ST[wb][:, dc, fcl * 128:(fcl + 1) * 128],
                                XN[:, dc, tq * 512:(tq + 1) * 512],
                                start=(dc == 0), stop=(dc == DC - 1)),
                                inc=(STn, 1) if dc == DC - 1 else None)
                        g_tok = rec.now(STn)
                        rec.wait("tensor", free.get(("PS", 3)))
                        for dc in range(DC):
                            rec.op("tensor", lambda E, dc=dc, fcl=fcl, tq=tq, wb=wb: E.matmul(
                                PS[3][:], W2ST[wb][:, dc, fcl * 128:(fcl + 1) * 128],
                                XN[:, dc, tq * 512:(tq + 1) * 512],
                                start=(dc == 0), stop=(dc == DC - 1)),
                                inc=(STn, 1) if dc == DC - 1 else None)
                        vv_tok = rec.now(STn)
                        rec.wait("scalar", g_tok)
                        rec.wait("scalar", free.get(("RP",)))
                        rec.op("scalar", lambda E: E.activation(
                            RP_SCR[:], PS[2][:], ACT.Silu), inc=(SSC, 1))
                        si_tok = rec.now(SSC)
                        free[("PS", 2)] = si_tok
                        rec.wait("vector", si_tok)
                        rec.wait("vector", vv_tok)
                        rec.wait("vector", free.get(("HBUF",)))
                        rec.op("vector", lambda E, fc=fc, tq=tq: E.tensor_tensor(
                            HBUF[:, fc, tq * 512:(tq + 1) * 512], RP_SCR[:], PS[3][:],
                            ALU.mult), inc=(SV, 1))
                        free[("PS", 3)] = rec.now(SV)
                        free[("RP",)] = rec.now(SV)
                        if fcl == 1 and tq == 1:
                            free[("W1ST", wb)] = rec.now(SV)
            h_tok = rec.now(SV)
            for dq in range(4):
                wb = dq % 2
                rec.wait("gpsimd", free.get(("W3ST", wb)))
                rec.op("gpsimd", lambda E, e=e, dq=dq, wb=wb: E.dma_start(
                    out=W3ST[wb],
                    in_=w3_d[e, :, dq * 512:(dq + 1) * 512].rearrange("(fc p) d -> p fc d", p=128)),
                    inc=(SW, 16))
                w3_tok = rec.now(SW)
                rec.wait("tensor", w3_tok)
                rec.wait("tensor", h_tok)
                for ti in range(HT):
                    tt = half * HT + ti
                    pb = ti % 2
                    rec.wait("tensor", free.get(("PS", pb)))
                    for fc in range(8):
                        rec.op("tensor", lambda E, fc=fc, ti=ti, wb=wb, pb=pb: E.matmul(
                            PS[pb][:], HBUF[:, fc, ti * 128:(ti + 1) * 128], W3ST[wb][:, fc, :],
                            start=(fc == 0), stop=(fc == 7)),
                            inc=(STn, 1) if fc == 7 else None)
                    o_tok = rec.now(STn)
                    if dq == 3 and ti == HT - 1:
                        free[("HBUF",)] = (STn, STn.n)
                    rec.wait("vector", o_tok)
                    rec.op("vector", lambda E, ti=ti, dq=dq, pb=pb, tt=tt, e=e:
                           E.scalar_tensor_tensor(
                               ACC[:, ti, dq * 512:(dq + 1) * 512], PS[pb][:],
                               WRT[:, tt, e:e + 1], ACC[:, ti, dq * 512:(dq + 1) * 512],
                               op0=ALU.mult, op1=ALU.add), inc=(SV, 1))
                    free[("PS", pb)] = rec.now(SV)
                    if ti == HT - 1:
                        free[("W3ST", wb)] = rec.now(SV)
        acc_tok = rec.now(SV)
        rec.wait("sync", acc_tok)
        for ti in range(HT):
            tt = half * HT + ti
            rec.op("sync", lambda E, tt=tt, ti=ti: E.dma_start(
                out=out_d[tt * 128:(tt + 1) * 128, :], in_=ACC[:, ti, :]), inc=(SST, 16))
        st_tok = rec.now(SST)
        for k in (("WST", 0), ("WST", 1), ("XSTG", 0), ("XSTG", 1),
                  ("CSST", 0), ("CSST", 1)):
            free[k] = st_tok
        free[("XNTOK",)] = st_tok

    def dbg_dump(key, src):
        if not debug:
            return
        tok = rec.now(SV)
        rec.wait("sync", tok)
        rec.op("sync", lambda E, key=key, src=src: E.dma_start(out=dbg[key], in_=src),
               inc=(SST, 16))

    pa_tok = rec.now(SV)
    emit_q_half(0, pa_tok)
    emit_router_weights()
    emit_attn_half(0)
    emit_atT_half()
    emit_wo_half()
    if debug:
        mid_tok = rec.now(SV)
        rec.wait("sync", mid_tok)
        rec.op("sync", lambda E: E.dma_start(out=dbg["xnT"], in_=XN[:]), inc=(SST, 16))
        rec.op("sync", lambda E: E.dma_start(out=dbg["y"], in_=AO[:]), inc=(SST, 16))
        rec.op("sync", lambda E: E.dma_start(out=dbg["qT"], in_=QT[:]), inc=(SST, 16))
        dump_done = rec.now(SST)
        rec.wait("vector", dump_done)
        rec.wait("tensor", dump_done)
        rec.wait("scalar", dump_done)
        rec.wait("gpsimd", dump_done)
    emit_ffn_half(0)

    rec.wait("vector", rec.now(STn))
    rec.wait("scalar", rec.now(STn))
    for ti in range(HT):
        tt = HT + ti
        dest = (lambda dc, ti=ti: XN[:, dc, ti * 128:(ti + 1) * 128])
        emit_xn_tile(tt, dest, do_router=False, do_lo=False, dup_load=(ti == 0))
    xn2_tok = rec.now(SV)
    emit_q_half(1, xn2_tok)
    emit_attn_half(1)
    emit_atT_half()
    emit_wo_half()
    emit_ffn_half(1)

    if debug:
        end_tok = rec.now(SV)
        rec.wait("sync", end_tok)
        rec.wait("sync", rec.now(SST))
        for key, src in (("kT", KT[:]), ("vT", VT[:]), ("W", WRT[:]),
                         ("lg", LGT[:]), ("ao", AO[:])):
            rec.op("sync", lambda E, key=key, src=src: E.dma_start(out=dbg[key], in_=src),
                   inc=(SST, 16))

    for s_ in (SST, SX, SCS, SW):
        rec.wait("gpsimd", rec.now(s_))

    with nc.Block() as block:
        def mk(eng_name):
            def body(E):
                for f in rec.ops[eng_name]:
                    f(E)
            return body
        block.sync(mk("sync"))
        block.tensor(mk("tensor"))
        block.vector(mk("vector"))
        block.scalar(mk("scalar"))
        block.gpsimd(mk("gpsimd"))

    counts = {k: len(v) for k, v in rec.ops.items()}
    print("bass program op counts:", counts, "total:", sum(counts.values()))
    return nc


def _host_prep(inputs):
    x = np.ascontiguousarray(np.asarray(inputs["L0_attn_x"], dtype=np.float32))
    wqT = np.ascontiguousarray(np.asarray(inputs["L0_attn_wq"], dtype=np.float32).T)
    wkT = np.ascontiguousarray(np.asarray(inputs["L0_attn_wk"], dtype=np.float32).T)
    wvT = np.ascontiguousarray(np.asarray(inputs["L0_attn_wv"], dtype=np.float32).T)
    woT = np.ascontiguousarray(np.asarray(inputs["L0_attn_wo"], dtype=np.float32).T)
    rwT = np.ascontiguousarray(np.asarray(inputs["L0_ffn_router_w"], dtype=np.float32).T)
    w1 = np.ascontiguousarray(np.asarray(inputs["L0_ffn_w1"], dtype=np.float32))
    w2 = np.ascontiguousarray(np.asarray(inputs["L0_ffn_w2"], dtype=np.float32))
    w3 = np.ascontiguousarray(np.asarray(inputs["L0_ffn_w3"], dtype=np.float32))

    inv_freq = 10000.0 ** (-(np.arange(0, DH, 2, dtype=np.float64) / DH))
    ang = np.outer(np.arange(T, dtype=np.float64), inv_freq)
    cos4 = np.tile(np.cos(ang).astype(np.float32), (1, 4))
    sin4 = np.tile(np.sin(ang).astype(np.float32), (1, 4))
    cs_tab = np.ascontiguousarray(np.concatenate([cos4, sin4], axis=1))

    ident = np.eye(128, dtype=ml_dtypes.bfloat16)
    kk = np.arange(128)[:, None]
    qq = np.arange(128)[None, :]
    mask_c = (kk <= qq).astype(ml_dtypes.bfloat16)
    mask_w = (kk > qq).astype(ml_dtypes.bfloat16)
    consts = np.ascontiguousarray(np.concatenate([ident, mask_c, mask_w], axis=1))

    shared = dict(wqT=wqT, wkT=wkT, wvT=wvT, woT=woT, rwT=rwT,
                  w1=w1, w2=w2, w3=w3, cs_tab=cs_tab, consts=consts)
    in_maps = []
    for c in range(8):
        m = dict(shared)
        m["x"] = np.ascontiguousarray(x[c])
        in_maps.append(m)
    return in_maps


def kernel(**inputs):
    in_maps = _host_prep(inputs)
    nc = build_nc(debug=DEBUG)
    res = run_bass_kernel_spmd(nc, in_maps, core_ids=list(range(8)))
    out = np.stack([np.asarray(r["out"]) for r in res.results], axis=0)
    if DEBUG:
        kernel.last_results = res.results
    return out


# version 04
# speedup vs baseline: 1.2938x; 1.2938x over previous
import math
import numpy as np
import ml_dtypes
import concourse.bass as bass
import concourse.mybir as mybir
from concourse.bass_utils import run_bass_kernel_spmd

F32 = mybir.dt.float32
BF16 = mybir.dt.bfloat16
AX = mybir.AxisListType
ALU = mybir.AluOpType
ACT = mybir.ActivationFunctionType

T = 2048
D = 2048
H = 16
KV = 4
DH = 128
FF = 1024
NE = 8
NT = 16
DC = 16
SCALE = 1.0 / math.sqrt(DH)
LN_EPS = 1e-5
RMS_EPS = 1e-6
HALF_T = 1024
HT = 8

DEBUG = False

ENGS = ("sync", "tensor", "vector", "scalar", "gpsimd")


class Sem:
    def __init__(self, handle, name):
        self.h = handle
        self.name = name
        self.n = 0


class Rec:
    def __init__(self):
        self.ops = {k: [] for k in ENGS}
        self._last_wait = {}

    def op(self, eng, fn, inc=None):
        if inc is not None:
            sem, k = inc
            sem.n += k

            def run(E, fn=fn, h=sem.h, k=k):
                fn(E).then_inc(h, k)
            self.ops[eng].append(run)
            return (sem, sem.n)
        else:
            self.ops[eng].append(lambda E, fn=fn: fn(E))
            return None

    def wait(self, eng, token):
        if token is None:
            return
        sem, v = token
        if v <= 0:
            return
        key = (eng, sem.name)
        if self._last_wait.get(key, -1) >= v:
            return
        self._last_wait[key] = v
        self.ops[eng].append(lambda E, h=sem.h, v=v: E.wait_ge(h, v))

    def now(self, sem):
        return (sem, sem.n)


def build_nc(debug=False):
    nc = bass.Bass("TRN2", target_bir_lowering=False)

    x_d = nc.dram_tensor("x", [T, D], F32, kind="ExternalInput").ap()
    wqT_d = nc.dram_tensor("wqT", [D, H * DH], F32, kind="ExternalInput").ap()
    wkT_d = nc.dram_tensor("wkT", [D, KV * DH], F32, kind="ExternalInput").ap()
    wvT_d = nc.dram_tensor("wvT", [D, KV * DH], F32, kind="ExternalInput").ap()
    woT_d = nc.dram_tensor("woT", [H * DH, D], F32, kind="ExternalInput").ap()
    rwT_d = nc.dram_tensor("rwT", [D, NE], F32, kind="ExternalInput").ap()
    w1_d = nc.dram_tensor("w1", [NE, D, FF], F32, kind="ExternalInput").ap()
    w2_d = nc.dram_tensor("w2", [NE, D, FF], F32, kind="ExternalInput").ap()
    w3_d = nc.dram_tensor("w3", [NE, FF, D], F32, kind="ExternalInput").ap()
    cs_d = nc.dram_tensor("cs_tab", [T, 512], F32, kind="ExternalInput").ap()
    const_d = nc.dram_tensor("consts", [128, 3 * 128], BF16, kind="ExternalInput").ap()

    out_d = nc.dram_tensor("out", [T, D], F32, kind="ExternalOutput").ap()

    dbg = {}
    if debug:
        dbg["xnT"] = nc.dram_tensor("dbg_xnT", [128, DC, HALF_T], BF16, kind="ExternalOutput").ap()
        dbg["kT"] = nc.dram_tensor("dbg_kT", [128, KV, T], BF16, kind="ExternalOutput").ap()
        dbg["vT"] = nc.dram_tensor("dbg_vT", [128, KV, NT, 129], BF16, kind="ExternalOutput").ap()
        dbg["lg"] = nc.dram_tensor("dbg_lg", [128, NT, NE], F32, kind="ExternalOutput").ap()
        dbg["W"] = nc.dram_tensor("dbg_W", [128, NT, NE], F32, kind="ExternalOutput").ap()
        dbg["qT"] = nc.dram_tensor("dbg_qT", [128, H, HALF_T], BF16, kind="ExternalOutput").ap()
        dbg["ao"] = nc.dram_tensor("dbg_ao", [128, HT, D], BF16, kind="ExternalOutput").ap()
        dbg["y"] = nc.dram_tensor("dbg_y", [128, HT, D], BF16, kind="ExternalOutput").ap()

    ARENA_SIZE = 209728
    arena = nc.alloc_sbuf_tensor("ARENA", [128, ARENA_SIZE], mybir.dt.uint8)
    ARENA_BASE = nc.lookup_mloc(arena).addr

    def sb(name, shape, dtype, off):
        return nc.alloc_sbuf_tensor_at(name, shape, dtype, offset=ARENA_BASE + off).ap()

    XN = sb("XN", [128, DC, HALF_T], BF16, 0)
    QT = sb("QT", [128, H, HALF_T], BF16, 32768)
    SQSCR = sb("SQSCR", [128, D], F32, 32768)
    XLO_TMP = sb("XLO_TMP", [128, DC, 128], BF16, 40960)
    HBUF = sb("HBUF", [128, 8, HALF_T], BF16, 32768)
    FXSTG = [sb("FXSTG0", [128, D], F32, 32768),
             sb("FXSTG1", [128, D], F32, 40960)]
    W3ST = [sb("W3ST0", [128, 8, 512], BF16, 49152),
            sb("W3ST1", [128, 8, 512], BF16, 57344)]

    AO = sb("AO", [128, HT, D], BF16, 65536)
    WKST = sb("WKST", [128, DC, 512], BF16, 65536)
    WVST = sb("WVST", [128, DC, 512], BF16, 81920)
    W1ST = [sb("W1ST0", [128, DC, 256], BF16, 65536),
            sb("W1ST1", [128, DC, 256], BF16, 73728)]
    W2ST = [sb("W2ST0", [128, DC, 256], BF16, 81920),
            sb("W2ST1", [128, DC, 256], BF16, 90112)]

    ACC = sb("ACC", [128, HT, D], F32, 98304)
    WST = [sb("WST0", [128, DC, 512], BF16, 98304),
           sb("WST1", [128, DC, 512], BF16, 114688)]
    XSTG = [sb("XSTG0", [128, D], F32, 131072),
            sb("XSTG1", [128, D], F32, 139264)]
    CSST = [sb("CSST0", [128, 512], F32, 147456),
            sb("CSST1", [128, 512], F32, 149504)]
    XNTOK = sb("XNTOK", [128, D], BF16, 151552)
    XNLO = sb("XNLO", [128, D], BF16, 155648)
    XNT_TMP = sb("XNT_TMP", [128, DC, 128], BF16, 159744)

    KT = sb("KT", [128, KV, T], BF16, 163840)
    VT = sb("VT", [128, KV, NT, 129], BF16, 180224)
    ID16 = sb("ID16", [128, 128], BF16, 196736)
    MASK_C = sb("MASK_C", [128, 128], BF16, 196992)
    MASK_W = sb("MASK_W", [128, 128], BF16, 197248)
    RWTF = sb("RWTF", [128, DC, NE], F32, 197504)
    RWT_HI = sb("RWT_HI", [128, DC, NE], BF16, 198016)
    RWT_LO = sb("RWT_LO", [128, DC, NE], BF16, 198272)
    LGT = sb("LGT", [128, NT, NE], F32, 198528)
    WRT = sb("WRT", [128, NT, NE], F32, 199040)
    STAT = sb("STAT", [128, 32], F32, 199552)
    ST8A = sb("ST8A", [128, NE], F32, 199680)
    ST8B = sb("ST8B", [128, NE], F32, 199712)
    ATB = [sb("ATB0", [128, 512], BF16, 199744),
           sb("ATB1", [128, 512], BF16, 200768)]
    NT_SCR = sb("NT_SCR", [128, 512], F32, 201792)
    RO_S1 = sb("RO_S1", [128, 256], F32, 203840)
    RO_S2 = sb("RO_S2", [128, 256], F32, 204864)
    RP_SCR = sb("RP_SCR", [128, 512], BF16, 205888)
    RMS_SQ = sb("RMS_SQ", [128, 128], F32, 206912)
    RO_S3 = sb("RO_S3", [128, 256], F32, 207424)
    RO_S4 = sb("RO_S4", [128, 256], F32, 208448)
    ST8C = sb("ST8C", [128, NE], F32, 209472)

    PS = [nc.alloc_psum_tensor(f"PS{i}", [128, 512], F32).ap() for i in range(6)]
    PT = [nc.alloc_psum_tensor(f"PT{i}", [128, 128], BF16).ap() for i in range(2)]

    rec = Rec()
    sems = {}
    for name in ("X", "CS", "W", "ST", "T", "V", "SC"):
        sems[name] = Sem(nc.alloc_semaphore(name=f"S_{name}"), name)
    SX, SCS, SW, SST, STn, SV, SSC = (sems[k] for k in ("X", "CS", "W", "ST", "T", "V", "SC"))

    nc.all_engine_barrier()
    sem_nums = [s.h.num for s in sems.values()]
    nc.gpsimd.dma_reset(range(min(sem_nums), max(sem_nums) + 1))
    for s in sems.values():
        nc.gpsimd.sem_clear(s.h)
    nc.all_engine_barrier()
    warm = nc.alloc_semaphore(name="S_WARM")
    nc.sync.dma_start(out=XSTG[0], in_=x_d[0:128, :]).then_inc(warm, 16)
    nc.gpsimd.dma_start(out=XSTG[1], in_=x_d[128:256, :]).then_inc(warm, 16)
    nc.gpsimd.wait_ge(warm, 32)
    for s in sems.values():
        nc.gpsimd.sem_clear(s.h)
    nc.all_engine_barrier()

    def stat(i):
        return STAT[:, i:i + 1]

    SUMS, MVc, EX2, M2C = 0, 1, 2, 3
    VARV, RSTD, NEGM, RCOL = 4, 5, 6, 7
    KRMS0 = 8
    M1C, M2CC, DMC, W1CC = 12, 13, 14, 15
    RKC0 = 16
    W2CC, EW2 = 20, 21
    SQc, STDV, EWC = 24, 25, 26

    free = {}

    for _rep in range(2):
        rec.op("sync", lambda E: E.dma_start(out=ID16, in_=const_d[:, 0:128]), inc=(SCS, 16))
        rec.op("sync", lambda E: E.dma_start(out=MASK_C, in_=const_d[:, 128:256]), inc=(SCS, 16))
        rec.op("sync", lambda E: E.dma_start(out=MASK_W, in_=const_d[:, 256:384]), inc=(SCS, 16))
        rec.op("sync", lambda E: E.dma_start(
            out=RWTF, in_=rwT_d.rearrange("(dc p) e -> p dc e", p=128)), inc=(SCS, 16))
    consts_tok = rec.now(SCS)
    rec.wait("vector", consts_tok)
    rec.op("vector", lambda E: E.tensor_copy(RWT_HI, RWTF), inc=(SV, 1))
    rec.wait("vector", rec.now(SV))
    rec.op("vector", lambda E: E.tensor_tensor(RWT_LO, RWTF, RWT_HI, ALU.subtract), inc=(SV, 1))
    rwhl_tok = rec.now(SV)
    rec.op("vector", lambda E: E.memset(
        VT[:, :, :, 128:129].rearrange("p a b c -> p (a b c)"), 1.0), inc=(SV, 1))

    for _rep in range(2):
        rec.op("gpsimd", lambda E: E.dma_start(
            out=WKST, in_=wkT_d.rearrange("(dc p) n -> p dc n", p=128)), inc=(SW, 16))
        rec.op("gpsimd", lambda E: E.dma_start(
            out=WVST, in_=wvT_d.rearrange("(dc p) n -> p dc n", p=128)), inc=(SW, 16))
    wkv_tok = rec.now(SW)

    def emit_xn_tile(tt, dest, do_router, do_lo, evac_guard_key=None, dup_load=False):
        b = tt % 2
        rec.wait("sync", free.get(("XSTG", b)))
        rec.op("sync", lambda E, tt=tt, b=b: E.dma_start(
            out=XSTG[b], in_=x_d[tt * 128:(tt + 1) * 128, :]), inc=(SX, 16))
        if dup_load:
            rec.op("sync", lambda E, tt=tt, b=b: E.dma_start(
                out=XSTG[b], in_=x_d[tt * 128:(tt + 1) * 128, :]), inc=(SX, 16))
        x_tok = rec.now(SX)

        rec.wait("vector", x_tok)
        rec.op("vector", lambda E, b=b: E.reduce_sum(stat(SUMS), XSTG[b], axis=AX.X))
        rec.wait("scalar", x_tok)
        rec.wait("scalar", free.get(("SQc",)))
        rec.op("scalar", lambda E, b=b: E.activation(
            SQSCR, XSTG[b], ACT.Square, accum_out=stat(SQc)), inc=(SSC, 1))
        sq_tok = rec.now(SSC)
        rec.wait("vector", sq_tok)
        rec.op("vector", lambda E: E.tensor_copy(STAT[:, 31:32], STAT[:, 30:31]))
        rec.op("vector", lambda E: E.tensor_scalar(stat(MVc), stat(SUMS), 1.0 / D, None, op0=ALU.mult))
        rec.op("vector", lambda E: E.tensor_scalar(
            stat(EX2), stat(SQc), 1.0 / D, LN_EPS, op0=ALU.mult, op1=ALU.add))
        rec.op("vector", lambda E: E.tensor_mul(stat(M2C), stat(MVc), stat(MVc)))
        rec.op("vector", lambda E: E.tensor_scalar(stat(NEGM), stat(MVc), -1.0, None, op0=ALU.mult))
        rec.op("vector", lambda E: E.tensor_sub(stat(VARV), stat(EX2), stat(M2C)),
               inc=(SV, 1))
        var_tok = rec.now(SV)
        rec.wait("scalar", var_tok)
        rec.op("scalar", lambda E: E.activation(stat(STDV), stat(VARV), ACT.Sqrt), inc=(SSC, 1))
        std_tok = rec.now(SSC)
        rec.wait("vector", std_tok)
        rec.op("vector", lambda E: E.reciprocal(stat(RSTD), stat(STDV)), inc=(SV, 1))
        rstd_tok = rec.now(SV)
        rec.wait("vector", rstd_tok)
        rec.wait("vector", free.get(("XNTOK",)))
        if do_lo:
            rec.op("vector", lambda E, b=b: E.tensor_scalar(
                SQSCR, XSTG[b], stat(NEGM), stat(RSTD), op0=ALU.add, op1=ALU.mult))
            rec.op("vector", lambda E: E.tensor_copy(XNTOK, SQSCR))
            rec.op("vector", lambda E: E.tensor_copy(STAT[:, 31:32], STAT[:, 30:31]))
            rec.op("vector", lambda E: E.tensor_tensor(XNLO, SQSCR, XNTOK, ALU.subtract),
                   inc=(SV, 1))
        else:
            rec.op("vector", lambda E, b=b: E.tensor_scalar(
                XNTOK, XSTG[b], stat(NEGM), stat(RSTD), op0=ALU.add, op1=ALU.mult),
                inc=(SV, 1))
        xn_tok = rec.now(SV)
        free[("XSTG", b)] = xn_tok
        free[("SQc",)] = xn_tok

        rec.wait("tensor", xn_tok)
        rec.wait("tensor", consts_tok)
        srcs = [(XNTOK, dest)]
        if do_lo:
            srcs.append((XNLO, lambda dc: XLO_TMP[:, dc, :]))
        for src, dst in srcs:
            for dc in range(DC):
                j = dc % 2
                rec.wait("tensor", free.get(("PT", j)))
                rec.op("tensor", lambda E, dc=dc, j=j, src=src: E.transpose(
                    PT[j], src[:, dc * 128:(dc + 1) * 128], ID16), inc=(STn, 1))
                tr_tok = rec.now(STn)
                rec.wait("vector", tr_tok)
                if evac_guard_key is not None:
                    rec.wait("vector", free.get(evac_guard_key))
                rec.op("vector", lambda E, j=j, d=dst(dc): E.tensor_copy(d, PT[j]), inc=(SV, 1))
                free[("PT", j)] = rec.now(SV)
        free[("XNTOK",)] = (STn, STn.n)
        xnT_tok = rec.now(SV)

        if do_router:
            rec.wait("tensor", xnT_tok)
            rec.wait("tensor", rwhl_tok)
            rec.wait("tensor", free.get(("PS", 5)))
            combos = [(dest, RWT_HI), (dest, RWT_LO),
                      ((lambda dc: XLO_TMP[:, dc, :]), RWT_HI)]
            nmm = 0
            for lt, rw in combos:
                for dc in range(DC):
                    first = nmm == 0
                    last = nmm == 3 * DC - 1
                    rec.op("tensor", lambda E, l0=lt(dc), r0=rw[:, dc, :], f=first, l=last:
                           E.matmul(PS[5][:, 0:NE], l0, r0, start=f, stop=l),
                           inc=(STn, 1) if last else None)
                    nmm += 1
            lg_tok = rec.now(STn)
            rec.wait("vector", lg_tok)
            rec.op("vector", lambda E, tt=tt: E.tensor_copy(LGT[:, tt, :], PS[5][:, 0:NE]),
                   inc=(SV, 1))
            free[("PS", 5)] = rec.now(SV)
        return xnT_tok

    def emit_rms_rope(ps_src, nheads, cs_slot):
        rec.wait("scalar", free.get(("KRMS",)))
        for h4 in range(nheads):
            rec.op("scalar", lambda E, h4=h4: E.activation(
                RMS_SQ, ps_src[:, h4 * 128:(h4 + 1) * 128], ACT.Square,
                accum_out=stat(KRMS0 + h4)),
                inc=(SSC, 1) if h4 == nheads - 1 else None)
        rs_tok = rec.now(SSC)
        rec.wait("vector", rs_tok)
        rec.op("vector", lambda E, n=nheads: E.tensor_scalar(
            STAT[:, KRMS0:KRMS0 + n], STAT[:, KRMS0:KRMS0 + n],
            1.0 / DH, RMS_EPS, op0=ALU.mult, op1=ALU.add), inc=(SV, 1))
        ms_tok = rec.now(SV)
        free[("KRMS",)] = ms_tok
        rec.wait("scalar", ms_tok)
        rec.op("scalar", lambda E, n=nheads: E.activation(
            STAT[:, RKC0:RKC0 + n], STAT[:, KRMS0:KRMS0 + n], ACT.Sqrt), inc=(SSC, 1))
        sq2_tok = rec.now(SSC)
        rec.wait("vector", sq2_tok)
        rec.op("vector", lambda E, n=nheads: E.reciprocal(
            STAT[:, RKC0:RKC0 + n], STAT[:, RKC0:RKC0 + n]), inc=(SV, 1))
        rk_tok = rec.now(SV)
        rec.wait("vector", rk_tok)
        for h4 in range(nheads):
            rec.op("vector", lambda E, h4=h4: E.tensor_scalar_mul(
                NT_SCR[:, h4 * 128:(h4 + 1) * 128], ps_src[:, h4 * 128:(h4 + 1) * 128],
                stat(RKC0 + h4)))
        nh = nheads
        nt3 = NT_SCR[:, 0:nh * 128].rearrange("p (h d) -> p h d", h=nh)
        x1 = nt3[:, :, 0:128:2]
        x2 = nt3[:, :, 1:128:2]
        cos3 = CSST[cs_slot][:, 0:nh * 64].rearrange("p (h d) -> p h d", h=nh)
        sin3 = CSST[cs_slot][:, 256:256 + nh * 64].rearrange("p (h d) -> p h d", h=nh)
        s1 = RO_S1[:, 0:nh * 64].rearrange("p (h d) -> p h d", h=nh)
        s2 = RO_S2[:, 0:nh * 64].rearrange("p (h d) -> p h d", h=nh)
        s3 = RO_S3[:, 0:nh * 64].rearrange("p (h d) -> p h d", h=nh)
        s4 = RO_S4[:, 0:nh * 64].rearrange("p (h d) -> p h d", h=nh)
        o3 = RP_SCR[:, 0:nh * 128].rearrange("p (h d) -> p h d", h=nh)
        rec.wait("vector", free.get(("RP",)))
        rec.op("vector", lambda E: E.tensor_mul(s1, x1, cos3))
        rec.op("vector", lambda E: E.tensor_mul(s2, x2, sin3))
        rec.op("vector", lambda E: E.tensor_mul(s3, x1, sin3))
        rec.op("vector", lambda E: E.tensor_mul(s4, x2, cos3))
        rec.op("vector", lambda E: E.tensor_sub(o3[:, :, 0:64], s1, s2))
        rec.op("vector", lambda E: E.tensor_add(o3[:, :, 64:128], s3, s4), inc=(SV, 1))
        return rec.now(SV)

    for tt in range(NT):
        if tt < HT:
            dest = (lambda dc, tt=tt: XN[:, dc, tt * 128:(tt + 1) * 128])
            guard_key = None
        else:
            dest = (lambda dc: XNT_TMP[:, dc, :])
            guard_key = ("XNT",)
        xnT_tok = emit_xn_tile(tt, dest, do_router=True, do_lo=True,
                               evac_guard_key=guard_key, dup_load=(tt == 0))

        cb = tt % 2
        rec.wait("sync", free.get(("CSST", cb)))
        rec.op("sync", lambda E, tt=tt, cb=cb: E.dma_start(
            out=CSST[cb], in_=cs_d[tt * 128:(tt + 1) * 128, :]), inc=(SCS, 16))
        if tt == 0:
            rec.op("sync", lambda E, tt=tt, cb=cb: E.dma_start(
                out=CSST[cb], in_=cs_d[tt * 128:(tt + 1) * 128, :]), inc=(SCS, 16))
        cs_tok = rec.now(SCS)

        rec.wait("tensor", wkv_tok)
        rec.wait("tensor", free.get(("PS", 0)))
        for dc in range(DC):
            rec.op("tensor", lambda E, d=dest(dc), dc=dc: E.matmul(
                PS[0][:], d, WKST[:, dc, :], start=(dc == 0), stop=(dc == DC - 1)),
                inc=(STn, 1) if dc == DC - 1 else None)
        k_tok = rec.now(STn)
        rec.wait("tensor", free.get(("PS", 1)))
        for dc in range(DC):
            rec.op("tensor", lambda E, d=dest(dc), dc=dc: E.matmul(
                PS[1][:], d, WVST[:, dc, :], start=(dc == 0), stop=(dc == DC - 1)),
                inc=(STn, 1) if dc == DC - 1 else None)
        v_tok = rec.now(STn)
        free[("XNT",)] = v_tok

        rec.wait("scalar", k_tok)
        rec.wait("vector", cs_tok)
        rr_tok = emit_rms_rope(PS[0], KV, cb)
        free[("CSST", cb)] = rr_tok
        free[("PS", 0)] = rr_tok
        rec.wait("tensor", rr_tok)
        for h4 in range(KV):
            j = h4 % 2
            rec.wait("tensor", free.get(("PT", j)))
            rec.op("tensor", lambda E, h4=h4, j=j: E.transpose(
                PT[j], RP_SCR[:, h4 * 128:(h4 + 1) * 128], ID16), inc=(STn, 1))
            tr_tok = rec.now(STn)
            rec.wait("vector", tr_tok)
            rec.op("vector", lambda E, h4=h4, j=j, tt=tt: E.tensor_copy(
                KT[:, h4, tt * 128:(tt + 1) * 128], PT[j]), inc=(SV, 1))
            free[("PT", j)] = rec.now(SV)
        free[("RP",)] = (STn, STn.n)

        rec.wait("vector", v_tok)
        rec.op("vector", lambda E, tt=tt: E.tensor_copy(
            VT[:, :, tt, 0:128],
            PS[1][:].rearrange("p (kv d) -> p kv d", kv=KV)), inc=(SV, 1))
        free[("PS", 1)] = rec.now(SV)

    def emit_q_half(half, xn_ready):
        for hq in range(4):
            wb = hq % 2
            rec.wait("gpsimd", free.get(("WST", wb)))
            rec.op("gpsimd", lambda E, hq=hq, wb=wb: E.dma_start(
                out=WST[wb],
                in_=wqT_d[:, hq * 512:(hq + 1) * 512].rearrange("(dc p) n -> p dc n", p=128)),
                inc=(SW, 16))
            wq_tok = rec.now(SW)
            for ti in range(HT):
                tt = half * HT + ti
                cb = ti % 2
                rec.wait("sync", free.get(("CSST", cb)))
                rec.op("sync", lambda E, tt=tt, cb=cb: E.dma_start(
                    out=CSST[cb], in_=cs_d[tt * 128:(tt + 1) * 128, :]), inc=(SCS, 16))
                cs_tok = rec.now(SCS)
                rec.wait("tensor", wq_tok)
                rec.wait("tensor", xn_ready)
                rec.wait("tensor", free.get(("PS", 0)))
                for dc in range(DC):
                    rec.op("tensor", lambda E, dc=dc, ti=ti, wb=wb: E.matmul(
                        PS[0][:], XN[:, dc, ti * 128:(ti + 1) * 128], WST[wb][:, dc, :],
                        start=(dc == 0), stop=(dc == DC - 1)),
                        inc=(STn, 1) if dc == DC - 1 else None)
                q_tok = rec.now(STn)
                rec.wait("scalar", q_tok)
                rec.wait("vector", cs_tok)
                rr_tok = emit_rms_rope(PS[0], 4, cb)
                free[("CSST", cb)] = rr_tok
                free[("PS", 0)] = rr_tok
                if ti == HT - 1:
                    free[("WST", wb)] = rr_tok
                rec.wait("tensor", rr_tok)
                for h4 in range(4):
                    j = h4 % 2
                    head = hq * 4 + h4
                    rec.wait("tensor", free.get(("PT", j)))
                    rec.op("tensor", lambda E, h4=h4, j=j: E.transpose(
                        PT[j], RP_SCR[:, h4 * 128:(h4 + 1) * 128], ID16), inc=(STn, 1))
                    tr_tok = rec.now(STn)
                    rec.wait("vector", tr_tok)
                    rec.op("vector", lambda E, head=head, j=j, ti=ti: E.tensor_copy(
                        QT[:, head, ti * 128:(ti + 1) * 128], PT[j]), inc=(SV, 1))
                    free[("PT", j)] = rec.now(SV)
                free[("RP",)] = (STn, STn.n)
        return rec.now(SV)

    def vgap():
        rec.op("vector", lambda E: E.tensor_copy(STAT[:, 31:32], STAT[:, 30:31]))

    def emit_router_weights():
        for tt in range(NT):
            lg = LGT[:, tt, :]
            rec.op("vector", lambda E, lg=lg: E.reduce_max(stat(M1C), lg, axis=AX.X),
                   inc=(SV, 1))
            rec.wait("vector", rec.now(SV))
            rec.op("vector", lambda E, lg=lg: E.tensor_scalar(
                ST8A[:], lg, stat(M1C), None, op0=ALU.is_equal))
            vgap()
            rec.op("vector", lambda E, lg=lg: E.scalar_tensor_tensor(
                ST8B[:], ST8A[:], -1e30, lg, op0=ALU.mult, op1=ALU.add))
            vgap()
            rec.op("vector", lambda E: E.reduce_max(stat(M2CC), ST8B[:], axis=AX.X),
                   inc=(SV, 1))
            rec.wait("vector", rec.now(SV))
            rec.op("vector", lambda E: E.tensor_sub(stat(DMC), stat(M2CC), stat(M1C)),
                   inc=(SV, 1))
            dm_tok = rec.now(SV)
            rec.wait("scalar", dm_tok)
            rec.op("scalar", lambda E: E.activation(stat(EWC), stat(DMC), ACT.Exp),
                   inc=(SSC, 1))
            ew_tok = rec.now(SSC)
            rec.wait("vector", ew_tok)
            rec.op("vector", lambda E: E.tensor_scalar(
                stat(EW2), stat(EWC), 1.0, None, op0=ALU.add))
            vgap()
            rec.op("vector", lambda E: E.reciprocal(stat(W1CC), stat(EW2)), inc=(SV, 1))
            rec.wait("vector", rec.now(SV))
            rec.op("vector", lambda E: E.tensor_scalar(
                stat(W2CC), stat(W1CC), -1.0, 1.0, op0=ALU.mult, op1=ALU.add), inc=(SV, 1))
            w2_tok = rec.now(SV)
            rec.op("vector", lambda E: E.tensor_scalar_mul(ST8B[:], ST8A[:], stat(W1CC)))
            rec.op("vector", lambda E, lg=lg: E.scalar_tensor_tensor(
                ST8C[:], ST8A[:], -1e30, lg, op0=ALU.mult, op1=ALU.add))
            vgap()
            rec.op("vector", lambda E: E.tensor_scalar(
                ST8C[:], ST8C[:], stat(M2CC), None, op0=ALU.is_equal))
            rec.wait("vector", w2_tok)
            vgap()
            rec.op("vector", lambda E, tt=tt: E.scalar_tensor_tensor(
                WRT[:, tt, :], ST8C[:], stat(W2CC), ST8B[:], op0=ALU.mult, op1=ALU.add),
                inc=(SV, 1))

    def emit_attn_half(half):
        rec.wait("tensor", rec.now(SV))
        rec.wait("vector", rec.now(STn))
        rec.wait("vector", consts_tok)
        for h in range(H):
            kv = h // 4
            for cr in range(2):
                cq = half * 2 + cr
                kclo = max(0, 4 * cq - 8)
                kchi = 4 * cq + 3
                kcs = list(range(kclo, kchi + 1))
                pv_first = [True] * 4

                def emit_S(i, kc):
                    sb_ = i % 2
                    rec.wait("tensor", free.get(("PS", 4 + sb_)))
                    rec.op("tensor", lambda E, kc=kc, sb_=sb_, kv=kv, h=h, cr=cr: E.matmul(
                        PS[4 + sb_][:], KT[:, kv, kc * 128:(kc + 1) * 128],
                        QT[:, h, cr * 512:(cr + 1) * 512], start=True, stop=True),
                        inc=(STn, 1))
                    s_tok = rec.now(STn)
                    rec.wait("scalar", s_tok)
                    rec.wait("scalar", free.get(("ATB", sb_)))
                    rec.op("scalar", lambda E, sb_=sb_: E.activation(
                        ATB[sb_][:], PS[4 + sb_][:], ACT.Exp, scale=SCALE), inc=(SSC, 1))
                    exp_tok = rec.now(SSC)
                    free[("PS", 4 + sb_)] = exp_tok
                    a_ready = exp_tok
                    causal_qr = kc - 4 * cq if 4 * cq <= kc <= 4 * cq + 3 else None
                    drop_row0 = (kc == 4 * cq - 8)
                    if causal_qr is not None or drop_row0:
                        rec.wait("vector", exp_tok)
                        if causal_qr is not None:
                            rec.op("vector", lambda E, sb_=sb_, qr=causal_qr: E.tensor_mul(
                                ATB[sb_][:, qr * 128:(qr + 1) * 128],
                                ATB[sb_][:, qr * 128:(qr + 1) * 128], MASK_C),
                                inc=None if drop_row0 else (SV, 1))
                        if drop_row0:
                            rec.op("vector", lambda E, sb_=sb_: E.memset(
                                ATB[sb_][0:1, :], 0.0), inc=(SV, 1))
                        a_ready = rec.now(SV)
                    return a_ready

                def emit_PV(i, kc, a_ready):
                    sb_ = i % 2
                    rec.wait("tensor", a_ready)
                    qs_lo = max(0, kc - 4 * cq)
                    valid_qrs = list(range(qs_lo, 4))
                    for vi, qr in enumerate(valid_qrs):
                        qs = 4 * cq + qr
                        is_stop = (kc == qs)
                        is_last = (vi == len(valid_qrs) - 1)
                        if pv_first[qr]:
                            rec.wait("tensor", free.get(("PS", qr)))
                        rec.op("tensor", lambda E, sb_=sb_, qr=qr, kv=kv, kc=kc,
                               f=pv_first[qr], st=is_stop: E.matmul(
                            PS[qr][:, 0:129], ATB[sb_][:, qr * 128:(qr + 1) * 128],
                            VT[:, kv, kc, :].squeeze(), start=f, stop=st),
                            inc=(STn, 1) if (is_stop or is_last) else None)
                        pv_first[qr] = False
                        if is_stop:
                            pv_tok = rec.now(STn)
                            ti = qs - half * HT
                            rec.wait("vector", pv_tok)
                            rec.op("vector", lambda E, qr=qr: E.reciprocal(
                                stat(RCOL), PS[qr][:, 128:129]), inc=(SV, 1))
                            rc_tok = rec.now(SV)
                            rec.wait("vector", rc_tok)
                            rec.op("vector", lambda E, qr=qr, ti=ti, h=h: E.tensor_scalar_mul(
                                AO[:, ti, h * 128:(h + 1) * 128], PS[qr][:, 0:128], stat(RCOL)),
                                inc=(SV, 1))
                            free[("PS", qr)] = rec.now(SV)
                    free[("ATB", sb_)] = (STn, STn.n)

                pend = None
                for i, kc in enumerate(kcs):
                    a_ready = emit_S(i, kc)
                    if pend is not None:
                        emit_PV(pend[0], pend[1], pend[2])
                    pend = (i, kc, a_ready)
                emit_PV(*pend)

    def emit_atT_half():
        rec.wait("tensor", rec.now(SV))
        pt4 = [PT[0][:], PT[1][:],
               PS[4].bitcast(BF16)[:, 0:128], PS[5].bitcast(BF16)[:, 0:128]]
        for ti in range(HT):
            for hc in range(H):
                j = (ti * H + hc) % 4
                rec.wait("tensor", free.get(("PT4", j)))
                rec.op("tensor", lambda E, ti=ti, hc=hc, j=j: E.transpose(
                    pt4[j], AO[:, ti, hc * 128:(hc + 1) * 128], ID16), inc=(STn, 1))
                tr_tok = rec.now(STn)
                rec.wait("vector", tr_tok)
                rec.op("vector", lambda E, ti=ti, hc=hc, j=j: E.tensor_copy(
                    QT[:, hc, ti * 128:(ti + 1) * 128], pt4[j]), inc=(SV, 1))
                free[("PT4", j)] = rec.now(SV)
        free[("PS", 4)] = free[("PT4", 2)]
        free[("PS", 5)] = free[("PT4", 3)]
        free[("PT", 0)] = free[("PT4", 0)]
        free[("PT", 1)] = free[("PT4", 1)]

    def emit_wo_half():
        rec.wait("tensor", rec.now(SV))
        for dq in range(4):
            wb = dq % 2
            rec.wait("gpsimd", free.get(("WST", wb)))
            rec.op("gpsimd", lambda E, dq=dq, wb=wb: E.dma_start(
                out=WST[wb],
                in_=woT_d[:, dq * 512:(dq + 1) * 512].rearrange("(hc p) n -> p hc n", p=128)),
                inc=(SW, 16))
            wo_tok = rec.now(SW)
            rec.wait("tensor", wo_tok)
            for ti in range(HT):
                pb = ti % 2
                rec.wait("tensor", free.get(("PS", pb)))
                for hc in range(H):
                    rec.op("tensor", lambda E, hc=hc, ti=ti, wb=wb, pb=pb: E.matmul(
                        PS[pb][:], QT[:, hc, ti * 128:(ti + 1) * 128], WST[wb][:, hc, :],
                        start=(hc == 0), stop=(hc == H - 1)),
                        inc=(STn, 1) if hc == H - 1 else None)
                y_tok = rec.now(STn)
                rec.wait("vector", y_tok)
                rec.op("vector", lambda E, ti=ti, dq=dq, pb=pb: E.tensor_copy(
                    AO[:, ti, dq * 512:(dq + 1) * 512], PS[pb][:]), inc=(SV, 1))
                free[("PS", pb)] = rec.now(SV)
                if ti == HT - 1:
                    free[("WST", wb)] = rec.now(SV)

    def emit_ffn_half(half):
        t_now = rec.now(STn)
        rec.wait("vector", t_now)
        rec.wait("sync", t_now)
        rec.wait("gpsimd", t_now)
        fold_done = {}
        for idx, ti in enumerate((0, 1, 4, 5, 2, 3, 6, 7)):
            tt = half * HT + ti
            b = idx % 2
            rec.wait("sync", free.get(("FXSTG", b)))
            rec.op("sync", lambda E, tt=tt, b=b: E.dma_start(
                out=FXSTG[b], in_=x_d[tt * 128:(tt + 1) * 128, :]), inc=(SX, 16))
            if idx == 0:
                rec.op("sync", lambda E, tt=tt, b=b: E.dma_start(
                    out=FXSTG[b], in_=x_d[tt * 128:(tt + 1) * 128, :]), inc=(SX, 16))
            x_tok = rec.now(SX)
            rec.wait("vector", x_tok)
            rec.op("vector", lambda E, ti=ti, b=b: E.tensor_tensor(
                ACC[:, ti, :], FXSTG[b], AO[:, ti, :], ALU.add), inc=(SV, 1))
            free[("FXSTG", b)] = rec.now(SV)
            fold_done[ti] = rec.now(SV)
        fold_tok = rec.now(SV)
        free[("W1ST", 0)] = fold_done[1]
        free[("W1ST", 1)] = fold_done[3]
        free[("W2ST", 0)] = fold_done[5]
        free[("W2ST", 1)] = fold_done[7]

        for e in range(NE):
            for fq in range(4):
                wb = fq % 2
                rec.wait("gpsimd", free.get(("W1ST", wb)))
                rec.op("gpsimd", lambda E, e=e, fq=fq, wb=wb: E.dma_start(
                    out=W1ST[wb],
                    in_=w1_d[e, :, fq * 256:(fq + 1) * 256].rearrange("(dc p) f -> p dc f", p=128)),
                    inc=(SW, 16))
                rec.wait("gpsimd", free.get(("W2ST", wb)))
                rec.op("gpsimd", lambda E, e=e, fq=fq, wb=wb: E.dma_start(
                    out=W2ST[wb],
                    in_=w2_d[e, :, fq * 256:(fq + 1) * 256].rearrange("(dc p) f -> p dc f", p=128)),
                    inc=(SW, 16))
                w12_tok = rec.now(SW)
                rec.wait("tensor", w12_tok)
                for fcl in range(2):
                    fc = fq * 2 + fcl
                    for tq in range(2):
                        rec.wait("tensor", free.get(("PS", 2)))
                        for dc in range(DC):
                            rec.op("tensor", lambda E, dc=dc, fcl=fcl, tq=tq, wb=wb: E.matmul(
                                PS[2][:], W1ST[wb][:, dc, fcl * 128:(fcl + 1) * 128],
                                XN[:, dc, tq * 512:(tq + 1) * 512],
                                start=(dc == 0), stop=(dc == DC - 1)),
                                inc=(STn, 1) if dc == DC - 1 else None)
                        g_tok = rec.now(STn)
                        rec.wait("tensor", free.get(("PS", 3)))
                        for dc in range(DC):
                            rec.op("tensor", lambda E, dc=dc, fcl=fcl, tq=tq, wb=wb: E.matmul(
                                PS[3][:], W2ST[wb][:, dc, fcl * 128:(fcl + 1) * 128],
                                XN[:, dc, tq * 512:(tq + 1) * 512],
                                start=(dc == 0), stop=(dc == DC - 1)),
                                inc=(STn, 1) if dc == DC - 1 else None)
                        vv_tok = rec.now(STn)
                        rec.wait("scalar", g_tok)
                        rec.wait("scalar", free.get(("RP",)))
                        rec.op("scalar", lambda E: E.activation(
                            RP_SCR[:], PS[2][:], ACT.Silu), inc=(SSC, 1))
                        si_tok = rec.now(SSC)
                        free[("PS", 2)] = si_tok
                        rec.wait("vector", si_tok)
                        rec.wait("vector", vv_tok)
                        rec.wait("vector", free.get(("HBUF",)))
                        rec.op("vector", lambda E, fc=fc, tq=tq: E.tensor_tensor(
                            HBUF[:, fc, tq * 512:(tq + 1) * 512], RP_SCR[:], PS[3][:],
                            ALU.mult), inc=(SV, 1))
                        free[("PS", 3)] = rec.now(SV)
                        free[("RP",)] = rec.now(SV)
                        if fcl == 1 and tq == 1:
                            free[("W1ST", wb)] = rec.now(SV)
                            free[("W2ST", wb)] = rec.now(SV)
            h_tok = rec.now(SV)
            for dq in range(4):
                wb = dq % 2
                rec.wait("gpsimd", free.get(("W3ST", wb)))
                rec.op("gpsimd", lambda E, e=e, dq=dq, wb=wb: E.dma_start(
                    out=W3ST[wb],
                    in_=w3_d[e, :, dq * 512:(dq + 1) * 512].rearrange("(fc p) d -> p fc d", p=128)),
                    inc=(SW, 16))
                w3_tok = rec.now(SW)
                rec.wait("tensor", w3_tok)
                rec.wait("tensor", h_tok)
                for ti in range(HT):
                    tt = half * HT + ti
                    pb = ti % 2
                    rec.wait("tensor", free.get(("PS", pb)))
                    for fc in range(8):
                        rec.op("tensor", lambda E, fc=fc, ti=ti, wb=wb, pb=pb: E.matmul(
                            PS[pb][:], HBUF[:, fc, ti * 128:(ti + 1) * 128], W3ST[wb][:, fc, :],
                            start=(fc == 0), stop=(fc == 7)),
                            inc=(STn, 1) if fc == 7 else None)
                    o_tok = rec.now(STn)
                    if dq == 3 and ti == HT - 1:
                        free[("HBUF",)] = (STn, STn.n)
                    rec.wait("vector", o_tok)
                    rec.op("vector", lambda E, ti=ti, dq=dq, pb=pb, tt=tt, e=e:
                           E.scalar_tensor_tensor(
                               ACC[:, ti, dq * 512:(dq + 1) * 512], PS[pb][:],
                               WRT[:, tt, e:e + 1], ACC[:, ti, dq * 512:(dq + 1) * 512],
                               op0=ALU.mult, op1=ALU.add), inc=(SV, 1))
                    free[("PS", pb)] = rec.now(SV)
                    if ti == HT - 1:
                        free[("W3ST", wb)] = rec.now(SV)
        acc_tok = rec.now(SV)
        rec.wait("sync", acc_tok)
        for ti in range(HT):
            tt = half * HT + ti
            rec.op("sync", lambda E, tt=tt, ti=ti: E.dma_start(
                out=out_d[tt * 128:(tt + 1) * 128, :], in_=ACC[:, ti, :]), inc=(SST, 16))
        st_tok = rec.now(SST)
        for k in (("WST", 0), ("WST", 1), ("XSTG", 0), ("XSTG", 1),
                  ("CSST", 0), ("CSST", 1)):
            free[k] = st_tok
        free[("XNTOK",)] = st_tok

    def dbg_dump(key, src):
        if not debug:
            return
        tok = rec.now(SV)
        rec.wait("sync", tok)
        rec.op("sync", lambda E, key=key, src=src: E.dma_start(out=dbg[key], in_=src),
               inc=(SST, 16))

    pa_tok = rec.now(SV)
    emit_q_half(0, pa_tok)
    emit_router_weights()
    emit_attn_half(0)
    emit_atT_half()
    emit_wo_half()
    if debug:
        mid_tok = rec.now(SV)
        rec.wait("sync", mid_tok)
        rec.op("sync", lambda E: E.dma_start(out=dbg["xnT"], in_=XN[:]), inc=(SST, 16))
        rec.op("sync", lambda E: E.dma_start(out=dbg["y"], in_=AO[:]), inc=(SST, 16))
        rec.op("sync", lambda E: E.dma_start(out=dbg["qT"], in_=QT[:]), inc=(SST, 16))
        dump_done = rec.now(SST)
        rec.wait("vector", dump_done)
        rec.wait("tensor", dump_done)
        rec.wait("scalar", dump_done)
        rec.wait("gpsimd", dump_done)
    emit_ffn_half(0)

    rec.wait("vector", rec.now(STn))
    rec.wait("scalar", rec.now(STn))
    for ti in range(HT):
        tt = HT + ti
        dest = (lambda dc, ti=ti: XN[:, dc, ti * 128:(ti + 1) * 128])
        emit_xn_tile(tt, dest, do_router=False, do_lo=False, dup_load=(ti == 0))
    xn2_tok = rec.now(SV)
    emit_q_half(1, xn2_tok)
    emit_attn_half(1)
    emit_atT_half()
    emit_wo_half()
    emit_ffn_half(1)

    if debug:
        end_tok = rec.now(SV)
        rec.wait("sync", end_tok)
        rec.wait("sync", rec.now(SST))
        for key, src in (("kT", KT[:]), ("vT", VT[:]), ("W", WRT[:]),
                         ("lg", LGT[:]), ("ao", AO[:])):
            rec.op("sync", lambda E, key=key, src=src: E.dma_start(out=dbg[key], in_=src),
                   inc=(SST, 16))

    for s_ in (SST, SX, SCS, SW):
        rec.wait("gpsimd", rec.now(s_))

    with nc.Block() as block:
        def mk(eng_name):
            def body(E):
                for f in rec.ops[eng_name]:
                    f(E)
            return body
        block.sync(mk("sync"))
        block.tensor(mk("tensor"))
        block.vector(mk("vector"))
        block.scalar(mk("scalar"))
        block.gpsimd(mk("gpsimd"))

    counts = {k: len(v) for k, v in rec.ops.items()}
    print("bass program op counts:", counts, "total:", sum(counts.values()))
    return nc


def _host_prep(inputs):
    x = np.ascontiguousarray(np.asarray(inputs["L0_attn_x"], dtype=np.float32))
    wqT = np.ascontiguousarray(np.asarray(inputs["L0_attn_wq"], dtype=np.float32).T)
    wkT = np.ascontiguousarray(np.asarray(inputs["L0_attn_wk"], dtype=np.float32).T)
    wvT = np.ascontiguousarray(np.asarray(inputs["L0_attn_wv"], dtype=np.float32).T)
    woT = np.ascontiguousarray(np.asarray(inputs["L0_attn_wo"], dtype=np.float32).T)
    rwT = np.ascontiguousarray(np.asarray(inputs["L0_ffn_router_w"], dtype=np.float32).T)
    w1 = np.ascontiguousarray(np.asarray(inputs["L0_ffn_w1"], dtype=np.float32))
    w2 = np.ascontiguousarray(np.asarray(inputs["L0_ffn_w2"], dtype=np.float32))
    w3 = np.ascontiguousarray(np.asarray(inputs["L0_ffn_w3"], dtype=np.float32))

    inv_freq = 10000.0 ** (-(np.arange(0, DH, 2, dtype=np.float64) / DH))
    ang = np.outer(np.arange(T, dtype=np.float64), inv_freq)
    cos4 = np.tile(np.cos(ang).astype(np.float32), (1, 4))
    sin4 = np.tile(np.sin(ang).astype(np.float32), (1, 4))
    cs_tab = np.ascontiguousarray(np.concatenate([cos4, sin4], axis=1))

    ident = np.eye(128, dtype=ml_dtypes.bfloat16)
    kk = np.arange(128)[:, None]
    qq = np.arange(128)[None, :]
    mask_c = (kk <= qq).astype(ml_dtypes.bfloat16)
    mask_w = (kk > qq).astype(ml_dtypes.bfloat16)
    consts = np.ascontiguousarray(np.concatenate([ident, mask_c, mask_w], axis=1))

    shared = dict(wqT=wqT, wkT=wkT, wvT=wvT, woT=woT, rwT=rwT,
                  w1=w1, w2=w2, w3=w3, cs_tab=cs_tab, consts=consts)
    in_maps = []
    for c in range(8):
        m = dict(shared)
        m["x"] = np.ascontiguousarray(x[c])
        in_maps.append(m)
    return in_maps


def kernel(**inputs):
    in_maps = _host_prep(inputs)
    nc = build_nc(debug=DEBUG)
    res = run_bass_kernel_spmd(nc, in_maps, core_ids=list(range(8)))
    out = np.stack([np.asarray(r["out"]) for r in res.results], axis=0)
    if DEBUG:
        kernel.last_results = res.results
    return out


# version 05
# speedup vs baseline: 1.3479x; 1.0418x over previous
import math
import numpy as np
import ml_dtypes
import concourse.bass as bass
import concourse.mybir as mybir
from concourse.bass_utils import run_bass_kernel_spmd

F32 = mybir.dt.float32
BF16 = mybir.dt.bfloat16
AX = mybir.AxisListType
ALU = mybir.AluOpType
ACT = mybir.ActivationFunctionType

T = 2048
D = 2048
H = 16
KV = 4
DH = 128
FF = 1024
NE = 8
NT = 16
DC = 16
SCALE = 1.0 / math.sqrt(DH)
LN_EPS = 1e-5
RMS_EPS = 1e-6
HALF_T = 1024
HT = 8

DEBUG = False

ENGS = ("sync", "tensor", "vector", "scalar", "gpsimd")


class Sem:
    def __init__(self, handle, name):
        self.h = handle
        self.name = name
        self.n = 0


class Rec:
    def __init__(self):
        self.ops = {k: [] for k in ENGS}
        self._last_wait = {}

    def op(self, eng, fn, inc=None):
        if inc is not None:
            sem, k = inc
            sem.n += k

            def run(E, fn=fn, h=sem.h, k=k):
                fn(E).then_inc(h, k)
            self.ops[eng].append(run)
            return (sem, sem.n)
        else:
            self.ops[eng].append(lambda E, fn=fn: fn(E))
            return None

    def wait(self, eng, token):
        if token is None:
            return
        sem, v = token
        if v <= 0:
            return
        key = (eng, sem.name)
        if self._last_wait.get(key, -1) >= v:
            return
        self._last_wait[key] = v
        self.ops[eng].append(lambda E, h=sem.h, v=v: E.wait_ge(h, v))

    def now(self, sem):
        return (sem, sem.n)


def build_nc(debug=False):
    nc = bass.Bass("TRN2", target_bir_lowering=False)

    x_d = nc.dram_tensor("x", [T, D], F32, kind="ExternalInput").ap()
    wqT_d = nc.dram_tensor("wqT", [D, H * DH], F32, kind="ExternalInput").ap()
    wkT_d = nc.dram_tensor("wkT", [D, KV * DH], F32, kind="ExternalInput").ap()
    wvT_d = nc.dram_tensor("wvT", [D, KV * DH], F32, kind="ExternalInput").ap()
    woT_d = nc.dram_tensor("woT", [H * DH, D], F32, kind="ExternalInput").ap()
    rwT_d = nc.dram_tensor("rwT", [D, NE], F32, kind="ExternalInput").ap()
    w1_d = nc.dram_tensor("w1", [NE, D, FF], F32, kind="ExternalInput").ap()
    w2_d = nc.dram_tensor("w2", [NE, D, FF], F32, kind="ExternalInput").ap()
    w3_d = nc.dram_tensor("w3", [NE, FF, D], F32, kind="ExternalInput").ap()
    cs_d = nc.dram_tensor("cs_tab", [T, 512], F32, kind="ExternalInput").ap()
    const_d = nc.dram_tensor("consts", [128, 3 * 128], BF16, kind="ExternalInput").ap()

    out_d = nc.dram_tensor("out", [T, D], F32, kind="ExternalOutput").ap()

    dbg = {}
    if debug:
        dbg["xnT"] = nc.dram_tensor("dbg_xnT", [128, DC, HALF_T], BF16, kind="ExternalOutput").ap()
        dbg["kT"] = nc.dram_tensor("dbg_kT", [128, KV, T], BF16, kind="ExternalOutput").ap()
        dbg["vT"] = nc.dram_tensor("dbg_vT", [128, KV, NT, 129], BF16, kind="ExternalOutput").ap()
        dbg["lg"] = nc.dram_tensor("dbg_lg", [128, NT, NE], F32, kind="ExternalOutput").ap()
        dbg["W"] = nc.dram_tensor("dbg_W", [128, NT, NE], F32, kind="ExternalOutput").ap()
        dbg["qT"] = nc.dram_tensor("dbg_qT", [128, H, HALF_T], BF16, kind="ExternalOutput").ap()
        dbg["ao"] = nc.dram_tensor("dbg_ao", [128, HT, D], BF16, kind="ExternalOutput").ap()
        dbg["y"] = nc.dram_tensor("dbg_y", [128, HT, D], BF16, kind="ExternalOutput").ap()

    ARENA_SIZE = 209728
    arena = nc.alloc_sbuf_tensor("ARENA", [128, ARENA_SIZE], mybir.dt.uint8)
    ARENA_BASE = nc.lookup_mloc(arena).addr

    def sb(name, shape, dtype, off):
        return nc.alloc_sbuf_tensor_at(name, shape, dtype, offset=ARENA_BASE + off).ap()

    XN = sb("XN", [128, DC, HALF_T], BF16, 0)
    QT = sb("QT", [128, H, HALF_T], BF16, 32768)
    SQSCR = sb("SQSCR", [128, D], F32, 32768)
    XLO_TMP = sb("XLO_TMP", [128, DC, 128], BF16, 40960)
    SQSCR_B = sb("SQSCR_B", [128, D], F32, 49152)
    XNTOK_B = sb("XNTOK_B", [128, D], BF16, 57344)
    XNLO_B = sb("XNLO_B", [128, D], BF16, 61440)
    HBUF = sb("HBUF", [128, 8, HALF_T], BF16, 32768)
    FXSTG = [sb("FXSTG0", [128, D], F32, 32768),
             sb("FXSTG1", [128, D], F32, 40960)]
    W3ST = [sb("W3ST0", [128, 8, 512], BF16, 49152),
            sb("W3ST1", [128, 8, 512], BF16, 57344)]

    AO = sb("AO", [128, HT, D], BF16, 65536)
    WKST = sb("WKST", [128, DC, 512], BF16, 65536)
    WVST = sb("WVST", [128, DC, 512], BF16, 81920)
    W1ST = [sb("W1ST0", [128, DC, 256], BF16, 65536),
            sb("W1ST1", [128, DC, 256], BF16, 73728)]
    W2ST = [sb("W2ST0", [128, DC, 256], BF16, 81920),
            sb("W2ST1", [128, DC, 256], BF16, 90112)]

    ACC = sb("ACC", [128, HT, D], F32, 98304)
    WST = [sb("WST0", [128, DC, 512], BF16, 98304),
           sb("WST1", [128, DC, 512], BF16, 114688)]
    XSTG = [sb("XSTG0", [128, D], F32, 131072),
            sb("XSTG1", [128, D], F32, 139264)]
    CSST = [sb("CSST0", [128, 512], F32, 147456),
            sb("CSST1", [128, 512], F32, 149504)]
    XNTOK = sb("XNTOK", [128, D], BF16, 151552)
    XNLO = sb("XNLO", [128, D], BF16, 155648)
    XNT_TMP = sb("XNT_TMP", [128, DC, 128], BF16, 159744)

    KT = sb("KT", [128, KV, T], BF16, 163840)
    VT = sb("VT", [128, KV, NT, 129], BF16, 180224)
    ID16 = sb("ID16", [128, 128], BF16, 196736)
    MASK_C = sb("MASK_C", [128, 128], BF16, 196992)
    MASK_W = sb("MASK_W", [128, 128], BF16, 197248)
    RWTF = sb("RWTF", [128, DC, NE], F32, 197504)
    RWT_HI = sb("RWT_HI", [128, DC, NE], BF16, 198016)
    RWT_LO = sb("RWT_LO", [128, DC, NE], BF16, 198272)
    LGT = sb("LGT", [128, NT, NE], F32, 198528)
    WRT = sb("WRT", [128, NT, NE], F32, 199040)
    STAT = sb("STAT", [128, 32], F32, 199552)
    ST8A = sb("ST8A", [128, NE], F32, 199680)
    ST8B = sb("ST8B", [128, NE], F32, 199712)
    ATB = [sb("ATB0", [128, 512], BF16, 199744),
           sb("ATB1", [128, 512], BF16, 200768)]
    NT_SCR = sb("NT_SCR", [128, 512], F32, 201792)
    RO_S1 = sb("RO_S1", [128, 256], F32, 203840)
    RO_S2 = sb("RO_S2", [128, 256], F32, 204864)
    RP_SCR = sb("RP_SCR", [128, 512], BF16, 205888)
    RMS_SQ = sb("RMS_SQ", [128, 128], F32, 206912)
    RO_S3 = sb("RO_S3", [128, 256], F32, 207424)
    RO_S4 = sb("RO_S4", [128, 256], F32, 208448)
    ST8C = sb("ST8C", [128, NE], F32, 209472)

    PS = [nc.alloc_psum_tensor(f"PS{i}", [128, 512], F32).ap() for i in range(6)]
    PT = [nc.alloc_psum_tensor(f"PT{i}", [128, 128], BF16).ap() for i in range(2)]

    rec = Rec()
    sems = {}
    for name in ("X", "CS", "W", "ST", "T", "V", "SC"):
        sems[name] = Sem(nc.alloc_semaphore(name=f"S_{name}"), name)
    SX, SCS, SW, SST, STn, SV, SSC = (sems[k] for k in ("X", "CS", "W", "ST", "T", "V", "SC"))

    nc.all_engine_barrier()
    sem_nums = [s.h.num for s in sems.values()]
    nc.gpsimd.dma_reset(range(min(sem_nums), max(sem_nums) + 1))
    for s in sems.values():
        nc.gpsimd.sem_clear(s.h)
    nc.all_engine_barrier()
    warm = nc.alloc_semaphore(name="S_WARM")
    nc.sync.dma_start(out=XSTG[0], in_=x_d[0:128, :]).then_inc(warm, 16)
    nc.gpsimd.dma_start(out=XSTG[1], in_=x_d[128:256, :]).then_inc(warm, 16)
    nc.gpsimd.wait_ge(warm, 32)
    for s in sems.values():
        nc.gpsimd.sem_clear(s.h)
    nc.all_engine_barrier()

    def stat(i):
        return STAT[:, i:i + 1]

    SUMS, MVc, EX2, M2C = 0, 1, 2, 3
    VARV, RSTD, NEGM, RCOL = 4, 5, 6, 7
    KRMS0 = 8
    M1C, M2CC, DMC, W1CC = 12, 13, 14, 15
    RKC0 = 16
    W2CC, EW2 = 20, 21
    SQc, STDV, EWC = 24, 25, 26

    free = {}

    for _rep in range(2):
        rec.op("sync", lambda E: E.dma_start(out=ID16, in_=const_d[:, 0:128]), inc=(SCS, 16))
        rec.op("sync", lambda E: E.dma_start(out=MASK_C, in_=const_d[:, 128:256]), inc=(SCS, 16))
        rec.op("sync", lambda E: E.dma_start(out=MASK_W, in_=const_d[:, 256:384]), inc=(SCS, 16))
        rec.op("sync", lambda E: E.dma_start(
            out=RWTF, in_=rwT_d.rearrange("(dc p) e -> p dc e", p=128)), inc=(SCS, 16))
    consts_tok = rec.now(SCS)
    rec.wait("vector", consts_tok)
    rec.op("vector", lambda E: E.tensor_copy(RWT_HI, RWTF), inc=(SV, 1))
    rec.wait("vector", rec.now(SV))
    rec.op("vector", lambda E: E.tensor_tensor(RWT_LO, RWTF, RWT_HI, ALU.subtract), inc=(SV, 1))
    rwhl_tok = rec.now(SV)
    rec.op("vector", lambda E: E.memset(
        VT[:, :, :, 128:129].rearrange("p a b c -> p (a b c)"), 1.0), inc=(SV, 1))

    for _rep in range(2):
        rec.op("gpsimd", lambda E: E.dma_start(
            out=WKST, in_=wkT_d.rearrange("(dc p) n -> p dc n", p=128)), inc=(SW, 16))
        rec.op("gpsimd", lambda E: E.dma_start(
            out=WVST, in_=wvT_d.rearrange("(dc p) n -> p dc n", p=128)), inc=(SW, 16))
    wkv_tok = rec.now(SW)

    XNTOK2 = [XNTOK, XNTOK_B]
    XNLO2 = [XNLO, XNLO_B]
    SQSCR2 = [SQSCR, SQSCR_B]
    SQc2 = [SQc, SQc + 3]

    def emit_ln_stage(tt, do_lo, dup_load=False, load_cs=True):
        b = tt % 2
        rec.wait("sync", free.get(("XSTG", b)))
        rec.op("sync", lambda E, tt=tt, b=b: E.dma_start(
            out=XSTG[b], in_=x_d[tt * 128:(tt + 1) * 128, :]), inc=(SX, 16))
        if dup_load:
            rec.op("sync", lambda E, tt=tt, b=b: E.dma_start(
                out=XSTG[b], in_=x_d[tt * 128:(tt + 1) * 128, :]), inc=(SX, 16))
        x_tok = rec.now(SX)
        cs_tok = None
        if load_cs:
            rec.wait("sync", free.get(("CSST", b)))
            rec.op("sync", lambda E, tt=tt, b=b: E.dma_start(
                out=CSST[b], in_=cs_d[tt * 128:(tt + 1) * 128, :]), inc=(SCS, 16))
            if dup_load:
                rec.op("sync", lambda E, tt=tt, b=b: E.dma_start(
                    out=CSST[b], in_=cs_d[tt * 128:(tt + 1) * 128, :]), inc=(SCS, 16))
            cs_tok = rec.now(SCS)

        rec.wait("vector", x_tok)
        rec.op("vector", lambda E, b=b: E.reduce_sum(stat(SUMS), XSTG[b], axis=AX.X))
        rec.wait("scalar", x_tok)
        rec.wait("scalar", free.get(("SQc", b)))
        rec.op("scalar", lambda E, b=b: E.activation(
            SQSCR2[b], XSTG[b], ACT.Square, accum_out=stat(SQc2[b])), inc=(SSC, 1))
        sq_tok = rec.now(SSC)
        rec.wait("vector", sq_tok)
        rec.op("vector", lambda E: E.tensor_copy(STAT[:, 31:32], STAT[:, 30:31]))
        rec.op("vector", lambda E: E.tensor_scalar(stat(MVc), stat(SUMS), 1.0 / D, None, op0=ALU.mult))
        rec.op("vector", lambda E, b=b: E.tensor_scalar(
            stat(EX2), stat(SQc2[b]), 1.0 / D, LN_EPS, op0=ALU.mult, op1=ALU.add))
        rec.op("vector", lambda E: E.tensor_mul(stat(M2C), stat(MVc), stat(MVc)))
        rec.op("vector", lambda E: E.tensor_scalar(stat(NEGM), stat(MVc), -1.0, None, op0=ALU.mult))
        rec.op("vector", lambda E: E.tensor_sub(stat(VARV), stat(EX2), stat(M2C)),
               inc=(SV, 1))
        var_tok = rec.now(SV)
        rec.wait("scalar", var_tok)
        rec.op("scalar", lambda E: E.activation(stat(STDV), stat(VARV), ACT.Sqrt), inc=(SSC, 1))
        std_tok = rec.now(SSC)
        rec.wait("vector", std_tok)
        rec.op("vector", lambda E: E.reciprocal(stat(RSTD), stat(STDV)), inc=(SV, 1))
        rstd_tok = rec.now(SV)
        rec.wait("vector", rstd_tok)
        rec.wait("vector", free.get(("XNTOK", b)))
        if do_lo:
            rec.op("vector", lambda E, b=b: E.tensor_scalar(
                SQSCR2[b], XSTG[b], stat(NEGM), stat(RSTD), op0=ALU.add, op1=ALU.mult))
            rec.op("vector", lambda E, b=b: E.tensor_copy(XNTOK2[b], SQSCR2[b]))
            rec.op("vector", lambda E: E.tensor_copy(STAT[:, 31:32], STAT[:, 30:31]))
            rec.op("vector", lambda E, b=b: E.tensor_tensor(
                XNLO2[b], SQSCR2[b], XNTOK2[b], ALU.subtract), inc=(SV, 1))
        else:
            rec.op("vector", lambda E, b=b: E.tensor_scalar(
                XNTOK2[b], XSTG[b], stat(NEGM), stat(RSTD), op0=ALU.add, op1=ALU.mult),
                inc=(SV, 1))
        xn_tok = rec.now(SV)
        free[("XSTG", b)] = xn_tok
        free[("SQc", b)] = xn_tok
        return xn_tok, cs_tok

    def emit_xt_stage(tt, dest, do_router, do_lo, xn_tok, evac_guard_key=None):
        b = tt % 2
        rec.wait("tensor", xn_tok)
        rec.wait("tensor", consts_tok)
        srcs = [(XNTOK2[b], dest)]
        if do_lo:
            srcs.append((XNLO2[b], lambda dc: XLO_TMP[:, dc, :]))
        for src, dst in srcs:
            for dc in range(DC):
                j = dc % 2
                rec.wait("tensor", free.get(("PT", j)))
                rec.op("tensor", lambda E, dc=dc, j=j, src=src: E.transpose(
                    PT[j], src[:, dc * 128:(dc + 1) * 128], ID16), inc=(STn, 1))
                tr_tok = rec.now(STn)
                rec.wait("vector", tr_tok)
                if evac_guard_key is not None:
                    rec.wait("vector", free.get(evac_guard_key))
                rec.op("vector", lambda E, j=j, d=dst(dc): E.tensor_copy(d, PT[j]), inc=(SV, 1))
                free[("PT", j)] = rec.now(SV)
        free[("XNTOK", b)] = (STn, STn.n)
        xnT_tok = rec.now(SV)

        if do_router:
            rec.wait("tensor", xnT_tok)
            rec.wait("tensor", rwhl_tok)
            rec.wait("tensor", free.get(("PS", 5)))
            combos = [(dest, RWT_HI), (dest, RWT_LO),
                      ((lambda dc: XLO_TMP[:, dc, :]), RWT_HI)]
            nmm = 0
            for lt, rw in combos:
                for dc in range(DC):
                    first = nmm == 0
                    last = nmm == 3 * DC - 1
                    rec.op("tensor", lambda E, l0=lt(dc), r0=rw[:, dc, :], f=first, l=last:
                           E.matmul(PS[5][:, 0:NE], l0, r0, start=f, stop=l),
                           inc=(STn, 1) if last else None)
                    nmm += 1
            lg_tok = rec.now(STn)
            rec.wait("vector", lg_tok)
            rec.op("vector", lambda E, tt=tt: E.tensor_copy(LGT[:, tt, :], PS[5][:, 0:NE]),
                   inc=(SV, 1))
            free[("PS", 5)] = rec.now(SV)
        return xnT_tok

    def emit_rms_rope(ps_src, nheads, cs_slot):
        rec.wait("scalar", free.get(("KRMS",)))
        for h4 in range(nheads):
            rec.op("scalar", lambda E, h4=h4: E.activation(
                RMS_SQ, ps_src[:, h4 * 128:(h4 + 1) * 128], ACT.Square,
                accum_out=stat(KRMS0 + h4)),
                inc=(SSC, 1) if h4 == nheads - 1 else None)
        rs_tok = rec.now(SSC)
        rec.wait("vector", rs_tok)
        rec.op("vector", lambda E, n=nheads: E.tensor_scalar(
            STAT[:, KRMS0:KRMS0 + n], STAT[:, KRMS0:KRMS0 + n],
            1.0 / DH, RMS_EPS, op0=ALU.mult, op1=ALU.add), inc=(SV, 1))
        ms_tok = rec.now(SV)
        free[("KRMS",)] = ms_tok
        rec.wait("scalar", ms_tok)
        rec.op("scalar", lambda E, n=nheads: E.activation(
            STAT[:, RKC0:RKC0 + n], STAT[:, KRMS0:KRMS0 + n], ACT.Sqrt), inc=(SSC, 1))
        sq2_tok = rec.now(SSC)
        rec.wait("vector", sq2_tok)
        rec.op("vector", lambda E, n=nheads: E.reciprocal(
            STAT[:, RKC0:RKC0 + n], STAT[:, RKC0:RKC0 + n]), inc=(SV, 1))
        rk_tok = rec.now(SV)
        rec.wait("vector", rk_tok)
        for h4 in range(nheads):
            rec.op("vector", lambda E, h4=h4: E.tensor_scalar_mul(
                NT_SCR[:, h4 * 128:(h4 + 1) * 128], ps_src[:, h4 * 128:(h4 + 1) * 128],
                stat(RKC0 + h4)))
        nh = nheads
        nt3 = NT_SCR[:, 0:nh * 128].rearrange("p (h d) -> p h d", h=nh)
        x1 = nt3[:, :, 0:128:2]
        x2 = nt3[:, :, 1:128:2]
        cos3 = CSST[cs_slot][:, 0:nh * 64].rearrange("p (h d) -> p h d", h=nh)
        sin3 = CSST[cs_slot][:, 256:256 + nh * 64].rearrange("p (h d) -> p h d", h=nh)
        s1 = RO_S1[:, 0:nh * 64].rearrange("p (h d) -> p h d", h=nh)
        s2 = RO_S2[:, 0:nh * 64].rearrange("p (h d) -> p h d", h=nh)
        s3 = RO_S3[:, 0:nh * 64].rearrange("p (h d) -> p h d", h=nh)
        s4 = RO_S4[:, 0:nh * 64].rearrange("p (h d) -> p h d", h=nh)
        o3 = RP_SCR[:, 0:nh * 128].rearrange("p (h d) -> p h d", h=nh)
        rec.wait("vector", free.get(("RP",)))
        rec.op("vector", lambda E: E.tensor_mul(s1, x1, cos3))
        rec.op("vector", lambda E: E.tensor_mul(s2, x2, sin3))
        rec.op("vector", lambda E: E.tensor_mul(s3, x1, sin3))
        rec.op("vector", lambda E: E.tensor_mul(s4, x2, cos3))
        rec.op("vector", lambda E: E.tensor_sub(o3[:, :, 0:64], s1, s2))
        rec.op("vector", lambda E: E.tensor_add(o3[:, :, 64:128], s3, s4), inc=(SV, 1))
        return rec.now(SV)

    def pa_stage2(tt, xn_tok, cs_tok):
        if tt < HT:
            dest = (lambda dc, tt=tt: XN[:, dc, tt * 128:(tt + 1) * 128])
            guard_key = None
        else:
            dest = (lambda dc: XNT_TMP[:, dc, :])
            guard_key = ("XNT",)
        emit_xt_stage(tt, dest, do_router=True, do_lo=True, xn_tok=xn_tok,
                      evac_guard_key=guard_key)
        cb = tt % 2
        rec.wait("tensor", wkv_tok)
        rec.wait("tensor", free.get(("PS", 0)))
        for dc in range(DC):
            rec.op("tensor", lambda E, d=dest(dc), dc=dc: E.matmul(
                PS[0][:], d, WKST[:, dc, :], start=(dc == 0), stop=(dc == DC - 1)),
                inc=(STn, 1) if dc == DC - 1 else None)
        k_tok = rec.now(STn)
        rec.wait("tensor", free.get(("PS", 1)))
        for dc in range(DC):
            rec.op("tensor", lambda E, d=dest(dc), dc=dc: E.matmul(
                PS[1][:], d, WVST[:, dc, :], start=(dc == 0), stop=(dc == DC - 1)),
                inc=(STn, 1) if dc == DC - 1 else None)
        v_tok = rec.now(STn)
        free[("XNT",)] = v_tok

        rec.wait("scalar", k_tok)
        rec.wait("vector", cs_tok)
        rr_tok = emit_rms_rope(PS[0], KV, cb)
        free[("CSST", cb)] = rr_tok
        free[("PS", 0)] = rr_tok
        rec.wait("tensor", rr_tok)
        for h4 in range(KV):
            j = h4 % 2
            rec.wait("tensor", free.get(("PT", j)))
            rec.op("tensor", lambda E, h4=h4, j=j: E.transpose(
                PT[j], RP_SCR[:, h4 * 128:(h4 + 1) * 128], ID16), inc=(STn, 1))
            tr_tok = rec.now(STn)
            rec.wait("vector", tr_tok)
            rec.op("vector", lambda E, h4=h4, j=j, tt=tt: E.tensor_copy(
                KT[:, h4, tt * 128:(tt + 1) * 128], PT[j]), inc=(SV, 1))
            free[("PT", j)] = rec.now(SV)
        free[("RP",)] = (STn, STn.n)

        rec.wait("vector", v_tok)
        rec.op("vector", lambda E, tt=tt: E.tensor_copy(
            VT[:, :, tt, 0:128],
            PS[1][:].rearrange("p (kv d) -> p kv d", kv=KV)), inc=(SV, 1))
        free[("PS", 1)] = rec.now(SV)

    pend = None
    for tt in range(NT):
        toks = emit_ln_stage(tt, do_lo=True, dup_load=(tt == 0))
        if pend is not None:
            pa_stage2(pend[0], pend[1], pend[2])
        pend = (tt, toks[0], toks[1])
    pa_stage2(*pend)

    def emit_q_half(half, xn_ready):
        for hq in range(4):
            wb = hq % 2
            rec.wait("gpsimd", free.get(("WST", wb)))
            rec.op("gpsimd", lambda E, hq=hq, wb=wb: E.dma_start(
                out=WST[wb],
                in_=wqT_d[:, hq * 512:(hq + 1) * 512].rearrange("(dc p) n -> p dc n", p=128)),
                inc=(SW, 16))
            wq_tok = rec.now(SW)
            for ti in range(HT):
                tt = half * HT + ti
                cb = ti % 2
                rec.wait("sync", free.get(("CSST", cb)))
                rec.op("sync", lambda E, tt=tt, cb=cb: E.dma_start(
                    out=CSST[cb], in_=cs_d[tt * 128:(tt + 1) * 128, :]), inc=(SCS, 16))
                cs_tok = rec.now(SCS)
                rec.wait("tensor", wq_tok)
                rec.wait("tensor", xn_ready)
                rec.wait("tensor", free.get(("PS", 0)))
                for dc in range(DC):
                    rec.op("tensor", lambda E, dc=dc, ti=ti, wb=wb: E.matmul(
                        PS[0][:], XN[:, dc, ti * 128:(ti + 1) * 128], WST[wb][:, dc, :],
                        start=(dc == 0), stop=(dc == DC - 1)),
                        inc=(STn, 1) if dc == DC - 1 else None)
                q_tok = rec.now(STn)
                rec.wait("scalar", q_tok)
                rec.wait("vector", cs_tok)
                rr_tok = emit_rms_rope(PS[0], 4, cb)
                free[("CSST", cb)] = rr_tok
                free[("PS", 0)] = rr_tok
                if ti == HT - 1:
                    free[("WST", wb)] = rr_tok
                rec.wait("tensor", rr_tok)
                for h4 in range(4):
                    j = h4 % 2
                    head = hq * 4 + h4
                    rec.wait("tensor", free.get(("PT", j)))
                    rec.op("tensor", lambda E, h4=h4, j=j: E.transpose(
                        PT[j], RP_SCR[:, h4 * 128:(h4 + 1) * 128], ID16), inc=(STn, 1))
                    tr_tok = rec.now(STn)
                    rec.wait("vector", tr_tok)
                    rec.op("vector", lambda E, head=head, j=j, ti=ti: E.tensor_copy(
                        QT[:, head, ti * 128:(ti + 1) * 128], PT[j]), inc=(SV, 1))
                    free[("PT", j)] = rec.now(SV)
                free[("RP",)] = (STn, STn.n)
        return rec.now(SV)

    def vgap():
        rec.op("vector", lambda E: E.tensor_copy(STAT[:, 31:32], STAT[:, 30:31]))

    def emit_router_weights():
        for tt in range(NT):
            lg = LGT[:, tt, :]
            rec.op("vector", lambda E, lg=lg: E.reduce_max(stat(M1C), lg, axis=AX.X),
                   inc=(SV, 1))
            rec.wait("vector", rec.now(SV))
            rec.op("vector", lambda E, lg=lg: E.tensor_scalar(
                ST8A[:], lg, stat(M1C), None, op0=ALU.is_equal))
            vgap()
            rec.op("vector", lambda E, lg=lg: E.scalar_tensor_tensor(
                ST8B[:], ST8A[:], -1e30, lg, op0=ALU.mult, op1=ALU.add))
            vgap()
            rec.op("vector", lambda E: E.reduce_max(stat(M2CC), ST8B[:], axis=AX.X),
                   inc=(SV, 1))
            rec.wait("vector", rec.now(SV))
            rec.op("vector", lambda E: E.tensor_sub(stat(DMC), stat(M2CC), stat(M1C)),
                   inc=(SV, 1))
            dm_tok = rec.now(SV)
            rec.wait("scalar", dm_tok)
            rec.op("scalar", lambda E: E.activation(stat(EWC), stat(DMC), ACT.Exp),
                   inc=(SSC, 1))
            ew_tok = rec.now(SSC)
            rec.wait("vector", ew_tok)
            rec.op("vector", lambda E: E.tensor_scalar(
                stat(EW2), stat(EWC), 1.0, None, op0=ALU.add))
            vgap()
            rec.op("vector", lambda E: E.reciprocal(stat(W1CC), stat(EW2)), inc=(SV, 1))
            rec.wait("vector", rec.now(SV))
            rec.op("vector", lambda E: E.tensor_scalar(
                stat(W2CC), stat(W1CC), -1.0, 1.0, op0=ALU.mult, op1=ALU.add), inc=(SV, 1))
            w2_tok = rec.now(SV)
            rec.op("vector", lambda E: E.tensor_scalar_mul(ST8B[:], ST8A[:], stat(W1CC)))
            rec.op("vector", lambda E, lg=lg: E.scalar_tensor_tensor(
                ST8C[:], ST8A[:], -1e30, lg, op0=ALU.mult, op1=ALU.add))
            vgap()
            rec.op("vector", lambda E: E.tensor_scalar(
                ST8C[:], ST8C[:], stat(M2CC), None, op0=ALU.is_equal))
            rec.wait("vector", w2_tok)
            vgap()
            rec.op("vector", lambda E, tt=tt: E.scalar_tensor_tensor(
                WRT[:, tt, :], ST8C[:], stat(W2CC), ST8B[:], op0=ALU.mult, op1=ALU.add),
                inc=(SV, 1))

    def emit_attn_half(half, ready_tok):
        rec.wait("tensor", ready_tok)
        rec.wait("vector", rec.now(STn))
        rec.wait("vector", consts_tok)
        for h in range(H):
            kv = h // 4
            for cr in range(2):
                cq = half * 2 + cr
                kclo = max(0, 4 * cq - 8)
                kchi = 4 * cq + 3
                kcs = list(range(kclo, kchi + 1))
                pv_first = [True] * 4

                def emit_S(i, kc):
                    sb_ = i % 2
                    rec.wait("tensor", free.get(("PS", 4 + sb_)))
                    rec.op("tensor", lambda E, kc=kc, sb_=sb_, kv=kv, h=h, cr=cr: E.matmul(
                        PS[4 + sb_][:], KT[:, kv, kc * 128:(kc + 1) * 128],
                        QT[:, h, cr * 512:(cr + 1) * 512], start=True, stop=True),
                        inc=(STn, 1))
                    s_tok = rec.now(STn)
                    rec.wait("scalar", s_tok)
                    rec.wait("scalar", free.get(("ATB", sb_)))
                    rec.op("scalar", lambda E, sb_=sb_: E.activation(
                        ATB[sb_][:], PS[4 + sb_][:], ACT.Exp, scale=SCALE), inc=(SSC, 1))
                    exp_tok = rec.now(SSC)
                    free[("PS", 4 + sb_)] = exp_tok
                    a_ready = exp_tok
                    causal_qr = kc - 4 * cq if 4 * cq <= kc <= 4 * cq + 3 else None
                    drop_row0 = (kc == 4 * cq - 8)
                    if causal_qr is not None or drop_row0:
                        rec.wait("vector", exp_tok)
                        if causal_qr is not None:
                            rec.op("vector", lambda E, sb_=sb_, qr=causal_qr: E.tensor_mul(
                                ATB[sb_][:, qr * 128:(qr + 1) * 128],
                                ATB[sb_][:, qr * 128:(qr + 1) * 128], MASK_C),
                                inc=None if drop_row0 else (SV, 1))
                        if drop_row0:
                            rec.op("vector", lambda E, sb_=sb_: E.memset(
                                ATB[sb_][0:1, :], 0.0), inc=(SV, 1))
                        a_ready = rec.now(SV)
                    return a_ready

                def emit_PV(i, kc, a_ready):
                    sb_ = i % 2
                    rec.wait("tensor", a_ready)
                    qs_lo = max(0, kc - 4 * cq)
                    valid_qrs = list(range(qs_lo, 4))
                    for vi, qr in enumerate(valid_qrs):
                        qs = 4 * cq + qr
                        is_stop = (kc == qs)
                        is_last = (vi == len(valid_qrs) - 1)
                        if pv_first[qr]:
                            rec.wait("tensor", free.get(("PS", qr)))
                        rec.op("tensor", lambda E, sb_=sb_, qr=qr, kv=kv, kc=kc,
                               f=pv_first[qr], st=is_stop: E.matmul(
                            PS[qr][:, 0:129], ATB[sb_][:, qr * 128:(qr + 1) * 128],
                            VT[:, kv, kc, :].squeeze(), start=f, stop=st),
                            inc=(STn, 1) if (is_stop or is_last) else None)
                        pv_first[qr] = False
                        if is_stop:
                            pv_tok = rec.now(STn)
                            ti = qs - half * HT
                            rec.wait("vector", pv_tok)
                            rec.op("vector", lambda E, qr=qr: E.reciprocal(
                                stat(RCOL), PS[qr][:, 128:129]), inc=(SV, 1))
                            rc_tok = rec.now(SV)
                            rec.wait("vector", rc_tok)
                            rec.op("vector", lambda E, qr=qr, ti=ti, h=h: E.tensor_scalar_mul(
                                AO[:, ti, h * 128:(h + 1) * 128], PS[qr][:, 0:128], stat(RCOL)),
                                inc=(SV, 1))
                            free[("PS", qr)] = rec.now(SV)
                    free[("ATB", sb_)] = (STn, STn.n)

                pend = None
                for i, kc in enumerate(kcs):
                    a_ready = emit_S(i, kc)
                    if pend is not None:
                        emit_PV(pend[0], pend[1], pend[2])
                    pend = (i, kc, a_ready)
                emit_PV(*pend)

    def emit_atT_half():
        rec.wait("tensor", rec.now(SV))
        pt4 = [PT[0][:], PT[1][:],
               PS[4].bitcast(BF16)[:, 0:128], PS[5].bitcast(BF16)[:, 0:128]]
        for ti in range(HT):
            for hc in range(H):
                j = (ti * H + hc) % 4
                rec.wait("tensor", free.get(("PT4", j)))
                rec.op("tensor", lambda E, ti=ti, hc=hc, j=j: E.transpose(
                    pt4[j], AO[:, ti, hc * 128:(hc + 1) * 128], ID16), inc=(STn, 1))
                tr_tok = rec.now(STn)
                rec.wait("vector", tr_tok)
                rec.op("vector", lambda E, ti=ti, hc=hc, j=j: E.tensor_copy(
                    QT[:, hc, ti * 128:(ti + 1) * 128], pt4[j]), inc=(SV, 1))
                free[("PT4", j)] = rec.now(SV)
        free[("PS", 4)] = free[("PT4", 2)]
        free[("PS", 5)] = free[("PT4", 3)]
        free[("PT", 0)] = free[("PT4", 0)]
        free[("PT", 1)] = free[("PT4", 1)]

    def emit_wo_half():
        rec.wait("tensor", rec.now(SV))
        for dq in range(4):
            wb = dq % 2
            rec.wait("gpsimd", free.get(("WST", wb)))
            rec.op("gpsimd", lambda E, dq=dq, wb=wb: E.dma_start(
                out=WST[wb],
                in_=woT_d[:, dq * 512:(dq + 1) * 512].rearrange("(hc p) n -> p hc n", p=128)),
                inc=(SW, 16))
            wo_tok = rec.now(SW)
            rec.wait("tensor", wo_tok)
            for ti in range(HT):
                pb = ti % 2
                rec.wait("tensor", free.get(("PS", pb)))
                for hc in range(H):
                    rec.op("tensor", lambda E, hc=hc, ti=ti, wb=wb, pb=pb: E.matmul(
                        PS[pb][:], QT[:, hc, ti * 128:(ti + 1) * 128], WST[wb][:, hc, :],
                        start=(hc == 0), stop=(hc == H - 1)),
                        inc=(STn, 1) if hc == H - 1 else None)
                y_tok = rec.now(STn)
                rec.wait("vector", y_tok)
                rec.op("vector", lambda E, ti=ti, dq=dq, pb=pb: E.tensor_copy(
                    AO[:, ti, dq * 512:(dq + 1) * 512], PS[pb][:]), inc=(SV, 1))
                free[("PS", pb)] = rec.now(SV)
                if ti == HT - 1:
                    free[("WST", wb)] = rec.now(SV)

    def emit_ffn_half(half):
        t_now = rec.now(STn)
        rec.wait("vector", t_now)
        rec.wait("sync", t_now)
        rec.wait("gpsimd", t_now)
        fold_done = {}
        for idx, ti in enumerate((0, 1, 4, 5, 2, 3, 6, 7)):
            tt = half * HT + ti
            b = idx % 2
            rec.wait("sync", free.get(("FXSTG", b)))
            rec.op("sync", lambda E, tt=tt, b=b: E.dma_start(
                out=FXSTG[b], in_=x_d[tt * 128:(tt + 1) * 128, :]), inc=(SX, 16))
            if idx == 0:
                rec.op("sync", lambda E, tt=tt, b=b: E.dma_start(
                    out=FXSTG[b], in_=x_d[tt * 128:(tt + 1) * 128, :]), inc=(SX, 16))
            x_tok = rec.now(SX)
            rec.wait("vector", x_tok)
            rec.op("vector", lambda E, ti=ti, b=b: E.tensor_tensor(
                ACC[:, ti, :], FXSTG[b], AO[:, ti, :], ALU.add), inc=(SV, 1))
            free[("FXSTG", b)] = rec.now(SV)
            fold_done[ti] = rec.now(SV)
        fold_tok = rec.now(SV)
        free[("W1ST", 0)] = fold_done[1]
        free[("W1ST", 1)] = fold_done[3]
        free[("W2ST", 0)] = fold_done[5]
        free[("W2ST", 1)] = fold_done[7]

        for e in range(NE):
            for fq in range(4):
                wb = fq % 2
                rec.wait("gpsimd", free.get(("W1ST", wb)))
                rec.op("gpsimd", lambda E, e=e, fq=fq, wb=wb: E.dma_start(
                    out=W1ST[wb],
                    in_=w1_d[e, :, fq * 256:(fq + 1) * 256].rearrange("(dc p) f -> p dc f", p=128)),
                    inc=(SW, 16))
                rec.wait("gpsimd", free.get(("W2ST", wb)))
                rec.op("gpsimd", lambda E, e=e, fq=fq, wb=wb: E.dma_start(
                    out=W2ST[wb],
                    in_=w2_d[e, :, fq * 256:(fq + 1) * 256].rearrange("(dc p) f -> p dc f", p=128)),
                    inc=(SW, 16))
                w12_tok = rec.now(SW)
                rec.wait("tensor", w12_tok)
                for fcl in range(2):
                    fc = fq * 2 + fcl
                    for tq in range(2):
                        rec.wait("tensor", free.get(("PS", 2)))
                        for dc in range(DC):
                            rec.op("tensor", lambda E, dc=dc, fcl=fcl, tq=tq, wb=wb: E.matmul(
                                PS[2][:], W1ST[wb][:, dc, fcl * 128:(fcl + 1) * 128],
                                XN[:, dc, tq * 512:(tq + 1) * 512],
                                start=(dc == 0), stop=(dc == DC - 1)),
                                inc=(STn, 1) if dc == DC - 1 else None)
                        g_tok = rec.now(STn)
                        rec.wait("tensor", free.get(("PS", 3)))
                        for dc in range(DC):
                            rec.op("tensor", lambda E, dc=dc, fcl=fcl, tq=tq, wb=wb: E.matmul(
                                PS[3][:], W2ST[wb][:, dc, fcl * 128:(fcl + 1) * 128],
                                XN[:, dc, tq * 512:(tq + 1) * 512],
                                start=(dc == 0), stop=(dc == DC - 1)),
                                inc=(STn, 1) if dc == DC - 1 else None)
                        vv_tok = rec.now(STn)
                        rec.wait("scalar", g_tok)
                        rec.wait("scalar", free.get(("RP",)))
                        rec.op("scalar", lambda E: E.activation(
                            RP_SCR[:], PS[2][:], ACT.Silu), inc=(SSC, 1))
                        si_tok = rec.now(SSC)
                        free[("PS", 2)] = si_tok
                        rec.wait("vector", si_tok)
                        rec.wait("vector", vv_tok)
                        rec.wait("vector", free.get(("HBUF",)))
                        rec.op("vector", lambda E, fc=fc, tq=tq: E.tensor_tensor(
                            HBUF[:, fc, tq * 512:(tq + 1) * 512], RP_SCR[:], PS[3][:],
                            ALU.mult), inc=(SV, 1))
                        free[("PS", 3)] = rec.now(SV)
                        free[("RP",)] = rec.now(SV)
                        if fcl == 1 and tq == 1:
                            free[("W1ST", wb)] = rec.now(SV)
                            free[("W2ST", wb)] = rec.now(SV)
            h_tok = rec.now(SV)
            for dq in range(4):
                wb = dq % 2
                rec.wait("gpsimd", free.get(("W3ST", wb)))
                rec.op("gpsimd", lambda E, e=e, dq=dq, wb=wb: E.dma_start(
                    out=W3ST[wb],
                    in_=w3_d[e, :, dq * 512:(dq + 1) * 512].rearrange("(fc p) d -> p fc d", p=128)),
                    inc=(SW, 16))
                w3_tok = rec.now(SW)
                rec.wait("tensor", w3_tok)
                rec.wait("tensor", h_tok)
                for ti in range(HT):
                    tt = half * HT + ti
                    pb = ti % 2
                    rec.wait("tensor", free.get(("PS", pb)))
                    for fc in range(8):
                        rec.op("tensor", lambda E, fc=fc, ti=ti, wb=wb, pb=pb: E.matmul(
                            PS[pb][:], HBUF[:, fc, ti * 128:(ti + 1) * 128], W3ST[wb][:, fc, :],
                            start=(fc == 0), stop=(fc == 7)),
                            inc=(STn, 1) if fc == 7 else None)
                    o_tok = rec.now(STn)
                    if dq == 3 and ti == HT - 1:
                        free[("HBUF",)] = (STn, STn.n)
                    rec.wait("vector", o_tok)
                    rec.op("vector", lambda E, ti=ti, dq=dq, pb=pb, tt=tt, e=e:
                           E.scalar_tensor_tensor(
                               ACC[:, ti, dq * 512:(dq + 1) * 512], PS[pb][:],
                               WRT[:, tt, e:e + 1], ACC[:, ti, dq * 512:(dq + 1) * 512],
                               op0=ALU.mult, op1=ALU.add), inc=(SV, 1))
                    free[("PS", pb)] = rec.now(SV)
                    if ti == HT - 1:
                        free[("W3ST", wb)] = rec.now(SV)
        acc_tok = rec.now(SV)
        rec.wait("sync", acc_tok)
        for ti in range(HT):
            tt = half * HT + ti
            rec.op("sync", lambda E, tt=tt, ti=ti: E.dma_start(
                out=out_d[tt * 128:(tt + 1) * 128, :], in_=ACC[:, ti, :]), inc=(SST, 16))
        st_tok = rec.now(SST)
        for k in (("WST", 0), ("WST", 1), ("XSTG", 0), ("XSTG", 1),
                  ("CSST", 0), ("CSST", 1)):
            free[k] = st_tok
        free[("XNTOK",)] = st_tok

    def dbg_dump(key, src):
        if not debug:
            return
        tok = rec.now(SV)
        rec.wait("sync", tok)
        rec.op("sync", lambda E, key=key, src=src: E.dma_start(out=dbg[key], in_=src),
               inc=(SST, 16))

    pa_tok = rec.now(SV)
    emit_q_half(0, pa_tok)
    q1_tok = rec.now(SV)
    emit_router_weights()
    emit_attn_half(0, q1_tok)
    emit_atT_half()
    emit_wo_half()
    if debug:
        mid_tok = rec.now(SV)
        rec.wait("sync", mid_tok)
        rec.op("sync", lambda E: E.dma_start(out=dbg["xnT"], in_=XN[:]), inc=(SST, 16))
        rec.op("sync", lambda E: E.dma_start(out=dbg["y"], in_=AO[:]), inc=(SST, 16))
        rec.op("sync", lambda E: E.dma_start(out=dbg["qT"], in_=QT[:]), inc=(SST, 16))
        dump_done = rec.now(SST)
        rec.wait("vector", dump_done)
        rec.wait("tensor", dump_done)
        rec.wait("scalar", dump_done)
        rec.wait("gpsimd", dump_done)
    emit_ffn_half(0)

    rec.wait("vector", rec.now(STn))
    rec.wait("scalar", rec.now(STn))
    pend = None
    for ti in range(HT):
        tt = HT + ti
        toks = emit_ln_stage(tt, do_lo=False, dup_load=(ti == 0), load_cs=False)
        if pend is not None:
            emit_xt_stage(pend[0], pend[1], do_router=False, do_lo=False, xn_tok=pend[2])
        pend = (tt, (lambda dc, ti=ti: XN[:, dc, ti * 128:(ti + 1) * 128]), toks[0])
    emit_xt_stage(pend[0], pend[1], do_router=False, do_lo=False, xn_tok=pend[2])
    xn2_tok = rec.now(SV)
    emit_q_half(1, xn2_tok)
    emit_attn_half(1, rec.now(SV))
    emit_atT_half()
    emit_wo_half()
    emit_ffn_half(1)

    if debug:
        end_tok = rec.now(SV)
        rec.wait("sync", end_tok)
        rec.wait("sync", rec.now(SST))
        for key, src in (("kT", KT[:]), ("vT", VT[:]), ("W", WRT[:]),
                         ("lg", LGT[:]), ("ao", AO[:])):
            rec.op("sync", lambda E, key=key, src=src: E.dma_start(out=dbg[key], in_=src),
                   inc=(SST, 16))

    for s_ in (SST, SX, SCS, SW):
        rec.wait("gpsimd", rec.now(s_))

    with nc.Block() as block:
        def mk(eng_name):
            def body(E):
                for f in rec.ops[eng_name]:
                    f(E)
            return body
        block.sync(mk("sync"))
        block.tensor(mk("tensor"))
        block.vector(mk("vector"))
        block.scalar(mk("scalar"))
        block.gpsimd(mk("gpsimd"))

    counts = {k: len(v) for k, v in rec.ops.items()}
    print("bass program op counts:", counts, "total:", sum(counts.values()))
    return nc


def _host_prep(inputs):
    x = np.ascontiguousarray(np.asarray(inputs["L0_attn_x"], dtype=np.float32))
    wqT = np.ascontiguousarray(np.asarray(inputs["L0_attn_wq"], dtype=np.float32).T)
    wkT = np.ascontiguousarray(np.asarray(inputs["L0_attn_wk"], dtype=np.float32).T)
    wvT = np.ascontiguousarray(np.asarray(inputs["L0_attn_wv"], dtype=np.float32).T)
    woT = np.ascontiguousarray(np.asarray(inputs["L0_attn_wo"], dtype=np.float32).T)
    rwT = np.ascontiguousarray(np.asarray(inputs["L0_ffn_router_w"], dtype=np.float32).T)
    w1 = np.ascontiguousarray(np.asarray(inputs["L0_ffn_w1"], dtype=np.float32))
    w2 = np.ascontiguousarray(np.asarray(inputs["L0_ffn_w2"], dtype=np.float32))
    w3 = np.ascontiguousarray(np.asarray(inputs["L0_ffn_w3"], dtype=np.float32))

    inv_freq = 10000.0 ** (-(np.arange(0, DH, 2, dtype=np.float64) / DH))
    ang = np.outer(np.arange(T, dtype=np.float64), inv_freq)
    cos4 = np.tile(np.cos(ang).astype(np.float32), (1, 4))
    sin4 = np.tile(np.sin(ang).astype(np.float32), (1, 4))
    cs_tab = np.ascontiguousarray(np.concatenate([cos4, sin4], axis=1))

    ident = np.eye(128, dtype=ml_dtypes.bfloat16)
    kk = np.arange(128)[:, None]
    qq = np.arange(128)[None, :]
    mask_c = (kk <= qq).astype(ml_dtypes.bfloat16)
    mask_w = (kk > qq).astype(ml_dtypes.bfloat16)
    consts = np.ascontiguousarray(np.concatenate([ident, mask_c, mask_w], axis=1))

    shared = dict(wqT=wqT, wkT=wkT, wvT=wvT, woT=woT, rwT=rwT,
                  w1=w1, w2=w2, w3=w3, cs_tab=cs_tab, consts=consts)
    in_maps = []
    for c in range(8):
        m = dict(shared)
        m["x"] = np.ascontiguousarray(x[c])
        in_maps.append(m)
    return in_maps


def kernel(**inputs):
    in_maps = _host_prep(inputs)
    nc = build_nc(debug=DEBUG)
    res = run_bass_kernel_spmd(nc, in_maps, core_ids=list(range(8)))
    out = np.stack([np.asarray(r["out"]) for r in res.results], axis=0)
    if DEBUG:
        kernel.last_results = res.results
    return out


# version 13
# speedup vs baseline: 1.4175x; 1.0517x over previous
import math
import numpy as np
import ml_dtypes
import concourse.bass as bass
import concourse.mybir as mybir
from concourse.bass_utils import run_bass_kernel_spmd

F32 = mybir.dt.float32
BF16 = mybir.dt.bfloat16
AX = mybir.AxisListType
ALU = mybir.AluOpType
ACT = mybir.ActivationFunctionType

T = 2048
D = 2048
H = 16
KV = 4
DH = 128
FF = 1024
NE = 8
NT = 16
DC = 16
SCALE = 1.0 / math.sqrt(DH)
LN_EPS = 1e-5
RMS_EPS = 1e-6
HALF_T = 1024
HT = 8

DEBUG = False

ENGS = ("sync", "tensor", "vector", "scalar", "gpsimd")


class Sem:
    def __init__(self, handle, name):
        self.h = handle
        self.name = name
        self.n = 0


class Rec:
    def __init__(self):
        self.ops = {k: [] for k in ENGS}
        self._last_wait = {}

    def op(self, eng, fn, inc=None):
        if inc is not None:
            sem, k = inc
            sem.n += k

            def run(E, fn=fn, h=sem.h, k=k):
                fn(E).then_inc(h, k)
            self.ops[eng].append(run)
            return (sem, sem.n)
        else:
            self.ops[eng].append(lambda E, fn=fn: fn(E))
            return None

    def wait(self, eng, token):
        if token is None:
            return
        sem, v = token
        if v <= 0:
            return
        key = (eng, sem.name)
        if self._last_wait.get(key, -1) >= v:
            return
        self._last_wait[key] = v
        self.ops[eng].append(lambda E, h=sem.h, v=v: E.wait_ge(h, v))

    def now(self, sem):
        return (sem, sem.n)


def build_nc(debug=False):
    nc = bass.Bass("TRN2", target_bir_lowering=False)

    x_d = nc.dram_tensor("x", [T, D], F32, kind="ExternalInput").ap()
    wqT_d = nc.dram_tensor("wqT", [D, H * DH], F32, kind="ExternalInput").ap()
    wkT_d = nc.dram_tensor("wkT", [D, KV * DH], F32, kind="ExternalInput").ap()
    wvT_d = nc.dram_tensor("wvT", [D, KV * DH], F32, kind="ExternalInput").ap()
    woT_d = nc.dram_tensor("woT", [H * DH, D], F32, kind="ExternalInput").ap()
    rwT_d = nc.dram_tensor("rwT", [D, NE], F32, kind="ExternalInput").ap()
    w1_d = nc.dram_tensor("w1", [NE, D, FF], F32, kind="ExternalInput").ap()
    w2_d = nc.dram_tensor("w2", [NE, D, FF], F32, kind="ExternalInput").ap()
    w3_d = nc.dram_tensor("w3", [NE, FF, D], F32, kind="ExternalInput").ap()
    cs_d = nc.dram_tensor("cs_tab", [T, 512], F32, kind="ExternalInput").ap()
    const_d = nc.dram_tensor("consts", [128, 3 * 128], BF16, kind="ExternalInput").ap()

    out_d = nc.dram_tensor("out", [T, D], F32, kind="ExternalOutput").ap()

    dbg = {}
    if debug:
        dbg["xnT"] = nc.dram_tensor("dbg_xnT", [128, DC, HALF_T], BF16, kind="ExternalOutput").ap()
        dbg["kT"] = nc.dram_tensor("dbg_kT", [128, KV, T], BF16, kind="ExternalOutput").ap()
        dbg["vT"] = nc.dram_tensor("dbg_vT", [128, KV, NT, 129], BF16, kind="ExternalOutput").ap()
        dbg["lg"] = nc.dram_tensor("dbg_lg", [128, NT, NE], F32, kind="ExternalOutput").ap()
        dbg["W"] = nc.dram_tensor("dbg_W", [128, NT, NE], F32, kind="ExternalOutput").ap()
        dbg["qT"] = nc.dram_tensor("dbg_qT", [128, H, HALF_T], BF16, kind="ExternalOutput").ap()
        dbg["ao"] = nc.dram_tensor("dbg_ao", [128, HT, D], BF16, kind="ExternalOutput").ap()
        dbg["y"] = nc.dram_tensor("dbg_y", [128, HT, D], F32, kind="ExternalOutput").ap()

    ARENA_SIZE = 212544
    arena = nc.alloc_sbuf_tensor("ARENA", [128, ARENA_SIZE], mybir.dt.uint8)
    ARENA_BASE = nc.lookup_mloc(arena).addr

    def sb(name, shape, dtype, off):
        return nc.alloc_sbuf_tensor_at(name, shape, dtype, offset=ARENA_BASE + off).ap()

    XN = sb("XN", [128, DC, HALF_T], BF16, 0)
    QT = sb("QT", [128, H, HALF_T], BF16, 32768)
    SQSCR = sb("SQSCR", [128, D], F32, 32768)
    XLO_TMP = sb("XLO_TMP", [128, DC, 128], BF16, 40960)
    SQSCR_B = sb("SQSCR_B", [128, D], F32, 49152)
    XNTOK_B = sb("XNTOK_B", [128, D], BF16, 57344)
    XNLO_B = sb("XNLO_B", [128, D], BF16, 61440)
    HBUF = sb("HBUF", [128, 8, HALF_T], BF16, 32768)
    FXSTG = [sb("FXSTG0", [128, D], F32, 32768),
             sb("FXSTG1", [128, D], F32, 40960)]
    W3ST = [sb("W3ST0", [128, 8, 512], BF16, 49152),
            sb("W3ST1", [128, 8, 512], BF16, 57344)]

    AO = sb("AO", [128, HT, D], BF16, 65536)
    WKST = sb("WKST", [128, DC, 512], BF16, 65536)
    WVST = sb("WVST", [128, DC, 512], BF16, 81920)
    W1ST = [sb("W1ST0", [128, DC, 256], BF16, 65536),
            sb("W1ST1", [128, DC, 256], BF16, 73728)]
    W2ST = [sb("W2ST0", [128, DC, 256], BF16, 81920),
            sb("W2ST1", [128, DC, 256], BF16, 90112)]
    WOST = [sb("WOST0", [128, DC, 512], BF16, 65536),
            sb("WOST1", [128, DC, 512], BF16, 81920)]
    SQSCR_R = sb("SQSCR_R", [128, D], F32, 65536)
    XNTOK_R = [sb("XNTOK_R0", [128, D], BF16, 73728),
               sb("XNTOK_R1", [128, D], BF16, 77824)]
    XSTG_R = [sb("XSTG_R0", [128, D], F32, 81920),
              sb("XSTG_R1", [128, D], F32, 90112)]

    ACC = sb("ACC", [128, HT, D], F32, 98304)
    WST = [sb("WST0", [128, DC, 512], BF16, 98304),
           sb("WST1", [128, DC, 512], BF16, 114688)]
    XSTG = [sb("XSTG0", [128, D], F32, 131072),
            sb("XSTG1", [128, D], F32, 139264),
            sb("XSTG2", [128, D], F32, 98304),
            sb("XSTG3", [128, D], F32, 106496)]
    CSST = [sb("CSST0", [128, 512], F32, 147456),
            sb("CSST1", [128, 512], F32, 149504)]
    XNTOK = sb("XNTOK", [128, D], BF16, 151552)
    XNLO = sb("XNLO", [128, D], BF16, 155648)
    XNT_TMP = sb("XNT_TMP", [128, DC, 128], BF16, 159744)

    KT = sb("KT", [128, KV, T], BF16, 163840)
    VT = sb("VT", [128, KV, NT, 129], BF16, 180224)
    ID16 = sb("ID16", [128, 128], BF16, 196736)
    MASK_C = sb("MASK_C", [128, 128], BF16, 196992)
    MASK_W = sb("MASK_W", [128, 128], BF16, 197248)
    RWTF = sb("RWTF", [128, DC, NE], F32, 197504)
    RWT_HI = sb("RWT_HI", [128, DC, NE], BF16, 198016)
    RWT_LO = sb("RWT_LO", [128, DC, NE], BF16, 198272)
    LGT = sb("LGT", [128, NT, NE], F32, 198528)
    WRT = sb("WRT", [128, NT, NE], F32, 199040)
    STAT = sb("STAT", [128, 32], F32, 199552)
    ST8A = sb("ST8A", [128, NE], F32, 199680)
    ST8B = sb("ST8B", [128, NE], F32, 199712)
    ATB = [sb("ATB0", [128, 512], BF16, 199744),
           sb("ATB1", [128, 512], BF16, 200768)]
    M1A = sb("M1A", [128, NT], F32, 209504)
    M2A = sb("M2A", [128, NT], F32, 209568)
    DMAr = sb("DMAr", [128, NT], F32, 209632)
    EWA = sb("EWA", [128, NT], F32, 209696)
    EW2A = sb("EW2A", [128, NT], F32, 209760)
    W1A = sb("W1A", [128, NT], F32, 209824)
    W2A = sb("W2A", [128, NT], F32, 209888)
    S8A = sb("S8A", [128, NT, NE], F32, 209952)
    S8B = sb("S8B", [128, NT, NE], F32, 210464)
    S8C = sb("S8C", [128, NT, NE], F32, 210976)
    NT_SCR = sb("NT_SCR", [128, 512], F32, 201792)
    RO_S1 = sb("RO_S1", [128, 256], F32, 203840)
    RO_S2 = sb("RO_S2", [128, 256], F32, 204864)
    RP_SCR = sb("RP_SCR", [128, 512], BF16, 205888)
    RP_B = sb("RP_B", [128, 512], BF16, 211488)
    RMS_SQ = sb("RMS_SQ", [128, 128], F32, 206912)
    RO_S3 = sb("RO_S3", [128, 256], F32, 207424)
    RO_S4 = sb("RO_S4", [128, 256], F32, 208448)
    ST8C = sb("ST8C", [128, NE], F32, 209472)

    PS = [nc.alloc_psum_tensor(f"PS{i}", [128, 512], F32).ap() for i in range(6)]
    PT = [nc.alloc_psum_tensor(f"PT{i}", [128, 128], BF16).ap() for i in range(2)]

    rec = Rec()
    sems = {}
    for name in ("X", "CS", "W", "ST", "T", "V", "SC", "G"):
        sems[name] = Sem(nc.alloc_semaphore(name=f"S_{name}"), name)
    (SX, SCS, SW, SST, STn, SV, SSC, SG) = (
        sems[k] for k in ("X", "CS", "W", "ST", "T", "V", "SC", "G"))

    nc.all_engine_barrier()
    sem_nums = [s.h.num for s in sems.values()]
    nc.gpsimd.dma_reset(range(min(sem_nums), max(sem_nums) + 1))
    for s in sems.values():
        nc.gpsimd.sem_clear(s.h)
    nc.all_engine_barrier()
    warm = nc.alloc_semaphore(name="S_WARM")
    nc.sync.dma_start(out=XSTG[0], in_=x_d[0:128, :]).then_inc(warm, 16)
    nc.gpsimd.dma_start(out=XSTG[1], in_=x_d[128:256, :]).then_inc(warm, 16)
    nc.gpsimd.wait_ge(warm, 32)
    for s in sems.values():
        nc.gpsimd.sem_clear(s.h)
    nc.all_engine_barrier()

    def stat(i):
        return STAT[:, i:i + 1]

    MVc, EX2, M2C, NEGM = 0, 1, 2, 3
    VARV, RSTD, RCOL = 4, 5, 6
    KRMS0 = 8
    SUMS2 = [12, 13]
    RKC0 = 16
    W1CC, M1C, M2CC, DMC = 20, 21, 22, 23
    STDV, EWC = 25, 26
    SQc2 = [24, 27]
    W2CC, EW2 = 28, 29

    free = {}

    for _rep in range(2):
        rec.op("sync", lambda E: E.dma_start(out=ID16, in_=const_d[:, 0:128]), inc=(SCS, 16))
        rec.op("sync", lambda E: E.dma_start(out=MASK_C, in_=const_d[:, 128:256]), inc=(SCS, 16))
        rec.op("sync", lambda E: E.dma_start(out=MASK_W, in_=const_d[:, 256:384]), inc=(SCS, 16))
        rec.op("sync", lambda E: E.dma_start(
            out=RWTF, in_=rwT_d.rearrange("(dc p) e -> p dc e", p=128)), inc=(SCS, 16))
    consts_tok = rec.now(SCS)
    rec.wait("vector", consts_tok)
    rec.op("vector", lambda E: E.tensor_copy(RWT_HI, RWTF), inc=(SV, 1))
    rec.wait("vector", rec.now(SV))
    rec.op("vector", lambda E: E.tensor_tensor(RWT_LO, RWTF, RWT_HI, ALU.subtract), inc=(SV, 1))
    rwhl_tok = rec.now(SV)
    rec.op("vector", lambda E: E.memset(
        VT[:, :, :, 128:129].rearrange("p a b c -> p (a b c)"), 1.0), inc=(SV, 1))

    for _rep in range(2):
        rec.op("gpsimd", lambda E: E.dma_start(
            out=WKST, in_=wkT_d.rearrange("(dc p) n -> p dc n", p=128)), inc=(SW, 16))
        rec.op("gpsimd", lambda E: E.dma_start(
            out=WVST, in_=wvT_d.rearrange("(dc p) n -> p dc n", p=128)), inc=(SW, 16))
    wkv_tok = rec.now(SW)

    RP2 = [RP_SCR, RP_B]
    XNTOK2 = [XNTOK, XNTOK_B]
    XNLO2 = [XNLO, XNLO_B]
    SQSCR2 = [SQSCR, SQSCR_B]

    def emit_ln_stage(tt, do_lo, dup_load=False, load_cs=True, xstg4=False,
                      bufs=None):
        XSTGl, SQSCRl, XNTOKl, sfx = ((XSTG, SQSCR2, XNTOK2, "") if bufs is None
                                      else bufs)
        b = tt % 2
        bq = b if len(SQSCRl) > 1 else 0
        bx = tt % 4 if xstg4 else tt % len(XSTGl)
        rec.wait("sync", free.get(("XSTG" + sfx, bx)))
        rec.op("sync", lambda E, tt=tt, bx=bx, XSTGl=XSTGl: E.dma_start(
            out=XSTGl[bx], in_=x_d[tt * 128:(tt + 1) * 128, :]), inc=(SX, 16))
        if dup_load:
            rec.op("sync", lambda E, tt=tt, bx=bx, XSTGl=XSTGl: E.dma_start(
                out=XSTGl[bx], in_=x_d[tt * 128:(tt + 1) * 128, :]), inc=(SX, 16))
        x_tok = rec.now(SX)
        cs_tok = None
        if load_cs:
            rec.wait("sync", free.get(("CSST", b)))
            rec.op("sync", lambda E, tt=tt, b=b: E.dma_start(
                out=CSST[b], in_=cs_d[tt * 128:(tt + 1) * 128, :]), inc=(SCS, 16))
            if dup_load:
                rec.op("sync", lambda E, tt=tt, b=b: E.dma_start(
                    out=CSST[b], in_=cs_d[tt * 128:(tt + 1) * 128, :]), inc=(SCS, 16))
            cs_tok = rec.now(SCS)

        rec.wait("scalar", x_tok)
        rec.wait("scalar", free.get(("SQc" + sfx, bq)))
        rec.op("scalar", lambda E, bq=bq, bx=bx, SQSCRl=SQSCRl, XSTGl=XSTGl: E.activation(
            SQSCRl[bq], XSTGl[bx], ACT.Square, accum_out=stat(SQc2[bq])))
        rec.op("scalar", lambda E, bq=bq, bx=bx, SQSCRl=SQSCRl, XSTGl=XSTGl: E.activation(
            SQSCRl[bq], XSTGl[bx], ACT.Copy, accum_out=stat(SUMS2[bq])), inc=(SSC, 1))
        sq_tok = rec.now(SSC)
        rec.wait("vector", sq_tok)
        rec.op("vector", lambda E: E.tensor_copy(STAT[:, 31:32], STAT[:, 30:31]))
        rec.op("vector", lambda E, b=b: E.tensor_scalar(stat(MVc), stat(SUMS2[bq]), 1.0 / D, None, op0=ALU.mult))
        rec.op("vector", lambda E, b=b: E.tensor_scalar(
            stat(EX2), stat(SQc2[bq]), 1.0 / D, LN_EPS, op0=ALU.mult, op1=ALU.add))
        rec.op("vector", lambda E: E.tensor_mul(stat(M2C), stat(MVc), stat(MVc)))
        rec.op("vector", lambda E: E.tensor_scalar(stat(NEGM), stat(MVc), -1.0, None, op0=ALU.mult))
        rec.op("vector", lambda E: E.tensor_sub(stat(VARV), stat(EX2), stat(M2C)),
               inc=(SV, 1))
        var_tok = rec.now(SV)
        rec.wait("scalar", var_tok)
        rec.op("scalar", lambda E: E.activation(stat(STDV), stat(VARV), ACT.Sqrt), inc=(SSC, 1))
        std_tok = rec.now(SSC)
        rec.wait("vector", std_tok)
        rec.op("vector", lambda E: E.reciprocal(stat(RSTD), stat(STDV)), inc=(SV, 1))
        rstd_tok = rec.now(SV)
        rec.wait("vector", rstd_tok)
        rec.wait("vector", free.get(("XNTOK" + sfx, b)))
        lo_tok = None
        if do_lo:
            rec.op("vector", lambda E, b=b, bx=bx: E.tensor_scalar(
                SQSCR2[b], XSTG[bx], stat(NEGM), stat(RSTD), op0=ALU.add, op1=ALU.mult))
            rec.op("vector", lambda E, b=b: E.tensor_copy(XNTOK2[b], SQSCR2[b]),
                   inc=(SV, 1))
            xn_tok = rec.now(SV)
            rec.wait("gpsimd", xn_tok)
            rec.wait("gpsimd", free.get(("XNLO", b)))
            rec.op("gpsimd", lambda E, b=b: E.tensor_tensor(
                XNLO2[b], SQSCR2[b], XNTOK2[b], ALU.subtract), inc=(SG, 1))
            lo_tok = rec.now(SG)
        else:
            rec.op("vector", lambda E, b=b, bx=bx, XNTOKl=XNTOKl, XSTGl=XSTGl: E.tensor_scalar(
                XNTOKl[b], XSTGl[bx], stat(NEGM), stat(RSTD), op0=ALU.add, op1=ALU.mult),
                inc=(SV, 1))
            xn_tok = rec.now(SV)
        free[("XSTG" + sfx, bx)] = xn_tok
        free[("SQc" + sfx, bq)] = lo_tok if do_lo else xn_tok
        return xn_tok, lo_tok, cs_tok

    def emit_xt_stage(tt, dest, do_router, do_lo, xn_tok, lo_tok=None,
                      evac_guard_key=None, hi_src=None, key_sfx=""):
        b = tt % 2
        rec.wait("tensor", xn_tok)
        rec.wait("tensor", consts_tok)
        srcs = [(XNTOK2[b] if hi_src is None else hi_src, dest)]
        if do_lo:
            srcs.append((XNLO2[b], lambda dc: XLO_TMP[:, dc, :]))
        for si, (src, dst) in enumerate(list(srcs)):
            if si == 1:
                rec.wait("tensor", lo_tok)
            for dc in range(DC):
                j = dc % 2
                rec.wait("tensor", free.get(("PT", j)))
                rec.op("tensor", lambda E, dc=dc, j=j, src=src: E.transpose(
                    PT[j], src[:, dc * 128:(dc + 1) * 128], ID16), inc=(STn, 1))
                tr_tok = rec.now(STn)
                rec.wait("vector", tr_tok)
                if evac_guard_key is not None:
                    rec.wait("vector", free.get(evac_guard_key))
                rec.op("vector", lambda E, j=j, d=dst(dc): E.tensor_copy(d, PT[j]), inc=(SV, 1))
                free[("PT", j)] = rec.now(SV)
        free[("XNTOK", b)] = (STn, STn.n)
        xnT_tok = rec.now(SV)

        if do_router:
            rec.wait("tensor", xnT_tok)
            rec.wait("tensor", rwhl_tok)
            rec.wait("tensor", free.get(("PS", 5)))
            combos = [(dest, RWT_HI), (dest, RWT_LO),
                      ((lambda dc: XLO_TMP[:, dc, :]), RWT_HI)]
            nmm = 0
            for lt, rw in combos:
                for dc in range(DC):
                    first = nmm == 0
                    last = nmm == 3 * DC - 1
                    rec.op("tensor", lambda E, l0=lt(dc), r0=rw[:, dc, :], f=first, l=last:
                           E.matmul(PS[5][:, 0:NE], l0, r0, start=f, stop=l),
                           inc=(STn, 1) if last else None)
                    nmm += 1
            lg_tok = rec.now(STn)
            rec.wait("vector", lg_tok)
            rec.op("vector", lambda E, tt=tt: E.tensor_copy(LGT[:, tt, :], PS[5][:, 0:NE]),
                   inc=(SV, 1))
            free[("PS", 5)] = rec.now(SV)
        return xnT_tok

    def emit_rms_rope(ps_src, nheads, cs_slot, rp_idx=0):
        RPX = RP2[rp_idx]
        rec.wait("scalar", free.get(("KRMS",)))
        for h4 in range(nheads):
            rec.op("scalar", lambda E, h4=h4: E.activation(
                RMS_SQ, ps_src[:, h4 * 128:(h4 + 1) * 128], ACT.Square,
                accum_out=stat(KRMS0 + h4)),
                inc=(SSC, 1) if h4 == nheads - 1 else None)
        rs_tok = rec.now(SSC)
        rec.wait("vector", rs_tok)
        rec.op("vector", lambda E, n=nheads: E.tensor_scalar(
            STAT[:, KRMS0:KRMS0 + n], STAT[:, KRMS0:KRMS0 + n],
            1.0 / DH, RMS_EPS, op0=ALU.mult, op1=ALU.add), inc=(SV, 1))
        ms_tok = rec.now(SV)
        free[("KRMS",)] = ms_tok
        rec.wait("scalar", ms_tok)
        rec.op("scalar", lambda E, n=nheads: E.activation(
            STAT[:, RKC0:RKC0 + n], STAT[:, KRMS0:KRMS0 + n], ACT.Sqrt), inc=(SSC, 1))
        sq2_tok = rec.now(SSC)
        rec.wait("vector", sq2_tok)
        rec.op("vector", lambda E, n=nheads: E.reciprocal(
            STAT[:, RKC0:RKC0 + n], STAT[:, RKC0:RKC0 + n]), inc=(SV, 1))
        rk_tok = rec.now(SV)
        rec.wait("scalar", rk_tok)
        for h4 in range(nheads):
            rec.op("scalar", lambda E, h4=h4: E.activation(
                NT_SCR[:, h4 * 128:(h4 + 1) * 128], ps_src[:, h4 * 128:(h4 + 1) * 128],
                ACT.Copy, scale=stat(RKC0 + h4)),
                inc=(SSC, 1) if h4 == nheads - 1 else None)
        ap_tok = rec.now(SSC)
        rec.wait("vector", ap_tok)
        nh = nheads
        nt3 = NT_SCR[:, 0:nh * 128].rearrange("p (h d) -> p h d", h=nh)
        x1 = nt3[:, :, 0:128:2]
        x2 = nt3[:, :, 1:128:2]
        cos3 = CSST[cs_slot][:, 0:nh * 64].rearrange("p (h d) -> p h d", h=nh)
        sin3 = CSST[cs_slot][:, 256:256 + nh * 64].rearrange("p (h d) -> p h d", h=nh)
        s1 = RO_S1[:, 0:nh * 64].rearrange("p (h d) -> p h d", h=nh)
        s2 = RO_S2[:, 0:nh * 64].rearrange("p (h d) -> p h d", h=nh)
        s3 = RO_S3[:, 0:nh * 64].rearrange("p (h d) -> p h d", h=nh)
        s4 = RO_S4[:, 0:nh * 64].rearrange("p (h d) -> p h d", h=nh)
        o3 = RPX[:, 0:nh * 128].rearrange("p (h d) -> p h d", h=nh)
        rec.wait("vector", free.get(("RP", rp_idx)))
        rec.op("vector", lambda E: E.tensor_mul(s1, x1, cos3))
        rec.op("vector", lambda E: E.tensor_mul(s2, x2, sin3))
        rec.op("vector", lambda E: E.tensor_mul(s3, x1, sin3))
        rec.op("vector", lambda E: E.tensor_mul(s4, x2, cos3))
        rec.op("vector", lambda E: E.tensor_sub(o3[:, :, 0:64], s1, s2))
        rec.op("vector", lambda E: E.tensor_add(o3[:, :, 64:128], s3, s4), inc=(SV, 1))
        return rec.now(SV)

    pend_rope = [None]

    def flush_rope_transposes():
        if pend_rope[0] is None:
            return
        rr_tok, rp_idx, nh, dest_fn = pend_rope[0]
        pend_rope[0] = None
        rec.wait("tensor", rr_tok)
        for h4 in range(nh):
            j = h4 % 2
            rec.wait("tensor", free.get(("PT", j)))
            rec.op("tensor", lambda E, h4=h4, j=j, rp=rp_idx: E.transpose(
                PT[j], RP2[rp][:, h4 * 128:(h4 + 1) * 128], ID16), inc=(STn, 1))
            tr_tok = rec.now(STn)
            rec.wait("vector", tr_tok)
            rec.op("vector", lambda E, h4=h4, j=j, d=dest_fn(h4): E.tensor_copy(
                d, PT[j]), inc=(SV, 1))
            free[("PT", j)] = rec.now(SV)
        free[("RP", rp_idx)] = (STn, STn.n)

    def pa_stage2(tt, xn_tok, lo_tok, cs_tok):
        if tt < HT:
            dest = (lambda dc, tt=tt: XN[:, dc, tt * 128:(tt + 1) * 128])
            guard_key = None
        else:
            dest = (lambda dc: XNT_TMP[:, dc, :])
            guard_key = ("XNT",)
        emit_xt_stage(tt, dest, do_router=True, do_lo=True, xn_tok=xn_tok,
                      lo_tok=lo_tok, evac_guard_key=guard_key)
        cb = tt % 2
        kb, vb = (0, 1) if tt % 2 == 0 else (2, 3)
        rec.wait("tensor", wkv_tok)
        rec.wait("tensor", free.get(("PS", kb)))
        for dc in range(DC):
            rec.op("tensor", lambda E, d=dest(dc), dc=dc, kb=kb: E.matmul(
                PS[kb][:], d, WKST[:, dc, :], start=(dc == 0), stop=(dc == DC - 1)),
                inc=(STn, 1) if dc == DC - 1 else None)
        k_tok = rec.now(STn)
        rec.wait("tensor", free.get(("PS", vb)))
        for dc in range(DC):
            rec.op("tensor", lambda E, d=dest(dc), dc=dc, vb=vb: E.matmul(
                PS[vb][:], d, WVST[:, dc, :], start=(dc == 0), stop=(dc == DC - 1)),
                inc=(STn, 1) if dc == DC - 1 else None)
        v_tok = rec.now(STn)
        free[("XNT",)] = v_tok
        flush_rope_transposes()

        rec.wait("scalar", k_tok)
        rec.wait("vector", cs_tok)
        rr_tok = emit_rms_rope(PS[kb], KV, cb, rp_idx=tt % 2)
        free[("CSST", cb)] = rr_tok
        free[("PS", kb)] = rr_tok
        pend_rope[0] = (rr_tok, tt % 2, KV,
                        (lambda h4, tt=tt: KT[:, h4, tt * 128:(tt + 1) * 128]))

        rec.wait("vector", v_tok)
        rec.op("vector", lambda E, tt=tt, vb=vb: E.tensor_copy(
            VT[:, :, tt, 0:128],
            PS[vb][:].rearrange("p (kv d) -> p kv d", kv=KV)), inc=(SV, 1))
        free[("PS", vb)] = rec.now(SV)

    pend = None
    for tt in range(NT):
        toks = emit_ln_stage(tt, do_lo=True, dup_load=(tt == 0), xstg4=True)
        if pend is not None:
            pa_stage2(*pend)
        pend = (tt, toks[0], toks[1], toks[2])
    pa_stage2(*pend)
    flush_rope_transposes()
    free[("WST", 0)] = free.get(("XSTG", 2))
    if free.get(("XSTG", 3)) is not None:
        free[("WST", 0)] = max(free[("WST", 0)], free[("XSTG", 3)],
                               key=lambda t: t[1])

    def emit_q_half(half, xn_ready):
        for hq in range(4):
            wb = hq % 2
            rec.wait("gpsimd", free.get(("WST", wb)))
            rec.op("gpsimd", lambda E, hq=hq, wb=wb: E.dma_start(
                out=WST[wb],
                in_=wqT_d[:, hq * 512:(hq + 1) * 512].rearrange("(dc p) n -> p dc n", p=128)),
                inc=(SW, 16))
            wq_tok = rec.now(SW)
            for ti in range(HT):
                tt = half * HT + ti
                cb = ti % 2
                rec.wait("sync", free.get(("CSST", cb)))
                rec.op("sync", lambda E, tt=tt, cb=cb: E.dma_start(
                    out=CSST[cb], in_=cs_d[tt * 128:(tt + 1) * 128, :]), inc=(SCS, 16))
                cs_tok = rec.now(SCS)
                qb = ti % 2
                rec.wait("tensor", wq_tok)
                rec.wait("tensor", xn_ready)
                rec.wait("tensor", free.get(("PS", qb)))
                for dc in range(DC):
                    rec.op("tensor", lambda E, dc=dc, ti=ti, wb=wb, qb=qb: E.matmul(
                        PS[qb][:], XN[:, dc, ti * 128:(ti + 1) * 128], WST[wb][:, dc, :],
                        start=(dc == 0), stop=(dc == DC - 1)),
                        inc=(STn, 1) if dc == DC - 1 else None)
                q_tok = rec.now(STn)
                flush_rope_transposes()
                rec.wait("scalar", q_tok)
                rec.wait("vector", cs_tok)
                rr_tok = emit_rms_rope(PS[qb], 4, cb, rp_idx=ti % 2)
                free[("CSST", cb)] = rr_tok
                free[("PS", qb)] = rr_tok
                if ti == HT - 1:
                    free[("WST", wb)] = rr_tok
                pend_rope[0] = (rr_tok, ti % 2, 4,
                                (lambda h4, hq=hq, ti=ti: QT[:, hq * 4 + h4,
                                                             ti * 128:(ti + 1) * 128]))
        flush_rope_transposes()
        return rec.now(SV)

    def vgap():
        rec.op("vector", lambda E: E.tensor_copy(STAT[:, 31:32], STAT[:, 30:31]))

    def emit_router_weights():
        for tt in range(NT):
            rec.op("vector", lambda E, tt=tt: E.reduce_max(
                M1A[:, tt:tt + 1], LGT[:, tt, :], axis=AX.X),
                inc=(SV, 1) if tt == NT - 1 else None)
        rec.wait("vector", rec.now(SV))
        for tt in range(NT):
            rec.op("vector", lambda E, tt=tt: E.tensor_scalar(
                S8A[:, tt, :], LGT[:, tt, :], M1A[:, tt:tt + 1], None, op0=ALU.is_equal))
        for tt in range(NT):
            rec.op("vector", lambda E, tt=tt: E.scalar_tensor_tensor(
                S8B[:, tt, :], S8A[:, tt, :], -1e30, LGT[:, tt, :],
                op0=ALU.mult, op1=ALU.add))
        for tt in range(NT):
            rec.op("vector", lambda E, tt=tt: E.reduce_max(
                M2A[:, tt:tt + 1], S8B[:, tt, :], axis=AX.X),
                inc=(SV, 1) if tt == NT - 1 else None)
        rec.wait("vector", rec.now(SV))
        rec.op("vector", lambda E: E.tensor_sub(DMAr[:], M2A[:], M1A[:]), inc=(SV, 1))
        dm_tok = rec.now(SV)
        rec.wait("scalar", dm_tok)
        rec.op("scalar", lambda E: E.activation(EWA[:], DMAr[:], ACT.Exp), inc=(SSC, 1))
        ew_tok = rec.now(SSC)
        rec.wait("vector", ew_tok)
        rec.op("vector", lambda E: E.tensor_scalar(EW2A[:], EWA[:], 1.0, None, op0=ALU.add))
        vgap()
        rec.op("vector", lambda E: E.reciprocal(W1A[:], EW2A[:]), inc=(SV, 1))
        rec.wait("vector", rec.now(SV))
        rec.op("vector", lambda E: E.tensor_scalar(
            W2A[:], W1A[:], -1.0, 1.0, op0=ALU.mult, op1=ALU.add), inc=(SV, 1))
        w2_tok = rec.now(SV)
        for tt in range(NT):
            rec.op("vector", lambda E, tt=tt: E.tensor_scalar_mul(
                S8B[:, tt, :], S8A[:, tt, :], W1A[:, tt:tt + 1]))
        for tt in range(NT):
            rec.op("vector", lambda E, tt=tt: E.scalar_tensor_tensor(
                S8C[:, tt, :], S8A[:, tt, :], -1e30, LGT[:, tt, :],
                op0=ALU.mult, op1=ALU.add))
        for tt in range(NT):
            rec.op("vector", lambda E, tt=tt: E.tensor_scalar(
                S8C[:, tt, :], S8C[:, tt, :], M2A[:, tt:tt + 1], None, op0=ALU.is_equal))
        rec.wait("vector", w2_tok)
        for tt in range(NT):
            rec.op("vector", lambda E, tt=tt: E.scalar_tensor_tensor(
                WRT[:, tt, :], S8C[:, tt, :], W2A[:, tt:tt + 1], S8B[:, tt, :],
                op0=ALU.mult, op1=ALU.add),
                inc=(SV, 1) if tt == NT - 1 else None)

    def emit_attn_half(half, ready_tok):
        rec.wait("tensor", ready_tok)
        rec.wait("vector", rec.now(STn))
        rec.wait("vector", consts_tok)
        for h in range(H):
            kv = h // 4
            for cr in range(2):
                cq = half * 2 + cr
                kclo = max(0, 4 * cq - 8)
                kchi = 4 * cq + 3
                kcs = list(range(kclo, kchi + 1))
                pv_first = [True] * 4

                def emit_S(i, kc):
                    sb_ = i % 2
                    rec.wait("tensor", free.get(("PS", 4 + sb_)))
                    rec.op("tensor", lambda E, kc=kc, sb_=sb_, kv=kv, h=h, cr=cr: E.matmul(
                        PS[4 + sb_][:], KT[:, kv, kc * 128:(kc + 1) * 128],
                        QT[:, h, cr * 512:(cr + 1) * 512], start=True, stop=True),
                        inc=(STn, 1))
                    s_tok = rec.now(STn)
                    rec.wait("scalar", s_tok)
                    rec.wait("scalar", free.get(("ATB", sb_)))
                    rec.op("scalar", lambda E, sb_=sb_: E.activation(
                        ATB[sb_][:], PS[4 + sb_][:], ACT.Exp, scale=SCALE), inc=(SSC, 1))
                    exp_tok = rec.now(SSC)
                    free[("PS", 4 + sb_)] = exp_tok
                    a_ready = exp_tok
                    causal_qr = kc - 4 * cq if 4 * cq <= kc <= 4 * cq + 3 else None
                    drop_row0 = (kc == 4 * cq - 8)
                    if causal_qr is not None or drop_row0:
                        rec.wait("vector", exp_tok)
                        if causal_qr is not None:
                            rec.op("vector", lambda E, sb_=sb_, qr=causal_qr: E.tensor_mul(
                                ATB[sb_][:, qr * 128:(qr + 1) * 128],
                                ATB[sb_][:, qr * 128:(qr + 1) * 128], MASK_C),
                                inc=None if drop_row0 else (SV, 1))
                        if drop_row0:
                            rec.op("vector", lambda E, sb_=sb_: E.memset(
                                ATB[sb_][0:1, :], 0.0), inc=(SV, 1))
                        a_ready = rec.now(SV)
                    return a_ready

                def emit_PV(i, kc, a_ready):
                    sb_ = i % 2
                    rec.wait("tensor", a_ready)
                    qs_lo = max(0, kc - 4 * cq)
                    valid_qrs = list(range(qs_lo, 4))
                    for vi, qr in enumerate(valid_qrs):
                        qs = 4 * cq + qr
                        is_stop = (kc == qs)
                        is_last = (vi == len(valid_qrs) - 1)
                        if pv_first[qr]:
                            rec.wait("tensor", free.get(("PS", qr)))
                        rec.op("tensor", lambda E, sb_=sb_, qr=qr, kv=kv, kc=kc,
                               f=pv_first[qr], st=is_stop: E.matmul(
                            PS[qr][:, 0:129], ATB[sb_][:, qr * 128:(qr + 1) * 128],
                            VT[:, kv, kc, :].squeeze(), start=f, stop=st),
                            inc=(STn, 1) if (is_stop or is_last) else None)
                        pv_first[qr] = False
                        if is_stop:
                            pv_tok = rec.now(STn)
                            ti = qs - half * HT
                            rec.wait("vector", pv_tok)
                            rec.op("vector", lambda E, qr=qr: E.reciprocal(
                                stat(RCOL), PS[qr][:, 128:129]), inc=(SV, 1))
                            rc_tok = rec.now(SV)
                            rec.wait("vector", rc_tok)
                            rec.op("vector", lambda E, qr=qr, ti=ti, h=h: E.tensor_scalar_mul(
                                AO[:, ti, h * 128:(h + 1) * 128], PS[qr][:, 0:128], stat(RCOL)),
                                inc=(SV, 1))
                            free[("PS", qr)] = rec.now(SV)
                    free[("ATB", sb_)] = (STn, STn.n)

                pend = None
                for i, kc in enumerate(kcs):
                    a_ready = emit_S(i, kc)
                    if pend is not None:
                        emit_PV(pend[0], pend[1], pend[2])
                    pend = (i, kc, a_ready)
                emit_PV(*pend)

    def emit_atT_half():
        rec.wait("tensor", rec.now(SV))
        pt4 = [PT[0][:], PT[1][:],
               PS[4].bitcast(BF16)[:, 0:128], PS[5].bitcast(BF16)[:, 0:128]]
        at_toks = {}
        for ti in range(HT):
            for hc in range(H):
                j = (ti * H + hc) % 4
                rec.wait("tensor", free.get(("PT4", j)))
                rec.op("tensor", lambda E, ti=ti, hc=hc, j=j: E.transpose(
                    pt4[j], AO[:, ti, hc * 128:(hc + 1) * 128], ID16), inc=(STn, 1))
                tr_tok = rec.now(STn)
                rec.wait("vector", tr_tok)
                rec.op("vector", lambda E, ti=ti, hc=hc, j=j: E.tensor_copy(
                    QT[:, hc, ti * 128:(ti + 1) * 128], pt4[j]), inc=(SV, 1))
                free[("PT4", j)] = rec.now(SV)
            at_toks[ti] = (STn, STn.n)
        free[("PS", 4)] = free[("PT4", 2)]
        free[("PS", 5)] = free[("PT4", 3)]
        free[("PT", 0)] = free[("PT4", 0)]
        free[("PT", 1)] = free[("PT4", 1)]
        return at_toks

    def emit_wo_half(at_toks):
        rec.wait("tensor", rec.now(SV))
        rec.wait("vector", rec.now(SST))
        for dq in range(4):
            wb = dq % 2
            if dq < 2:
                rec.wait("gpsimd", at_toks[3 if wb == 0 else 7])
            rec.wait("gpsimd", free.get(("WOST", wb)))
            rec.op("gpsimd", lambda E, dq=dq, wb=wb: E.dma_start(
                out=WOST[wb],
                in_=woT_d[:, dq * 512:(dq + 1) * 512].rearrange("(hc p) n -> p hc n", p=128)),
                inc=(SW, 16))
            wo_tok = rec.now(SW)
            rec.wait("tensor", wo_tok)
            for ti in range(HT):
                pb = ti % 2
                rec.wait("tensor", free.get(("PS", pb)))
                for hc in range(H):
                    rec.op("tensor", lambda E, hc=hc, ti=ti, wb=wb, pb=pb: E.matmul(
                        PS[pb][:], QT[:, hc, ti * 128:(ti + 1) * 128], WOST[wb][:, hc, :],
                        start=(hc == 0), stop=(hc == H - 1)),
                        inc=(STn, 1) if hc == H - 1 else None)
                y_tok = rec.now(STn)
                rec.wait("vector", y_tok)
                rec.op("vector", lambda E, ti=ti, dq=dq, pb=pb: E.tensor_copy(
                    ACC[:, ti, dq * 512:(dq + 1) * 512], PS[pb][:]), inc=(SV, 1))
                free[("PS", pb)] = rec.now(SV)
                if ti == HT - 1:
                    free[("WOST", wb)] = (STn, STn.n)

    def emit_ffn_half(half, tail_cb=None):
        gv_done = [None]
        t_now = rec.now(STn)
        rec.wait("vector", t_now)
        rec.wait("sync", t_now)
        rec.wait("gpsimd", t_now)
        free[("W1ST", 0)] = t_now
        free[("W1ST", 1)] = t_now
        free[("W2ST", 0)] = t_now
        free[("W2ST", 1)] = t_now
        for ti in range(HT):
            tt = half * HT + ti
            b = ti % 2
            rec.wait("sync", free.get(("FXSTG", b)))
            rec.op("sync", lambda E, tt=tt, b=b: E.dma_start(
                out=FXSTG[b], in_=x_d[tt * 128:(tt + 1) * 128, :]), inc=(SX, 16))
            if ti == 0:
                rec.op("sync", lambda E, tt=tt, b=b: E.dma_start(
                    out=FXSTG[b], in_=x_d[tt * 128:(tt + 1) * 128, :]), inc=(SX, 16))
            x_tok = rec.now(SX)
            rec.wait("vector", x_tok)
            rec.op("vector", lambda E, ti=ti, b=b: E.tensor_tensor(
                ACC[:, ti, :], ACC[:, ti, :], FXSTG[b], ALU.add), inc=(SV, 1))
            free[("FXSTG", b)] = rec.now(SV)

        for e in range(NE):
            for fq in range(4):
                wb = fq % 2
                rec.wait("gpsimd", free.get(("W1ST", wb)))
                rec.op("gpsimd", lambda E, e=e, fq=fq, wb=wb: E.dma_start(
                    out=W1ST[wb],
                    in_=w1_d[e, :, fq * 256:(fq + 1) * 256].rearrange("(dc p) f -> p dc f", p=128)),
                    inc=(SW, 16))
                rec.wait("gpsimd", free.get(("W2ST", wb)))
                rec.op("gpsimd", lambda E, e=e, fq=fq, wb=wb: E.dma_start(
                    out=W2ST[wb],
                    in_=w2_d[e, :, fq * 256:(fq + 1) * 256].rearrange("(dc p) f -> p dc f", p=128)),
                    inc=(SW, 16))
                w12_tok = rec.now(SW)
                rec.wait("tensor", w12_tok)
                for fcl in range(2):
                    fc = fq * 2 + fcl
                    for tq in range(2):
                        rec.wait("tensor", free.get(("PS", 2)))
                        for dc in range(DC):
                            rec.op("tensor", lambda E, dc=dc, fcl=fcl, tq=tq, wb=wb: E.matmul(
                                PS[2][:], W1ST[wb][:, dc, fcl * 128:(fcl + 1) * 128],
                                XN[:, dc, tq * 512:(tq + 1) * 512],
                                start=(dc == 0), stop=(dc == DC - 1)),
                                inc=(STn, 1) if dc == DC - 1 else None)
                        g_tok = rec.now(STn)
                        rec.wait("tensor", free.get(("PS", 3)))
                        for dc in range(DC):
                            rec.op("tensor", lambda E, dc=dc, fcl=fcl, tq=tq, wb=wb: E.matmul(
                                PS[3][:], W2ST[wb][:, dc, fcl * 128:(fcl + 1) * 128],
                                XN[:, dc, tq * 512:(tq + 1) * 512],
                                start=(dc == 0), stop=(dc == DC - 1)),
                                inc=(STn, 1) if dc == DC - 1 else None)
                        vv_tok = rec.now(STn)
                        rec.wait("scalar", g_tok)
                        rec.wait("scalar", free.get(("RP", 0)))
                        rec.op("scalar", lambda E: E.activation(
                            RP_SCR[:], PS[2][:], ACT.Silu), inc=(SSC, 1))
                        si_tok = rec.now(SSC)
                        free[("PS", 2)] = si_tok
                        if e == NE - 1 and fq == 3 and fcl == 1 and tq == 1:
                            gv_done[0] = rec.now(STn)
                        rec.wait("vector", si_tok)
                        rec.wait("vector", vv_tok)
                        rec.wait("vector", free.get(("HBUF",)))
                        rec.op("vector", lambda E, fc=fc, tq=tq: E.tensor_tensor(
                            HBUF[:, fc, tq * 512:(tq + 1) * 512], RP_SCR[:], PS[3][:],
                            ALU.mult), inc=(SV, 1))
                        free[("PS", 3)] = rec.now(SV)
                        free[("RP", 0)] = rec.now(SV)
                        if fcl == 1 and tq == 1:
                            free[("W1ST", wb)] = rec.now(SV)
                            free[("W2ST", wb)] = rec.now(SV)
            h_tok = rec.now(SV)
            if e == NE - 1 and tail_cb is not None:
                tail_cb()
            for dq in range(4):
                wb = dq % 2
                rec.wait("gpsimd", free.get(("W3ST", wb)))
                rec.op("gpsimd", lambda E, e=e, dq=dq, wb=wb: E.dma_start(
                    out=W3ST[wb],
                    in_=w3_d[e, :, dq * 512:(dq + 1) * 512].rearrange("(fc p) d -> p fc d", p=128)),
                    inc=(SW, 16))
                w3_tok = rec.now(SW)
                rec.wait("tensor", w3_tok)
                rec.wait("tensor", h_tok)
                for ti in range(HT):
                    tt = half * HT + ti
                    pb = ti % 2
                    rec.wait("tensor", free.get(("PS", pb)))
                    for fc in range(8):
                        rec.op("tensor", lambda E, fc=fc, ti=ti, wb=wb, pb=pb: E.matmul(
                            PS[pb][:], HBUF[:, fc, ti * 128:(ti + 1) * 128], W3ST[wb][:, fc, :],
                            start=(fc == 0), stop=(fc == 7)),
                            inc=(STn, 1) if fc == 7 else None)
                    o_tok = rec.now(STn)
                    if dq == 3 and ti == HT - 1:
                        free[("HBUF",)] = (STn, STn.n)
                    rec.wait("vector", o_tok)
                    rec.op("vector", lambda E, ti=ti, dq=dq, pb=pb, tt=tt, e=e:
                           E.scalar_tensor_tensor(
                               ACC[:, ti, dq * 512:(dq + 1) * 512], PS[pb][:],
                               WRT[:, tt, e:e + 1], ACC[:, ti, dq * 512:(dq + 1) * 512],
                               op0=ALU.mult, op1=ALU.add), inc=(SV, 1))
                    free[("PS", pb)] = rec.now(SV)
                    if ti == HT - 1:
                        free[("W3ST", wb)] = rec.now(SV)
        acc_tok = rec.now(SV)
        rec.wait("sync", acc_tok)
        for ti in range(HT):
            tt = half * HT + ti
            rec.op("sync", lambda E, tt=tt, ti=ti: E.dma_start(
                out=out_d[tt * 128:(tt + 1) * 128, :], in_=ACC[:, ti, :]), inc=(SST, 16))
        st_tok = rec.now(SST)
        for k in (("WST", 0), ("WST", 1), ("XSTG", 0), ("XSTG", 1),
                  ("CSST", 0), ("CSST", 1)):
            free[k] = st_tok
        return gv_done[0]

    def dbg_dump(key, src):
        if not debug:
            return
        tok = rec.now(SV)
        rec.wait("sync", tok)
        rec.op("sync", lambda E, key=key, src=src: E.dma_start(out=dbg[key], in_=src),
               inc=(SST, 16))

    pa_tok = rec.now(SV)
    emit_q_half(0, pa_tok)
    q1_tok = rec.now(SV)
    emit_attn_half(0, q1_tok)
    at_toks = emit_atT_half()
    emit_router_weights()
    emit_wo_half(at_toks)
    if debug:
        mid_tok = rec.now(SV)
        rec.wait("sync", mid_tok)
        rec.op("sync", lambda E: E.dma_start(out=dbg["xnT"], in_=XN[:]), inc=(SST, 16))
        rec.op("sync", lambda E: E.dma_start(out=dbg["y"], in_=ACC[:]), inc=(SST, 16))
        rec.op("sync", lambda E: E.dma_start(out=dbg["qT"], in_=QT[:]), inc=(SST, 16))
        dump_done = rec.now(SST)
        rec.wait("vector", dump_done)
        rec.wait("tensor", dump_done)
        rec.wait("scalar", dump_done)
        rec.wait("gpsimd", dump_done)
    gv_tok = emit_ffn_half(0)

    rec.wait("vector", gv_tok)
    rec.wait("scalar", gv_tok)
    rec.wait("sync", gv_tok)
    rbufs = (XSTG_R, [SQSCR_R], XNTOK_R, "_R")
    pend = None
    for ti in range(HT):
        tt = HT + ti
        toks = emit_ln_stage(tt, do_lo=False, dup_load=(ti == 0), load_cs=False,
                             bufs=rbufs)
        if pend is not None:
            emit_xt_stage(pend[0], pend[1], do_router=False, do_lo=False,
                          xn_tok=pend[2], hi_src=XNTOK_R[pend[0] % 2], key_sfx="_R")
        pend = (tt, (lambda dc, ti=ti: XN[:, dc, ti * 128:(ti + 1) * 128]), toks[0])
    emit_xt_stage(pend[0], pend[1], do_router=False, do_lo=False,
                  xn_tok=pend[2], hi_src=XNTOK_R[pend[0] % 2], key_sfx="_R")
    xn2_tok = rec.now(SV)
    emit_q_half(1, xn2_tok)
    emit_attn_half(1, rec.now(SV))
    at_toks = emit_atT_half()
    emit_wo_half(at_toks)
    emit_ffn_half(1)

    if debug:
        end_tok = rec.now(SV)
        rec.wait("sync", end_tok)
        rec.wait("sync", rec.now(SST))
        for key, src in (("kT", KT[:]), ("vT", VT[:]), ("W", WRT[:]),
                         ("lg", LGT[:]), ("ao", AO[:])):
            rec.op("sync", lambda E, key=key, src=src: E.dma_start(out=dbg[key], in_=src),
                   inc=(SST, 16))

    for s_ in (SST, SX, SCS, SW):
        rec.wait("gpsimd", rec.now(s_))

    with nc.Block() as block:
        def mk(eng_name):
            def body(E):
                for f in rec.ops[eng_name]:
                    f(E)
            return body
        block.sync(mk("sync"))
        block.tensor(mk("tensor"))
        block.vector(mk("vector"))
        block.scalar(mk("scalar"))
        block.gpsimd(mk("gpsimd"))

    counts = {k: len(v) for k, v in rec.ops.items()}
    print("bass program op counts:", counts, "total:", sum(counts.values()))
    return nc


def _host_prep(inputs):
    x = np.ascontiguousarray(np.asarray(inputs["L0_attn_x"], dtype=np.float32))
    wqT = np.ascontiguousarray(np.asarray(inputs["L0_attn_wq"], dtype=np.float32).T)
    wkT = np.ascontiguousarray(np.asarray(inputs["L0_attn_wk"], dtype=np.float32).T)
    wvT = np.ascontiguousarray(np.asarray(inputs["L0_attn_wv"], dtype=np.float32).T)
    woT = np.ascontiguousarray(np.asarray(inputs["L0_attn_wo"], dtype=np.float32).T)
    rwT = np.ascontiguousarray(np.asarray(inputs["L0_ffn_router_w"], dtype=np.float32).T)
    w1 = np.ascontiguousarray(np.asarray(inputs["L0_ffn_w1"], dtype=np.float32))
    w2 = np.ascontiguousarray(np.asarray(inputs["L0_ffn_w2"], dtype=np.float32))
    w3 = np.ascontiguousarray(np.asarray(inputs["L0_ffn_w3"], dtype=np.float32))

    inv_freq = 10000.0 ** (-(np.arange(0, DH, 2, dtype=np.float64) / DH))
    ang = np.outer(np.arange(T, dtype=np.float64), inv_freq)
    cos4 = np.tile(np.cos(ang).astype(np.float32), (1, 4))
    sin4 = np.tile(np.sin(ang).astype(np.float32), (1, 4))
    cs_tab = np.ascontiguousarray(np.concatenate([cos4, sin4], axis=1))

    ident = np.eye(128, dtype=ml_dtypes.bfloat16)
    kk = np.arange(128)[:, None]
    qq = np.arange(128)[None, :]
    mask_c = (kk <= qq).astype(ml_dtypes.bfloat16)
    mask_w = (kk > qq).astype(ml_dtypes.bfloat16)
    consts = np.ascontiguousarray(np.concatenate([ident, mask_c, mask_w], axis=1))

    shared = dict(wqT=wqT, wkT=wkT, wvT=wvT, woT=woT, rwT=rwT,
                  w1=w1, w2=w2, w3=w3, cs_tab=cs_tab, consts=consts)
    in_maps = []
    for c in range(8):
        m = dict(shared)
        m["x"] = np.ascontiguousarray(x[c])
        in_maps.append(m)
    return in_maps


def kernel(**inputs):
    in_maps = _host_prep(inputs)
    nc = build_nc(debug=DEBUG)
    res = run_bass_kernel_spmd(nc, in_maps, core_ids=list(range(8)))
    out = np.stack([np.asarray(r["out"]) for r in res.results], axis=0)
    if DEBUG:
        kernel.last_results = res.results
    return out


# version 14
# speedup vs baseline: 1.4848x; 1.0475x over previous
import math
import numpy as np
import ml_dtypes
import concourse.bass as bass
import concourse.mybir as mybir
from concourse.bass_utils import run_bass_kernel_spmd

F32 = mybir.dt.float32
BF16 = mybir.dt.bfloat16
AX = mybir.AxisListType
ALU = mybir.AluOpType
ACT = mybir.ActivationFunctionType

T = 2048
D = 2048
H = 16
KV = 4
DH = 128
FF = 1024
NE = 8
NT = 16
DC = 16
SCALE = 1.0 / math.sqrt(DH)
LN_EPS = 1e-5
RMS_EPS = 1e-6
HALF_T = 1024
HT = 8

DEBUG = False

ENGS = ("sync", "tensor", "vector", "scalar", "gpsimd")


class Sem:
    def __init__(self, handle, name):
        self.h = handle
        self.name = name
        self.n = 0


class Rec:
    def __init__(self):
        self.ops = {k: [] for k in ENGS}
        self._last_wait = {}

    def op(self, eng, fn, inc=None):
        if inc is not None:
            sem, k = inc
            sem.n += k

            def run(E, fn=fn, h=sem.h, k=k):
                fn(E).then_inc(h, k)
            self.ops[eng].append(run)
            return (sem, sem.n)
        else:
            self.ops[eng].append(lambda E, fn=fn: fn(E))
            return None

    def wait(self, eng, token):
        if token is None:
            return
        sem, v = token
        if v <= 0:
            return
        key = (eng, sem.name)
        if self._last_wait.get(key, -1) >= v:
            return
        self._last_wait[key] = v
        self.ops[eng].append(lambda E, h=sem.h, v=v: E.wait_ge(h, v))

    def now(self, sem):
        return (sem, sem.n)


def build_nc(debug=False):
    nc = bass.Bass("TRN2", target_bir_lowering=False)

    x_d = nc.dram_tensor("x", [T, D], F32, kind="ExternalInput").ap()
    wqT_d = nc.dram_tensor("wqT", [D, H * DH], F32, kind="ExternalInput").ap()
    wkT_d = nc.dram_tensor("wkT", [D, KV * DH], F32, kind="ExternalInput").ap()
    wvT_d = nc.dram_tensor("wvT", [D, KV * DH], F32, kind="ExternalInput").ap()
    woT_d = nc.dram_tensor("woT", [H * DH, D], F32, kind="ExternalInput").ap()
    rwT_d = nc.dram_tensor("rwT", [D, NE], F32, kind="ExternalInput").ap()
    w1_d = nc.dram_tensor("w1", [NE, D, FF], F32, kind="ExternalInput").ap()
    w2_d = nc.dram_tensor("w2", [NE, D, FF], F32, kind="ExternalInput").ap()
    w3_d = nc.dram_tensor("w3", [NE, FF, D], F32, kind="ExternalInput").ap()
    cs_d = nc.dram_tensor("cs_tab", [T, 512], F32, kind="ExternalInput").ap()
    const_d = nc.dram_tensor("consts", [128, 3 * 128], BF16, kind="ExternalInput").ap()

    out_d = nc.dram_tensor("out", [T, D], F32, kind="ExternalOutput").ap()

    dbg = {}
    if debug:
        dbg["xnT"] = nc.dram_tensor("dbg_xnT", [128, DC, HALF_T], BF16, kind="ExternalOutput").ap()
        dbg["kT"] = nc.dram_tensor("dbg_kT", [128, KV, T], BF16, kind="ExternalOutput").ap()
        dbg["vT"] = nc.dram_tensor("dbg_vT", [128, KV, NT, 129], BF16, kind="ExternalOutput").ap()
        dbg["lg"] = nc.dram_tensor("dbg_lg", [128, NT, NE], F32, kind="ExternalOutput").ap()
        dbg["W"] = nc.dram_tensor("dbg_W", [128, NT, NE], F32, kind="ExternalOutput").ap()
        dbg["qT"] = nc.dram_tensor("dbg_qT", [128, H, HALF_T], BF16, kind="ExternalOutput").ap()
        dbg["ao"] = nc.dram_tensor("dbg_ao", [128, HT, D], BF16, kind="ExternalOutput").ap()
        dbg["y"] = nc.dram_tensor("dbg_y", [128, HT, D], F32, kind="ExternalOutput").ap()

    ARENA_SIZE = 212544
    arena = nc.alloc_sbuf_tensor("ARENA", [128, ARENA_SIZE], mybir.dt.uint8)
    ARENA_BASE = nc.lookup_mloc(arena).addr

    def sb(name, shape, dtype, off):
        return nc.alloc_sbuf_tensor_at(name, shape, dtype, offset=ARENA_BASE + off).ap()

    XN = sb("XN", [128, DC, HALF_T], BF16, 0)
    QT = sb("QT", [128, H, HALF_T], BF16, 32768)
    SQSCR = sb("SQSCR", [128, D], F32, 32768)
    XLO_TMP = sb("XLO_TMP", [128, DC, 128], BF16, 40960)
    SQSCR_B = sb("SQSCR_B", [128, D], F32, 49152)
    XNTOK_B = sb("XNTOK_B", [128, D], BF16, 57344)
    XNLO_B = sb("XNLO_B", [128, D], BF16, 61440)
    HBUF = sb("HBUF", [128, 8, HALF_T], BF16, 32768)
    FXSTG = [sb("FXSTG0", [128, D], F32, 32768),
             sb("FXSTG1", [128, D], F32, 40960)]
    W3ST = [sb("W3ST0", [128, 8, 512], BF16, 49152),
            sb("W3ST1", [128, 8, 512], BF16, 57344)]

    AO = sb("AO", [128, HT, D], BF16, 65536)
    WKST = sb("WKST", [128, DC, 512], BF16, 65536)
    WVST = sb("WVST", [128, DC, 512], BF16, 81920)
    W1ST = [sb("W1ST0", [128, DC, 256], BF16, 65536),
            sb("W1ST1", [128, DC, 256], BF16, 73728)]
    W2ST = [sb("W2ST0", [128, DC, 256], BF16, 81920),
            sb("W2ST1", [128, DC, 256], BF16, 90112)]
    WOST = [sb("WOST0", [128, DC, 512], BF16, 65536),
            sb("WOST1", [128, DC, 512], BF16, 81920)]
    SQSCR_R = sb("SQSCR_R", [128, D], F32, 65536)
    XNTOK_R = [sb("XNTOK_R0", [128, D], BF16, 73728),
               sb("XNTOK_R1", [128, D], BF16, 77824)]
    XSTG_R = [sb("XSTG_R0", [128, D], F32, 81920),
              sb("XSTG_R1", [128, D], F32, 90112)]

    ACC = sb("ACC", [128, HT, D], F32, 98304)
    WST = [sb("WST0", [128, DC, 512], BF16, 98304),
           sb("WST1", [128, DC, 512], BF16, 114688)]
    XSTG = [sb("XSTG0", [128, D], F32, 131072),
            sb("XSTG1", [128, D], F32, 139264),
            sb("XSTG2", [128, D], F32, 98304),
            sb("XSTG3", [128, D], F32, 106496)]
    CSST = [sb("CSST0", [128, 512], F32, 147456),
            sb("CSST1", [128, 512], F32, 149504)]
    XNTOK = sb("XNTOK", [128, D], BF16, 151552)
    XNLO = sb("XNLO", [128, D], BF16, 155648)
    XNT_TMP = sb("XNT_TMP", [128, DC, 128], BF16, 159744)

    KT = sb("KT", [128, KV, T], BF16, 163840)
    VT = sb("VT", [128, KV, NT, 129], BF16, 180224)
    ID16 = sb("ID16", [128, 128], BF16, 196736)
    MASK_C = sb("MASK_C", [128, 128], BF16, 196992)
    MASK_W = sb("MASK_W", [128, 128], BF16, 197248)
    RWTF = sb("RWTF", [128, DC, NE], F32, 197504)
    RWT_HI = sb("RWT_HI", [128, DC, NE], BF16, 198016)
    RWT_LO = sb("RWT_LO", [128, DC, NE], BF16, 198272)
    LGT = sb("LGT", [128, NT, NE], F32, 198528)
    WRT = sb("WRT", [128, NT, NE], F32, 199040)
    STAT = sb("STAT", [128, 32], F32, 199552)
    ST8A = sb("ST8A", [128, NE], F32, 199680)
    ST8B = sb("ST8B", [128, NE], F32, 199712)
    ATB = [sb("ATB0", [128, 512], BF16, 199744),
           sb("ATB1", [128, 512], BF16, 200768)]
    M1A = sb("M1A", [128, NT], F32, 209504)
    M2A = sb("M2A", [128, NT], F32, 209568)
    DMAr = sb("DMAr", [128, NT], F32, 209632)
    EWA = sb("EWA", [128, NT], F32, 209696)
    EW2A = sb("EW2A", [128, NT], F32, 209760)
    W1A = sb("W1A", [128, NT], F32, 209824)
    W2A = sb("W2A", [128, NT], F32, 209888)
    S8A = sb("S8A", [128, NT, NE], F32, 209952)
    S8B = sb("S8B", [128, NT, NE], F32, 210464)
    S8C = sb("S8C", [128, NT, NE], F32, 210976)
    NT_SCR = sb("NT_SCR", [128, 512], F32, 201792)
    RO_S1 = sb("RO_S1", [128, 256], F32, 203840)
    RO_S2 = sb("RO_S2", [128, 256], F32, 204864)
    RP_SCR = sb("RP_SCR", [128, 512], BF16, 205888)
    RP_B = sb("RP_B", [128, 512], BF16, 211488)
    RMS_SQ = sb("RMS_SQ", [128, 128], F32, 206912)
    RO_S3 = sb("RO_S3", [128, 256], F32, 207424)
    RO_S4 = sb("RO_S4", [128, 256], F32, 208448)
    ST8C = sb("ST8C", [128, NE], F32, 209472)

    PS = [nc.alloc_psum_tensor(f"PS{i}", [128, 512], F32).ap() for i in range(6)]
    PT = [nc.alloc_psum_tensor(f"PT{i}", [128, 128], BF16).ap() for i in range(2)]

    rec = Rec()
    sems = {}
    for name in ("X", "CS", "W", "ST", "T", "V", "SC", "G"):
        sems[name] = Sem(nc.alloc_semaphore(name=f"S_{name}"), name)
    (SX, SCS, SW, SST, STn, SV, SSC, SG) = (
        sems[k] for k in ("X", "CS", "W", "ST", "T", "V", "SC", "G"))

    nc.all_engine_barrier()
    sem_nums = [s.h.num for s in sems.values()]
    nc.gpsimd.dma_reset(range(min(sem_nums), max(sem_nums) + 1))
    for s in sems.values():
        nc.gpsimd.sem_clear(s.h)
    nc.all_engine_barrier()
    warm_a = nc.alloc_semaphore(name="S_WARMA")
    warm_b = nc.alloc_semaphore(name="S_WARMB")
    nc.sync.dma_start(out=XSTG[0], in_=x_d[0:128, :]).then_inc(warm_a, 16)
    nc.gpsimd.dma_start(out=XSTG[1], in_=x_d[128:256, :]).then_inc(warm_b, 16)
    nc.gpsimd.wait_ge(warm_a, 16)
    nc.gpsimd.wait_ge(warm_b, 16)
    for s in sems.values():
        nc.gpsimd.sem_clear(s.h)
    nc.all_engine_barrier()

    def stat(i):
        return STAT[:, i:i + 1]

    MVc, EX2, M2C, NEGM = 0, 1, 2, 3
    VARV, RSTD, RCOL = 4, 5, 6
    KRMS0 = 8
    SUMS2 = [12, 13]
    RKC0 = 16
    W1CC, M1C, M2CC, DMC = 20, 21, 22, 23
    STDV, EWC = 25, 26
    SQc2 = [24, 27]
    W2CC, EW2 = 28, 29

    free = {}

    for _rep in range(2):
        rec.op("sync", lambda E: E.dma_start(out=ID16, in_=const_d[:, 0:128]), inc=(SCS, 16))
        rec.op("sync", lambda E: E.dma_start(out=MASK_C, in_=const_d[:, 128:256]), inc=(SCS, 16))
        rec.op("sync", lambda E: E.dma_start(out=MASK_W, in_=const_d[:, 256:384]), inc=(SCS, 16))
        rec.op("sync", lambda E: E.dma_start(
            out=RWTF, in_=rwT_d.rearrange("(dc p) e -> p dc e", p=128)), inc=(SCS, 16))
    consts_tok = rec.now(SCS)
    rec.wait("vector", consts_tok)
    rec.op("vector", lambda E: E.tensor_copy(RWT_HI, RWTF), inc=(SV, 1))
    rec.wait("vector", rec.now(SV))
    rec.op("vector", lambda E: E.tensor_tensor(RWT_LO, RWTF, RWT_HI, ALU.subtract), inc=(SV, 1))
    rwhl_tok = rec.now(SV)
    rec.op("vector", lambda E: E.memset(
        VT[:, :, :, 128:129].rearrange("p a b c -> p (a b c)"), 1.0), inc=(SV, 1))

    for _rep in range(2):
        rec.op("gpsimd", lambda E: E.dma_start(
            out=WKST, in_=wkT_d.rearrange("(dc p) n -> p dc n", p=128)), inc=(SW, 16))
        rec.op("gpsimd", lambda E: E.dma_start(
            out=WVST, in_=wvT_d.rearrange("(dc p) n -> p dc n", p=128)), inc=(SW, 16))
    wkv_tok = rec.now(SW)

    RP2 = [RP_SCR, RP_B]
    XNTOK2 = [XNTOK, XNTOK_B]
    XNLO2 = [XNLO, XNLO_B]
    SQSCR2 = [SQSCR, SQSCR_B]

    def emit_ln_stage(tt, do_lo, dup_load=False, load_cs=True, xstg4=False,
                      bufs=None):
        XSTGl, SQSCRl, XNTOKl, sfx = ((XSTG, SQSCR2, XNTOK2, "") if bufs is None
                                      else bufs)
        b = tt % 2
        bq = b if len(SQSCRl) > 1 else 0
        bx = tt % 4 if xstg4 else tt % len(XSTGl)
        rec.wait("sync", free.get(("XSTG" + sfx, bx)))
        rec.op("sync", lambda E, tt=tt, bx=bx, XSTGl=XSTGl: E.dma_start(
            out=XSTGl[bx], in_=x_d[tt * 128:(tt + 1) * 128, :]), inc=(SX, 16))
        if dup_load:
            rec.op("sync", lambda E, tt=tt, bx=bx, XSTGl=XSTGl: E.dma_start(
                out=XSTGl[bx], in_=x_d[tt * 128:(tt + 1) * 128, :]), inc=(SX, 16))
        x_tok = rec.now(SX)
        cs_tok = None
        if load_cs:
            rec.wait("sync", free.get(("CSST", b)))
            rec.op("sync", lambda E, tt=tt, b=b: E.dma_start(
                out=CSST[b], in_=cs_d[tt * 128:(tt + 1) * 128, :]), inc=(SCS, 16))
            if dup_load:
                rec.op("sync", lambda E, tt=tt, b=b: E.dma_start(
                    out=CSST[b], in_=cs_d[tt * 128:(tt + 1) * 128, :]), inc=(SCS, 16))
            cs_tok = rec.now(SCS)

        rec.wait("scalar", x_tok)
        rec.wait("scalar", free.get(("SQc" + sfx, bq)))
        rec.op("scalar", lambda E, bq=bq, bx=bx, SQSCRl=SQSCRl, XSTGl=XSTGl: E.activation(
            SQSCRl[bq], XSTGl[bx], ACT.Square, accum_out=stat(SQc2[bq])))
        rec.op("scalar", lambda E, bq=bq, bx=bx, SQSCRl=SQSCRl, XSTGl=XSTGl: E.activation(
            SQSCRl[bq], XSTGl[bx], ACT.Copy, accum_out=stat(SUMS2[bq])), inc=(SSC, 1))
        sq_tok = rec.now(SSC)
        rec.wait("vector", sq_tok)
        rec.op("vector", lambda E: E.tensor_copy(STAT[:, 31:32], STAT[:, 30:31]))
        rec.op("vector", lambda E, b=b: E.tensor_scalar(stat(MVc), stat(SUMS2[bq]), 1.0 / D, None, op0=ALU.mult))
        rec.op("vector", lambda E, b=b: E.tensor_scalar(
            stat(EX2), stat(SQc2[bq]), 1.0 / D, LN_EPS, op0=ALU.mult, op1=ALU.add))
        rec.op("vector", lambda E: E.tensor_mul(stat(M2C), stat(MVc), stat(MVc)))
        rec.op("vector", lambda E: E.tensor_scalar(stat(NEGM), stat(MVc), -1.0, None, op0=ALU.mult))
        rec.op("vector", lambda E: E.tensor_sub(stat(VARV), stat(EX2), stat(M2C)),
               inc=(SV, 1))
        var_tok = rec.now(SV)
        rec.wait("scalar", var_tok)
        rec.op("scalar", lambda E: E.activation(stat(STDV), stat(VARV), ACT.Sqrt), inc=(SSC, 1))
        std_tok = rec.now(SSC)
        rec.wait("vector", std_tok)
        rec.op("vector", lambda E: E.reciprocal(stat(RSTD), stat(STDV)), inc=(SV, 1))
        rstd_tok = rec.now(SV)
        rec.wait("vector", rstd_tok)
        rec.wait("vector", free.get(("XNTOK" + sfx, b)))
        lo_tok = None
        if do_lo:
            rec.op("vector", lambda E, b=b, bx=bx: E.tensor_scalar(
                SQSCR2[b], XSTG[bx], stat(NEGM), stat(RSTD), op0=ALU.add, op1=ALU.mult))
            rec.op("vector", lambda E, b=b: E.tensor_copy(XNTOK2[b], SQSCR2[b]),
                   inc=(SV, 1))
            xn_tok = rec.now(SV)
            rec.wait("gpsimd", xn_tok)
            rec.wait("gpsimd", free.get(("XNLO", b)))
            rec.op("gpsimd", lambda E, b=b: E.tensor_tensor(
                XNLO2[b], SQSCR2[b], XNTOK2[b], ALU.subtract), inc=(SG, 1))
            lo_tok = rec.now(SG)
        else:
            rec.op("vector", lambda E, b=b, bx=bx, XNTOKl=XNTOKl, XSTGl=XSTGl: E.tensor_scalar(
                XNTOKl[b], XSTGl[bx], stat(NEGM), stat(RSTD), op0=ALU.add, op1=ALU.mult),
                inc=(SV, 1))
            xn_tok = rec.now(SV)
        free[("XSTG" + sfx, bx)] = xn_tok
        free[("SQc" + sfx, bq)] = lo_tok if do_lo else xn_tok
        return xn_tok, lo_tok, cs_tok

    def emit_xt_stage(tt, dest, do_router, do_lo, xn_tok, lo_tok=None,
                      evac_guard_key=None, hi_src=None, key_sfx=""):
        b = tt % 2
        rec.wait("tensor", xn_tok)
        rec.wait("tensor", consts_tok)
        srcs = [(XNTOK2[b] if hi_src is None else hi_src, dest)]
        if do_lo:
            srcs.append((XNLO2[b], lambda dc: XLO_TMP[:, dc, :]))
        for si, (src, dst) in enumerate(list(srcs)):
            if si == 1:
                rec.wait("tensor", lo_tok)
            for dc in range(DC):
                j = dc % 2
                rec.wait("tensor", free.get(("PT", j)))
                rec.op("tensor", lambda E, dc=dc, j=j, src=src: E.transpose(
                    PT[j], src[:, dc * 128:(dc + 1) * 128], ID16), inc=(STn, 1))
                tr_tok = rec.now(STn)
                rec.wait("vector", tr_tok)
                if evac_guard_key is not None:
                    rec.wait("vector", free.get(evac_guard_key))
                rec.op("vector", lambda E, j=j, d=dst(dc): E.tensor_copy(d, PT[j]), inc=(SV, 1))
                free[("PT", j)] = rec.now(SV)
        free[("XNTOK", b)] = (STn, STn.n)
        xnT_tok = rec.now(SV)

        if do_router:
            rec.wait("tensor", xnT_tok)
            rec.wait("tensor", rwhl_tok)
            rec.wait("tensor", free.get(("PS", 5)))
            combos = [(dest, RWT_HI), (dest, RWT_LO),
                      ((lambda dc: XLO_TMP[:, dc, :]), RWT_HI)]
            nmm = 0
            for lt, rw in combos:
                for dc in range(DC):
                    first = nmm == 0
                    last = nmm == 3 * DC - 1
                    rec.op("tensor", lambda E, l0=lt(dc), r0=rw[:, dc, :], f=first, l=last:
                           E.matmul(PS[5][:, 0:NE], l0, r0, start=f, stop=l),
                           inc=(STn, 1) if last else None)
                    nmm += 1
            lg_tok = rec.now(STn)
            rec.wait("vector", lg_tok)
            rec.op("vector", lambda E, tt=tt: E.tensor_copy(LGT[:, tt, :], PS[5][:, 0:NE]),
                   inc=(SV, 1))
            free[("PS", 5)] = rec.now(SV)
        return xnT_tok

    def emit_rms_rope(ps_src, nheads, cs_slot, rp_idx=0):
        RPX = RP2[rp_idx]
        rec.wait("scalar", free.get(("KRMS",)))
        for h4 in range(nheads):
            rec.op("scalar", lambda E, h4=h4: E.activation(
                RMS_SQ, ps_src[:, h4 * 128:(h4 + 1) * 128], ACT.Square,
                accum_out=stat(KRMS0 + h4)),
                inc=(SSC, 1) if h4 == nheads - 1 else None)
        rs_tok = rec.now(SSC)
        rec.wait("vector", rs_tok)
        rec.op("vector", lambda E, n=nheads: E.tensor_scalar(
            STAT[:, KRMS0:KRMS0 + n], STAT[:, KRMS0:KRMS0 + n],
            1.0 / DH, RMS_EPS, op0=ALU.mult, op1=ALU.add), inc=(SV, 1))
        ms_tok = rec.now(SV)
        free[("KRMS",)] = ms_tok
        rec.wait("scalar", ms_tok)
        rec.op("scalar", lambda E, n=nheads: E.activation(
            STAT[:, RKC0:RKC0 + n], STAT[:, KRMS0:KRMS0 + n], ACT.Sqrt), inc=(SSC, 1))
        sq2_tok = rec.now(SSC)
        rec.wait("vector", sq2_tok)
        rec.op("vector", lambda E, n=nheads: E.reciprocal(
            STAT[:, RKC0:RKC0 + n], STAT[:, RKC0:RKC0 + n]), inc=(SV, 1))
        rk_tok = rec.now(SV)
        rec.wait("scalar", rk_tok)
        for h4 in range(nheads):
            rec.op("scalar", lambda E, h4=h4: E.activation(
                NT_SCR[:, h4 * 128:(h4 + 1) * 128], ps_src[:, h4 * 128:(h4 + 1) * 128],
                ACT.Copy, scale=stat(RKC0 + h4)),
                inc=(SSC, 1) if h4 == nheads - 1 else None)
        ap_tok = rec.now(SSC)
        rec.wait("vector", ap_tok)
        nh = nheads
        nt3 = NT_SCR[:, 0:nh * 128].rearrange("p (h d) -> p h d", h=nh)
        x1 = nt3[:, :, 0:128:2]
        x2 = nt3[:, :, 1:128:2]
        cos3 = CSST[cs_slot][:, 0:nh * 64].rearrange("p (h d) -> p h d", h=nh)
        sin3 = CSST[cs_slot][:, 256:256 + nh * 64].rearrange("p (h d) -> p h d", h=nh)
        s1 = RO_S1[:, 0:nh * 64].rearrange("p (h d) -> p h d", h=nh)
        s2 = RO_S2[:, 0:nh * 64].rearrange("p (h d) -> p h d", h=nh)
        s3 = RO_S3[:, 0:nh * 64].rearrange("p (h d) -> p h d", h=nh)
        s4 = RO_S4[:, 0:nh * 64].rearrange("p (h d) -> p h d", h=nh)
        o3 = RPX[:, 0:nh * 128].rearrange("p (h d) -> p h d", h=nh)
        rec.wait("vector", free.get(("RP", rp_idx)))
        rec.op("vector", lambda E: E.tensor_mul(s1, x1, cos3))
        rec.op("vector", lambda E: E.tensor_mul(s2, x2, sin3))
        rec.op("vector", lambda E: E.tensor_mul(s3, x1, sin3))
        rec.op("vector", lambda E: E.tensor_mul(s4, x2, cos3))
        rec.op("vector", lambda E: E.tensor_sub(o3[:, :, 0:64], s1, s2))
        rec.op("vector", lambda E: E.tensor_add(o3[:, :, 64:128], s3, s4), inc=(SV, 1))
        return rec.now(SV)

    pend_rope = [None]

    def flush_rope_transposes():
        if pend_rope[0] is None:
            return
        rr_tok, rp_idx, nh, dest_fn = pend_rope[0]
        pend_rope[0] = None
        rec.wait("tensor", rr_tok)
        for h4 in range(nh):
            j = h4 % 2
            rec.wait("tensor", free.get(("PT", j)))
            rec.op("tensor", lambda E, h4=h4, j=j, rp=rp_idx: E.transpose(
                PT[j], RP2[rp][:, h4 * 128:(h4 + 1) * 128], ID16), inc=(STn, 1))
            tr_tok = rec.now(STn)
            rec.wait("vector", tr_tok)
            rec.op("vector", lambda E, h4=h4, j=j, d=dest_fn(h4): E.tensor_copy(
                d, PT[j]), inc=(SV, 1))
            free[("PT", j)] = rec.now(SV)
        free[("RP", rp_idx)] = (STn, STn.n)

    def pa_stage2(tt, xn_tok, lo_tok, cs_tok):
        if tt < HT:
            dest = (lambda dc, tt=tt: XN[:, dc, tt * 128:(tt + 1) * 128])
            guard_key = None
        else:
            dest = (lambda dc: XNT_TMP[:, dc, :])
            guard_key = ("XNT",)
        emit_xt_stage(tt, dest, do_router=True, do_lo=True, xn_tok=xn_tok,
                      lo_tok=lo_tok, evac_guard_key=guard_key)
        cb = tt % 2
        kb, vb = (0, 1) if tt % 2 == 0 else (2, 3)
        rec.wait("tensor", wkv_tok)
        rec.wait("tensor", free.get(("PS", kb)))
        for dc in range(DC):
            rec.op("tensor", lambda E, d=dest(dc), dc=dc, kb=kb: E.matmul(
                PS[kb][:], d, WKST[:, dc, :], start=(dc == 0), stop=(dc == DC - 1)),
                inc=(STn, 1) if dc == DC - 1 else None)
        k_tok = rec.now(STn)
        rec.wait("tensor", free.get(("PS", vb)))
        for dc in range(DC):
            rec.op("tensor", lambda E, d=dest(dc), dc=dc, vb=vb: E.matmul(
                PS[vb][:], d, WVST[:, dc, :], start=(dc == 0), stop=(dc == DC - 1)),
                inc=(STn, 1) if dc == DC - 1 else None)
        v_tok = rec.now(STn)
        free[("XNT",)] = v_tok
        flush_rope_transposes()

        rec.wait("scalar", k_tok)
        rec.wait("vector", cs_tok)
        rr_tok = emit_rms_rope(PS[kb], KV, cb, rp_idx=tt % 2)
        free[("CSST", cb)] = rr_tok
        free[("PS", kb)] = rr_tok
        pend_rope[0] = (rr_tok, tt % 2, KV,
                        (lambda h4, tt=tt: KT[:, h4, tt * 128:(tt + 1) * 128]))

        rec.wait("vector", v_tok)
        rec.op("vector", lambda E, tt=tt, vb=vb: E.tensor_copy(
            VT[:, :, tt, 0:128],
            PS[vb][:].rearrange("p (kv d) -> p kv d", kv=KV)), inc=(SV, 1))
        free[("PS", vb)] = rec.now(SV)

    pend = None
    for tt in range(NT):
        toks = emit_ln_stage(tt, do_lo=True, dup_load=(tt == 0), xstg4=True)
        if pend is not None:
            pa_stage2(*pend)
        pend = (tt, toks[0], toks[1], toks[2])
    pa_stage2(*pend)
    flush_rope_transposes()
    free[("WST", 0)] = free.get(("XSTG", 2))
    if free.get(("XSTG", 3)) is not None:
        free[("WST", 0)] = max(free[("WST", 0)], free[("XSTG", 3)],
                               key=lambda t: t[1])

    def emit_q_half(half, xn_ready):
        for hq in range(4):
            wb = hq % 2
            rec.wait("gpsimd", free.get(("WST", wb)))
            rec.op("gpsimd", lambda E, hq=hq, wb=wb: E.dma_start(
                out=WST[wb],
                in_=wqT_d[:, hq * 512:(hq + 1) * 512].rearrange("(dc p) n -> p dc n", p=128)),
                inc=(SW, 16))
            wq_tok = rec.now(SW)
            for ti in range(HT):
                tt = half * HT + ti
                cb = ti % 2
                rec.wait("sync", free.get(("CSST", cb)))
                rec.op("sync", lambda E, tt=tt, cb=cb: E.dma_start(
                    out=CSST[cb], in_=cs_d[tt * 128:(tt + 1) * 128, :]), inc=(SCS, 16))
                cs_tok = rec.now(SCS)
                qb = ti % 2
                rec.wait("tensor", wq_tok)
                rec.wait("tensor", xn_ready)
                rec.wait("tensor", free.get(("PS", qb)))
                for dc in range(DC):
                    rec.op("tensor", lambda E, dc=dc, ti=ti, wb=wb, qb=qb: E.matmul(
                        PS[qb][:], XN[:, dc, ti * 128:(ti + 1) * 128], WST[wb][:, dc, :],
                        start=(dc == 0), stop=(dc == DC - 1)),
                        inc=(STn, 1) if dc == DC - 1 else None)
                q_tok = rec.now(STn)
                flush_rope_transposes()
                rec.wait("scalar", q_tok)
                rec.wait("vector", cs_tok)
                rr_tok = emit_rms_rope(PS[qb], 4, cb, rp_idx=ti % 2)
                free[("CSST", cb)] = rr_tok
                free[("PS", qb)] = rr_tok
                if ti == HT - 1:
                    free[("WST", wb)] = rr_tok
                pend_rope[0] = (rr_tok, ti % 2, 4,
                                (lambda h4, hq=hq, ti=ti: QT[:, hq * 4 + h4,
                                                             ti * 128:(ti + 1) * 128]))
        flush_rope_transposes()
        return rec.now(SV)

    def vgap():
        rec.op("vector", lambda E: E.tensor_copy(STAT[:, 31:32], STAT[:, 30:31]))

    def emit_router_weights():
        for tt in range(NT):
            rec.op("vector", lambda E, tt=tt: E.reduce_max(
                M1A[:, tt:tt + 1], LGT[:, tt, :], axis=AX.X),
                inc=(SV, 1) if tt == NT - 1 else None)
        rec.wait("vector", rec.now(SV))
        for tt in range(NT):
            rec.op("vector", lambda E, tt=tt: E.tensor_scalar(
                S8A[:, tt, :], LGT[:, tt, :], M1A[:, tt:tt + 1], None, op0=ALU.is_equal))
        for tt in range(NT):
            rec.op("vector", lambda E, tt=tt: E.scalar_tensor_tensor(
                S8B[:, tt, :], S8A[:, tt, :], -1e30, LGT[:, tt, :],
                op0=ALU.mult, op1=ALU.add))
        for tt in range(NT):
            rec.op("vector", lambda E, tt=tt: E.reduce_max(
                M2A[:, tt:tt + 1], S8B[:, tt, :], axis=AX.X),
                inc=(SV, 1) if tt == NT - 1 else None)
        rec.wait("vector", rec.now(SV))
        rec.op("vector", lambda E: E.tensor_sub(DMAr[:], M2A[:], M1A[:]), inc=(SV, 1))
        dm_tok = rec.now(SV)
        rec.wait("scalar", dm_tok)
        rec.op("scalar", lambda E: E.activation(EWA[:], DMAr[:], ACT.Exp), inc=(SSC, 1))
        ew_tok = rec.now(SSC)
        rec.wait("vector", ew_tok)
        rec.op("vector", lambda E: E.tensor_scalar(EW2A[:], EWA[:], 1.0, None, op0=ALU.add))
        vgap()
        rec.op("vector", lambda E: E.reciprocal(W1A[:], EW2A[:]), inc=(SV, 1))
        rec.wait("vector", rec.now(SV))
        rec.op("vector", lambda E: E.tensor_scalar(
            W2A[:], W1A[:], -1.0, 1.0, op0=ALU.mult, op1=ALU.add), inc=(SV, 1))
        w2_tok = rec.now(SV)
        for tt in range(NT):
            rec.op("vector", lambda E, tt=tt: E.tensor_scalar_mul(
                S8B[:, tt, :], S8A[:, tt, :], W1A[:, tt:tt + 1]))
        for tt in range(NT):
            rec.op("vector", lambda E, tt=tt: E.scalar_tensor_tensor(
                S8C[:, tt, :], S8A[:, tt, :], -1e30, LGT[:, tt, :],
                op0=ALU.mult, op1=ALU.add))
        for tt in range(NT):
            rec.op("vector", lambda E, tt=tt: E.tensor_scalar(
                S8C[:, tt, :], S8C[:, tt, :], M2A[:, tt:tt + 1], None, op0=ALU.is_equal))
        rec.wait("vector", w2_tok)
        for tt in range(NT):
            rec.op("vector", lambda E, tt=tt: E.scalar_tensor_tensor(
                WRT[:, tt, :], S8C[:, tt, :], W2A[:, tt:tt + 1], S8B[:, tt, :],
                op0=ALU.mult, op1=ALU.add),
                inc=(SV, 1) if tt == NT - 1 else None)

    def emit_attn_half(half, ready_tok):
        rec.wait("tensor", ready_tok)
        rec.wait("vector", rec.now(STn))
        rec.wait("vector", consts_tok)
        for h in range(H):
            kv = h // 4
            for cr in range(2):
                cq = half * 2 + cr
                kclo = max(0, 4 * cq - 8)
                kchi = 4 * cq + 3
                kcs = list(range(kclo, kchi + 1))
                pv_first = [True] * 4

                def emit_S(i, kc):
                    sb_ = i % 2
                    rec.wait("tensor", free.get(("PS", 4 + sb_)))
                    rec.op("tensor", lambda E, kc=kc, sb_=sb_, kv=kv, h=h, cr=cr: E.matmul(
                        PS[4 + sb_][:], KT[:, kv, kc * 128:(kc + 1) * 128],
                        QT[:, h, cr * 512:(cr + 1) * 512], start=True, stop=True),
                        inc=(STn, 1))
                    s_tok = rec.now(STn)
                    rec.wait("scalar", s_tok)
                    rec.wait("scalar", free.get(("ATB", sb_)))
                    rec.op("scalar", lambda E, sb_=sb_: E.activation(
                        ATB[sb_][:], PS[4 + sb_][:], ACT.Exp, scale=SCALE), inc=(SSC, 1))
                    exp_tok = rec.now(SSC)
                    free[("PS", 4 + sb_)] = exp_tok
                    a_ready = exp_tok
                    causal_qr = kc - 4 * cq if 4 * cq <= kc <= 4 * cq + 3 else None
                    drop_row0 = (kc == 4 * cq - 8)
                    if causal_qr is not None or drop_row0:
                        rec.wait("vector", exp_tok)
                        if causal_qr is not None:
                            rec.op("vector", lambda E, sb_=sb_, qr=causal_qr: E.tensor_mul(
                                ATB[sb_][:, qr * 128:(qr + 1) * 128],
                                ATB[sb_][:, qr * 128:(qr + 1) * 128], MASK_C),
                                inc=None if drop_row0 else (SV, 1))
                        if drop_row0:
                            rec.op("vector", lambda E, sb_=sb_: E.memset(
                                ATB[sb_][0:1, :], 0.0), inc=(SV, 1))
                        a_ready = rec.now(SV)
                    return a_ready

                def emit_PV(i, kc, a_ready):
                    sb_ = i % 2
                    rec.wait("tensor", a_ready)
                    qs_lo = max(0, kc - 4 * cq)
                    valid_qrs = list(range(qs_lo, 4))
                    for vi, qr in enumerate(valid_qrs):
                        qs = 4 * cq + qr
                        is_stop = (kc == qs)
                        is_last = (vi == len(valid_qrs) - 1)
                        if pv_first[qr]:
                            rec.wait("tensor", free.get(("PS", qr)))
                        rec.op("tensor", lambda E, sb_=sb_, qr=qr, kv=kv, kc=kc,
                               f=pv_first[qr], st=is_stop: E.matmul(
                            PS[qr][:, 0:129], ATB[sb_][:, qr * 128:(qr + 1) * 128],
                            VT[:, kv, kc, :].squeeze(), start=f, stop=st),
                            inc=(STn, 1) if (is_stop or is_last) else None)
                        pv_first[qr] = False
                        if is_stop:
                            pv_tok = rec.now(STn)
                            ti = qs - half * HT
                            rec.wait("vector", pv_tok)
                            rec.op("vector", lambda E, qr=qr: E.reciprocal(
                                stat(RCOL), PS[qr][:, 128:129]), inc=(SV, 1))
                            rc_tok = rec.now(SV)
                            rec.wait("vector", rc_tok)
                            rec.op("vector", lambda E, qr=qr, ti=ti, h=h: E.tensor_scalar_mul(
                                AO[:, ti, h * 128:(h + 1) * 128], PS[qr][:, 0:128], stat(RCOL)),
                                inc=(SV, 1))
                            free[("PS", qr)] = rec.now(SV)
                    free[("ATB", sb_)] = (STn, STn.n)

                pend = None
                for i, kc in enumerate(kcs):
                    a_ready = emit_S(i, kc)
                    if pend is not None:
                        emit_PV(pend[0], pend[1], pend[2])
                    pend = (i, kc, a_ready)
                emit_PV(*pend)

    def emit_atT_half():
        rec.wait("tensor", rec.now(SV))
        pt4 = [PT[0][:], PT[1][:],
               PS[4].bitcast(BF16)[:, 0:128], PS[5].bitcast(BF16)[:, 0:128]]
        at_toks = {}
        for ti in range(HT):
            for hc in range(H):
                j = (ti * H + hc) % 4
                rec.wait("tensor", free.get(("PT4", j)))
                rec.op("tensor", lambda E, ti=ti, hc=hc, j=j: E.transpose(
                    pt4[j], AO[:, ti, hc * 128:(hc + 1) * 128], ID16), inc=(STn, 1))
                tr_tok = rec.now(STn)
                rec.wait("vector", tr_tok)
                rec.op("vector", lambda E, ti=ti, hc=hc, j=j: E.tensor_copy(
                    QT[:, hc, ti * 128:(ti + 1) * 128], pt4[j]), inc=(SV, 1))
                free[("PT4", j)] = rec.now(SV)
            at_toks[ti] = (STn, STn.n)
        free[("PS", 4)] = free[("PT4", 2)]
        free[("PS", 5)] = free[("PT4", 3)]
        free[("PT", 0)] = free[("PT4", 0)]
        free[("PT", 1)] = free[("PT4", 1)]
        return at_toks

    def emit_wo_half(at_toks):
        rec.wait("tensor", rec.now(SV))
        rec.wait("vector", rec.now(SST))
        for dq in range(4):
            wb = dq % 2
            if dq < 2:
                rec.wait("gpsimd", at_toks[3 if wb == 0 else 7])
            rec.wait("gpsimd", free.get(("WOST", wb)))
            rec.op("gpsimd", lambda E, dq=dq, wb=wb: E.dma_start(
                out=WOST[wb],
                in_=woT_d[:, dq * 512:(dq + 1) * 512].rearrange("(hc p) n -> p hc n", p=128)),
                inc=(SW, 16))
            wo_tok = rec.now(SW)
            rec.wait("tensor", wo_tok)
            for ti in range(HT):
                pb = ti % 2
                rec.wait("tensor", free.get(("PS", pb)))
                for hc in range(H):
                    rec.op("tensor", lambda E, hc=hc, ti=ti, wb=wb, pb=pb: E.matmul(
                        PS[pb][:], QT[:, hc, ti * 128:(ti + 1) * 128], WOST[wb][:, hc, :],
                        start=(hc == 0), stop=(hc == H - 1)),
                        inc=(STn, 1) if hc == H - 1 else None)
                y_tok = rec.now(STn)
                rec.wait("vector", y_tok)
                rec.op("vector", lambda E, ti=ti, dq=dq, pb=pb: E.tensor_copy(
                    ACC[:, ti, dq * 512:(dq + 1) * 512], PS[pb][:]), inc=(SV, 1))
                free[("PS", pb)] = rec.now(SV)
                if ti == HT - 1:
                    free[("WOST", wb)] = (STn, STn.n)

    def emit_ffn_half(half, tail_cb=None):
        gv_done = [None]
        t_now = rec.now(STn)
        rec.wait("vector", t_now)
        rec.wait("sync", t_now)
        rec.wait("gpsimd", t_now)
        free[("W1ST", 0)] = t_now
        free[("W1ST", 1)] = t_now
        free[("W2ST", 0)] = t_now
        free[("W2ST", 1)] = t_now
        for ti in range(HT):
            tt = half * HT + ti
            b = ti % 2
            rec.wait("sync", free.get(("FXSTG", b)))
            rec.op("sync", lambda E, tt=tt, b=b: E.dma_start(
                out=FXSTG[b], in_=x_d[tt * 128:(tt + 1) * 128, :]), inc=(SX, 16))
            if ti == 0:
                rec.op("sync", lambda E, tt=tt, b=b: E.dma_start(
                    out=FXSTG[b], in_=x_d[tt * 128:(tt + 1) * 128, :]), inc=(SX, 16))
            x_tok = rec.now(SX)
            rec.wait("vector", x_tok)
            rec.op("vector", lambda E, ti=ti, b=b: E.tensor_tensor(
                ACC[:, ti, :], ACC[:, ti, :], FXSTG[b], ALU.add), inc=(SV, 1))
            free[("FXSTG", b)] = rec.now(SV)

        for e in range(NE):
            for fq in range(4):
                wb = fq % 2
                rec.wait("gpsimd", free.get(("W1ST", wb)))
                rec.op("gpsimd", lambda E, e=e, fq=fq, wb=wb: E.dma_start(
                    out=W1ST[wb],
                    in_=w1_d[e, :, fq * 256:(fq + 1) * 256].rearrange("(dc p) f -> p dc f", p=128)),
                    inc=(SW, 16))
                rec.wait("gpsimd", free.get(("W2ST", wb)))
                rec.op("gpsimd", lambda E, e=e, fq=fq, wb=wb: E.dma_start(
                    out=W2ST[wb],
                    in_=w2_d[e, :, fq * 256:(fq + 1) * 256].rearrange("(dc p) f -> p dc f", p=128)),
                    inc=(SW, 16))
                w12_tok = rec.now(SW)
                rec.wait("tensor", w12_tok)
                for fcl in range(2):
                    fc = fq * 2 + fcl
                    for tq in range(2):
                        rec.wait("tensor", free.get(("PS", 2)))
                        for dc in range(DC):
                            rec.op("tensor", lambda E, dc=dc, fcl=fcl, tq=tq, wb=wb: E.matmul(
                                PS[2][:], W1ST[wb][:, dc, fcl * 128:(fcl + 1) * 128],
                                XN[:, dc, tq * 512:(tq + 1) * 512],
                                start=(dc == 0), stop=(dc == DC - 1)),
                                inc=(STn, 1) if dc == DC - 1 else None)
                        g_tok = rec.now(STn)
                        rec.wait("tensor", free.get(("PS", 3)))
                        for dc in range(DC):
                            rec.op("tensor", lambda E, dc=dc, fcl=fcl, tq=tq, wb=wb: E.matmul(
                                PS[3][:], W2ST[wb][:, dc, fcl * 128:(fcl + 1) * 128],
                                XN[:, dc, tq * 512:(tq + 1) * 512],
                                start=(dc == 0), stop=(dc == DC - 1)),
                                inc=(STn, 1) if dc == DC - 1 else None)
                        vv_tok = rec.now(STn)
                        rec.wait("scalar", g_tok)
                        rec.wait("scalar", free.get(("RP", 0)))
                        rec.op("scalar", lambda E: E.activation(
                            RP_SCR[:], PS[2][:], ACT.Silu), inc=(SSC, 1))
                        si_tok = rec.now(SSC)
                        free[("PS", 2)] = si_tok
                        if e == NE - 1 and fq == 3 and fcl == 1 and tq == 1:
                            gv_done[0] = rec.now(STn)
                        rec.wait("vector", si_tok)
                        rec.wait("vector", vv_tok)
                        rec.wait("vector", free.get(("HBUF",)))
                        rec.op("vector", lambda E, fc=fc, tq=tq: E.tensor_tensor(
                            HBUF[:, fc, tq * 512:(tq + 1) * 512], RP_SCR[:], PS[3][:],
                            ALU.mult), inc=(SV, 1))
                        free[("PS", 3)] = rec.now(SV)
                        free[("RP", 0)] = rec.now(SV)
                        if fcl == 1 and tq == 1:
                            free[("W1ST", wb)] = rec.now(SV)
                            free[("W2ST", wb)] = rec.now(SV)
            h_tok = rec.now(SV)
            if e == NE - 1 and tail_cb is not None:
                tail_cb()
            for dq in range(4):
                wb = dq % 2
                rec.wait("gpsimd", free.get(("W3ST", wb)))
                rec.op("gpsimd", lambda E, e=e, dq=dq, wb=wb: E.dma_start(
                    out=W3ST[wb],
                    in_=w3_d[e, :, dq * 512:(dq + 1) * 512].rearrange("(fc p) d -> p fc d", p=128)),
                    inc=(SW, 16))
                w3_tok = rec.now(SW)
                rec.wait("tensor", w3_tok)
                rec.wait("tensor", h_tok)
                for ti in range(HT):
                    tt = half * HT + ti
                    pb = ti % 2
                    rec.wait("tensor", free.get(("PS", pb)))
                    for fc in range(8):
                        rec.op("tensor", lambda E, fc=fc, ti=ti, wb=wb, pb=pb: E.matmul(
                            PS[pb][:], HBUF[:, fc, ti * 128:(ti + 1) * 128], W3ST[wb][:, fc, :],
                            start=(fc == 0), stop=(fc == 7)),
                            inc=(STn, 1) if fc == 7 else None)
                    o_tok = rec.now(STn)
                    if dq == 3 and ti == HT - 1:
                        free[("HBUF",)] = (STn, STn.n)
                    rec.wait("vector", o_tok)
                    rec.op("vector", lambda E, ti=ti, dq=dq, pb=pb, tt=tt, e=e:
                           E.scalar_tensor_tensor(
                               ACC[:, ti, dq * 512:(dq + 1) * 512], PS[pb][:],
                               WRT[:, tt, e:e + 1], ACC[:, ti, dq * 512:(dq + 1) * 512],
                               op0=ALU.mult, op1=ALU.add), inc=(SV, 1))
                    free[("PS", pb)] = rec.now(SV)
                    if ti == HT - 1:
                        free[("W3ST", wb)] = rec.now(SV)
        acc_tok = rec.now(SV)
        rec.wait("sync", acc_tok)
        for ti in range(HT):
            tt = half * HT + ti
            rec.op("sync", lambda E, tt=tt, ti=ti: E.dma_start(
                out=out_d[tt * 128:(tt + 1) * 128, :], in_=ACC[:, ti, :]), inc=(SST, 16))
        st_tok = rec.now(SST)
        for k in (("WST", 0), ("WST", 1), ("XSTG", 0), ("XSTG", 1),
                  ("CSST", 0), ("CSST", 1)):
            free[k] = st_tok
        return gv_done[0]

    def dbg_dump(key, src):
        if not debug:
            return
        tok = rec.now(SV)
        rec.wait("sync", tok)
        rec.op("sync", lambda E, key=key, src=src: E.dma_start(out=dbg[key], in_=src),
               inc=(SST, 16))

    pa_tok = rec.now(SV)
    emit_q_half(0, pa_tok)
    q1_tok = rec.now(SV)
    emit_attn_half(0, q1_tok)
    at_toks = emit_atT_half()
    emit_router_weights()
    emit_wo_half(at_toks)
    if debug:
        mid_tok = rec.now(SV)
        rec.wait("sync", mid_tok)
        rec.op("sync", lambda E: E.dma_start(out=dbg["xnT"], in_=XN[:]), inc=(SST, 16))
        rec.op("sync", lambda E: E.dma_start(out=dbg["y"], in_=ACC[:]), inc=(SST, 16))
        rec.op("sync", lambda E: E.dma_start(out=dbg["qT"], in_=QT[:]), inc=(SST, 16))
        dump_done = rec.now(SST)
        rec.wait("vector", dump_done)
        rec.wait("tensor", dump_done)
        rec.wait("scalar", dump_done)
        rec.wait("gpsimd", dump_done)
    gv_tok = emit_ffn_half(0)

    rec.wait("vector", gv_tok)
    rec.wait("scalar", gv_tok)
    rec.wait("sync", gv_tok)
    rbufs = (XSTG_R, [SQSCR_R], XNTOK_R, "_R")
    pend = None
    for ti in range(HT):
        tt = HT + ti
        toks = emit_ln_stage(tt, do_lo=False, dup_load=(ti == 0), load_cs=False,
                             bufs=rbufs)
        if pend is not None:
            emit_xt_stage(pend[0], pend[1], do_router=False, do_lo=False,
                          xn_tok=pend[2], hi_src=XNTOK_R[pend[0] % 2], key_sfx="_R")
        pend = (tt, (lambda dc, ti=ti: XN[:, dc, ti * 128:(ti + 1) * 128]), toks[0])
    emit_xt_stage(pend[0], pend[1], do_router=False, do_lo=False,
                  xn_tok=pend[2], hi_src=XNTOK_R[pend[0] % 2], key_sfx="_R")
    xn2_tok = rec.now(SV)
    emit_q_half(1, xn2_tok)
    emit_attn_half(1, rec.now(SV))
    at_toks = emit_atT_half()
    emit_wo_half(at_toks)
    emit_ffn_half(1)

    if debug:
        end_tok = rec.now(SV)
        rec.wait("sync", end_tok)
        rec.wait("sync", rec.now(SST))
        for key, src in (("kT", KT[:]), ("vT", VT[:]), ("W", WRT[:]),
                         ("lg", LGT[:]), ("ao", AO[:])):
            rec.op("sync", lambda E, key=key, src=src: E.dma_start(out=dbg[key], in_=src),
                   inc=(SST, 16))

    for s_ in (SST, SX, SCS, SW):
        rec.wait("gpsimd", rec.now(s_))

    with nc.Block() as block:
        def mk(eng_name):
            def body(E):
                for f in rec.ops[eng_name]:
                    f(E)
            return body
        block.sync(mk("sync"))
        block.tensor(mk("tensor"))
        block.vector(mk("vector"))
        block.scalar(mk("scalar"))
        block.gpsimd(mk("gpsimd"))

    counts = {k: len(v) for k, v in rec.ops.items()}
    print("bass program op counts:", counts, "total:", sum(counts.values()))
    return nc


def _host_prep(inputs):
    x = np.ascontiguousarray(np.asarray(inputs["L0_attn_x"], dtype=np.float32))
    wqT = np.ascontiguousarray(np.asarray(inputs["L0_attn_wq"], dtype=np.float32).T)
    wkT = np.ascontiguousarray(np.asarray(inputs["L0_attn_wk"], dtype=np.float32).T)
    wvT = np.ascontiguousarray(np.asarray(inputs["L0_attn_wv"], dtype=np.float32).T)
    woT = np.ascontiguousarray(np.asarray(inputs["L0_attn_wo"], dtype=np.float32).T)
    rwT = np.ascontiguousarray(np.asarray(inputs["L0_ffn_router_w"], dtype=np.float32).T)
    w1 = np.ascontiguousarray(np.asarray(inputs["L0_ffn_w1"], dtype=np.float32))
    w2 = np.ascontiguousarray(np.asarray(inputs["L0_ffn_w2"], dtype=np.float32))
    w3 = np.ascontiguousarray(np.asarray(inputs["L0_ffn_w3"], dtype=np.float32))

    inv_freq = 10000.0 ** (-(np.arange(0, DH, 2, dtype=np.float64) / DH))
    ang = np.outer(np.arange(T, dtype=np.float64), inv_freq)
    cos4 = np.tile(np.cos(ang).astype(np.float32), (1, 4))
    sin4 = np.tile(np.sin(ang).astype(np.float32), (1, 4))
    cs_tab = np.ascontiguousarray(np.concatenate([cos4, sin4], axis=1))

    ident = np.eye(128, dtype=ml_dtypes.bfloat16)
    kk = np.arange(128)[:, None]
    qq = np.arange(128)[None, :]
    mask_c = (kk <= qq).astype(ml_dtypes.bfloat16)
    mask_w = (kk > qq).astype(ml_dtypes.bfloat16)
    consts = np.ascontiguousarray(np.concatenate([ident, mask_c, mask_w], axis=1))

    shared = dict(wqT=wqT, wkT=wkT, wvT=wvT, woT=woT, rwT=rwT,
                  w1=w1, w2=w2, w3=w3, cs_tab=cs_tab, consts=consts)
    in_maps = []
    for c in range(8):
        m = dict(shared)
        m["x"] = np.ascontiguousarray(x[c])
        in_maps.append(m)
    return in_maps


def kernel(**inputs):
    in_maps = _host_prep(inputs)
    nc = build_nc(debug=DEBUG)
    res = run_bass_kernel_spmd(nc, in_maps, core_ids=list(range(8)))
    out = np.stack([np.asarray(r["out"]) for r in res.results], axis=0)
    if DEBUG:
        kernel.last_results = res.results
    return out
